# version 04
# baseline (speedup 1.0000x reference)
import math
from contextlib import ExitStack

import numpy as np
import ml_dtypes
import concourse.bass as bass
import concourse.mybir as mybir
from concourse.bass_utils import run_bass_kernel_spmd

F32 = mybir.dt.float32
BF16 = mybir.dt.bfloat16
I32 = mybir.dt.int32
AF = mybir.ActivationFunctionType
ALU = mybir.AluOpType
AXL = mybir.AxisListType
BF = ml_dtypes.bfloat16

B, T, D = 2, 8192, 1024
NCORES = 8
SPAN = 2048
HALO = 384
XL = SPAN + HALO
L0S = 256
L0N = XL - L0S
NQT = L0N // 128
H0, HD0 = 8, 64
SCALE0 = 1.0 / 8.0
FFNH = 3072
L1S = 320
L1N = XL - L1S
SOFF = HALO - L1S
H1, HD1 = 16, 64
CH = 64
NCH = SPAN // CH
NPAIR = H1 // 2
NEXP, EDIM = 16, 512
SHH = 1024


def _chunks(n, step=512):
    return [(i, min(step, n - i)) for i in range(0, n, step)]


class Ctx:

    def __init__(self, nc):
        self.nc = nc
        self.es = ExitStack()
        self.cnt = {}
        self.sem = {}
        for name in ["DL", "DG", "PE", "DV", "AC", "PO"]:
            self.sem[name] = self.es.enter_context(nc.semaphore(name))
            self.cnt[name] = 0

    def dma(self, inst, s):
        self.cnt[s] += 16
        inst.then_inc(self.sem[s], 16)
        return (s, self.cnt[s])

    def fin(self, inst, s):
        self.cnt[s] += 1
        inst.then_inc(self.sem[s], 1)
        return (s, self.cnt[s])

    def wait(self, eng, ev):
        if ev is not None:
            eng.wait_ge(self.sem[ev[0]], ev[1])

    def wait_all(self, eng, evs):
        seen = {}
        for ev in evs:
            if ev is None:
                continue
            s, v = ev
            seen[s] = max(seen.get(s, 0), v)
        for s, v in seen.items():
            eng.wait_ge(self.sem[s], v)

    def now(self, s):
        return (s, self.cnt[s])

    def barrier(self, engines):
        evs = [self.now(s) for s in ["PE", "DV", "AC", "DL", "DG", "PO"]]
        for eng in engines:
            self.wait_all(eng, evs)


def mt(view):
    return view


def _rope_tables(c):
    start = SPAN * (c % 4)
    pos = np.arange(start - HALO, start - HALO + XL, dtype=np.float64)
    j = np.arange(32, dtype=np.float64)
    inv = 10000.0 ** (-2.0 * j / 64.0)
    ang = pos[None, :] * inv[:, None]
    cos = np.cos(ang).astype(np.float32)
    sin = np.sin(ang).astype(np.float32)
    cc = np.empty((128, XL), np.float32)
    ss = np.empty((128, XL), np.float32)
    for hh in range(2):
        r = 64 * hh
        cc[r:r + 32] = cos
        cc[r + 32:r + 64] = cos
        ss[r:r + 32] = -sin
        ss[r + 32:r + 64] = sin
    return cc, ss


def host_inputs(inputs):
    x = np.asarray(inputs["L0_attn_x"])
    i = np.arange(128)
    sl = (i[:, None] > i[None, :]).astype(BF)
    ui = (i[:, None] <= i[None, :]).astype(BF)
    ident = np.eye(128, dtype=BF)
    onesrow = np.ones((1, 128), dtype=BF)
    onescol = np.ones((128, 1), dtype=BF)
    selh = np.zeros((16, 8 * 128), dtype=BF)
    for ft in range(8):
        selh[2 * ft, ft * 128 + 0:ft * 128 + 64] = 1
        selh[2 * ft + 1, ft * 128 + 64:ft * 128 + 128] = 1
    sele = np.zeros((16, 16 * 128), dtype=BF)
    for e in range(16):
        sele[e, 128 * e:128 * (e + 1)] = 1
    j64 = np.arange(64)
    su64 = (j64[:, None] < j64[None, :]).astype(np.float32)
    sl64 = (j64[:, None] > j64[None, :]).astype(np.float32)
    ui64 = (j64[:, None] <= j64[None, :]).astype(np.float32)
    bd = np.zeros((128, 128), np.float32)
    bd[0:64, 0:64] = 1
    bd[64:128, 64:128] = 1

    def bdm(m64):
        out = np.zeros((128, 128), np.float32)
        out[0:64, 0:64] = m64
        out[64:128, 64:128] = m64
        return out

    selh2 = np.zeros((128, 8 * 16), dtype=BF)
    for ft in range(8):
        selh2[0:64, 16 * ft + 2 * ft] = 1
        selh2[64:128, 16 * ft + 2 * ft + 1] = 1
    selhf_ = np.zeros((16, 8 * 128), dtype=np.float32)
    for ft in range(8):
        selhf_[2 * ft, ft * 128 + 0:ft * 128 + 64] = 1
        selhf_[2 * ft + 1, ft * 128 + 64:ft * 128 + 128] = 1
    ctrl_w_ = np.asarray(inputs["L1_attn_ctrl_w"], np.float32)
    ctrl_b_ = np.asarray(inputs["L1_attn_ctrl_b"], np.float32)
    ctrlP = np.zeros((1152, 1024), np.float32)
    ctrlP[0:16] = ctrl_w_[0:16]
    ctrlP[32:48] = ctrl_w_[16:32]
    ctrlP[64:1088] = ctrl_w_[32:1056]
    ctrlTP = np.ascontiguousarray(ctrlP.T)
    cb_ = np.zeros(1152, np.float32)
    cb_[0:16] = ctrl_b_[0:16]
    cb_[32:48] = ctrl_b_[16:32]
    cb_[64:1088] = ctrl_b_[32:1056]
    ctrlb2v = np.ascontiguousarray(cb_.reshape(9, 128).T)
    mslbdn_ = (-bdm(sl64)).astype(BF)
    msubdn_ = (-bdm(su64)).astype(BF)
    muibd_ = bdm(ui64).astype(BF)
    nw = np.asarray(inputs["L1_attn_norm_w"]).astype(np.float32)
    nwcol = np.tile(nw, 2).reshape(128, 1)
    outb = np.asarray(inputs["L1_attn_out_b"]).astype(np.float32).reshape(1024, 1)
    ctrlb = np.asarray(inputs["L1_attn_ctrl_b"]).astype(np.float32).reshape(1056, 1)
    rbias_row = np.broadcast_to(
        np.asarray(inputs["L1_ffn_router_bias"]).astype(np.float32), (128, 16)).copy()

    common = {
        "Wqkv": np.asarray(inputs["L0_attn_Wqkv"], np.float32),
        "Wout": np.asarray(inputs["L0_attn_Wout"], np.float32),
        "wgT": np.ascontiguousarray(np.asarray(inputs["L0_ffn_w_gate"]).T).astype(np.int32),
        "wuT": np.ascontiguousarray(np.asarray(inputs["L0_ffn_w_up"]).T).astype(np.int32),
        "wdT": np.ascontiguousarray(np.asarray(inputs["L0_ffn_w_down"]).T).astype(np.int32),
        "qkvT": np.ascontiguousarray(np.asarray(inputs["L1_attn_qkv_w"], np.float32).T),
        "ctrlT": ctrlTP,
        "convq": np.asarray(inputs["L1_attn_convq_w"], np.float32),
        "convk": np.asarray(inputs["L1_attn_convk_w"], np.float32),
        "convv": np.asarray(inputs["L1_attn_convv_w"], np.float32),
        "routT": np.ascontiguousarray(np.asarray(inputs["L1_ffn_router_w"], np.float32).T),
        "rbias": rbias_row,
        "eupT": np.asarray(inputs["L1_ffn_up_proj"], np.float32),
        "egaT": np.asarray(inputs["L1_ffn_gate_proj"], np.float32),
        "edoT": np.asarray(inputs["L1_ffn_down_proj"], np.float32),
        "shgT": np.ascontiguousarray(np.asarray(inputs["L1_ffn_sh_gate"], np.float32).T),
        "shuT": np.ascontiguousarray(np.asarray(inputs["L1_ffn_sh_up"], np.float32).T),
        "shdT": np.ascontiguousarray(np.asarray(inputs["L1_ffn_sh_down"], np.float32).T),
        "ctrlb": ctrlb,
        "maskSL": sl, "maskUI": ui, "ident": ident,
        "onesrow": onesrow, "onescol": onescol,
        "selh": selh, "sele": sele,
        "mslbdn": mslbdn_, "msubdn": msubdn_,
        "muibd": muibd_, "mbdn": (-bd).astype(BF),
        "mbd": bd.astype(BF),
        "nwcol": nwcol, "outb": outb,
        "selh2": selh2, "selhf": selhf_, "ctrlb2": ctrlb2v,
        "msl4": np.tile(mslbdn_, (1, 4)), "msu4": np.tile(msubdn_, (1, 4)),
        "mui4": np.tile(muibd_, (1, 4)), "mbd4": np.tile(bd.astype(BF), (1, 4)),
        "mbdn4": np.tile((-bd).astype(BF), (1, 4)),
        "mbd8": np.tile(bd.astype(BF), (1, 8)),
    }
    in_maps = []
    for c in range(NCORES):
        b, q = divmod(c, 4)
        start = SPAN * q
        lo = start - HALO
        xs = np.zeros((XL, D), np.float32)
        src_lo = max(lo, 0)
        xs[src_lo - lo:, :] = x[b, src_lo:start + SPAN, :]
        cc, ss = _rope_tables(c)
        validc = np.ones((128, 19), np.float32)
        mask3 = np.ones((128, 3), np.float32)
        if q == 0:
            validc[:, 0:3] = 0.0
            mask3[:] = 0.0
        m = dict(common)
        m.update({"xT": np.ascontiguousarray(xs.T), "ropeC": cc, "ropeS": ss,
                  "validc": validc, "mask3": mask3})
        in_maps.append(m)
    return in_maps


def build_launch1(dbg=(), stages="ABCDE", glim=(NPAIR, NCH // 4)):
    nc = bass.Bass("TRN2", target_bir_lowering=False)
    g = {}

    def din(name, shape, dtype=F32):
        g[name] = nc.dram_tensor(name, list(shape), dtype, kind="ExternalInput").ap()

    def dout(name, shape, dtype=F32):
        g[name] = nc.dram_tensor(name, list(shape), dtype, kind="ExternalOutput").ap()

    din("xT", (D, XL))
    din("Wqkv", (D, 1536)); din("Wout", (512, D))
    din("wgT", (D, FFNH), I32); din("wuT", (D, FFNH), I32); din("wdT", (FFNH, D), I32)
    din("qkvT", (D, 3 * D)); din("ctrlT", (D, 1152)); din("ctrlb", (1056, 1))
    din("convq", (D, 4)); din("convk", (D, 4)); din("convv", (D, 4))
    din("routT", (D, 16)); din("rbias", (128, 16))
    din("eupT", (NEXP, D, EDIM)); din("egaT", (NEXP, D, EDIM)); din("edoT", (NEXP, EDIM, D))
    din("shgT", (D, SHH)); din("shuT", (D, SHH)); din("shdT", (SHH, D))
    din("ropeC", (128, XL)); din("ropeS", (128, XL))
    din("maskSL", (128, 128), BF16); din("maskUI", (128, 128), BF16)
    din("ident", (128, 128), BF16)
    din("onesrow", (1, 128), BF16); din("onescol", (128, 1), BF16)
    din("selh", (16, 8 * 128), BF16); din("sele", (16, 16 * 128), BF16)
    din("mslbdn", (128, 128), BF16); din("msubdn", (128, 128), BF16)
    din("muibd", (128, 128), BF16); din("mbdn", (128, 128), BF16)
    din("mbd", (128, 128), BF16)
    din("validc", (128, 19)); din("mask3", (128, 3))
    din("nwcol", (128, 1)); din("outb", (D, 1))
    din("selh2", (128, 8 * 16), BF16); din("selhf", (16, 8 * 128))
    din("ctrlb2", (128, 9))
    din("msl4", (128, 512), BF16); din("msu4", (128, 512), BF16)
    din("mui4", (128, 512), BF16); din("mbd4", (128, 512), BF16)
    din("mbdn4", (128, 512), BF16); din("mbd8", (128, 1024), BF16)
    g["xn1sv"] = nc.dram_tensor("xn1sv", [D, L1N], BF16, kind="Internal").ap()
    g["kbsv"] = nc.dram_tensor("kbsv", [D, SPAN], BF16, kind="Internal").ap()

    dout("x1sp", (D, L0N))
    dout("g1sp", (D, SPAN), BF16)
    dout("qtsp", (NPAIR, 128, NCH * 128), BF16)
    dout("ogsp", (D, SPAN), BF16)
    dout("ufsp", (NPAIR, 128, NCH * 256), BF16)
    dout("ftsp", (NPAIR, 128, NCH * 128), BF16)
    dout("ktsp", (NPAIR, 128, NCH * 128), BF16)
    dout("ptsp", (NPAIR, 128, NCH * 128), BF16)
    dout("mpsp", (NPAIR, 128, NCH * 128), BF16)
    dout("stsp", (NPAIR, 128, NCH * 128), BF16)
    dout("mcsp", (NPAIR, 128, 128))
    dout("ccsp", (NPAIR, 128, 128))
    for name, shape, dt in dbg:
        dout(name, shape, dt)

    k = Ctx(nc)
    with k.es, nc.Block():
        emit_launch1(nc, k, g, dbg, stages, glim)
    return nc


def _ftv(t, n):
    v = t.ap().rearrange("p (a n) -> p a n", n=n)
    return [v[:, a2, :] for a2 in range(v.shape[1])]


def _warm(pe, tgt, lhsT, rhs, n=12):
    for _ in range(n):
        pe.matmul(tgt, lhsT, rhs, start=True, stop=True)


def emit_launch1(nc, k, g, dbg=(), stages="ABCDE", glim=(NPAIR, NCH // 4)):
    sync, gp, pe, dv, ac = nc.sync, nc.gpsimd, nc.tensor, nc.vector, nc.scalar
    es = k.es
    sb = lambda name, shape, dtype=F32: es.enter_context(nc.sbuf_tensor(name, list(shape), dtype))
    psa = lambda name, shape, dtype=F32: es.enter_context(nc.psum_tensor(name, list(shape), dtype))
    DBG = {name for name, _, _ in dbg}

    ident = sb("identb", (128, 128), BF16)
    onesr = sb("onesrb", (1, 128), BF16)
    onesc = sb("onescb", (128, 1), BF16)
    onesrf = sb("onesrf", (1, 128), F32)
    onescf = sb("onescf", (128, 1), F32)
    mSL = sb("mSLb", (128, 128), BF16)
    mUI = sb("mUIb", (128, 128), BF16)
    validc = sb("validcb", (128, 19), F32)
    const_ev = []
    for t, name in [(ident, "ident"), (onesr, "onesrow"), (onesc, "onescol"),
                    (mSL, "maskSL"), (mUI, "maskUI"), (validc, "validc")]:
        const_ev.append(k.dma(sync.dma_start(out=t[:, :], in_=g[name]), "DL"))
    dv.memset(onesrf[:, :], 1.0)
    ev = k.fin(dv.memset(onescf[:, :], 1.0), "DV")
    const_ev.append(ev)

    es_xn = k.es.enter_context(ExitStack())
    xn = es_xn.enter_context(nc.sbuf_tensor("xnb", [128, 8 * XL], BF16, side="right"))
    xnv = _ftv(xn, XL)

    xTv = g["xT"].rearrange("(a p) n -> a p n", p=128)

    with ExitStack() as esA:
        sbA = lambda name, shape, dtype=F32: esA.enter_context(nc.sbuf_tensor(name, list(shape), dtype))
        psA = lambda name, shape, dtype=F32: esA.enter_context(nc.psum_tensor(name, list(shape), dtype))
        X = sbA("X", (128, 8 * XL), F32)
        Xv = _ftv(X, XL)
        x_ev = []
        for ft in range(8):
            x_ev.append(k.dma(sync.dma_start(out=Xv[ft], in_=xTv[ft]), "DL"))

        wrkQ = [sbA(f"wrkQ{i}", (128, 512), F32) for i in range(2)]
        pS = psA("pS", (1, 512))
        pQ = psA("pQ", (1, 512))
        srow = sbA("srow", (1, XL), F32)
        qrow = sbA("qrow", (1, XL), F32)
        q_ev = [None, None]
        ch = _chunks(XL)
        for ci, (c0, cn) in enumerate(ch):
            stat_done = []
            for ft in range(8):
                slot = ft % 2
                k.wait(ac, x_ev[ft])
                k.wait(ac, q_ev[slot])
                sq = k.fin(ac.activation(wrkQ[slot][:, 0:cn], Xv[ft][:, c0:c0 + cn],
                                         AF.Square), "AC")
                k.wait_all(pe, [sq] + ([const_ev[-1]] if ci == 0 and ft == 0 else []))
                pe.matmul(pS[0:1, 0:cn], onescf[:, :], Xv[ft][:, c0:c0 + cn],
                          start=(ft == 0), stop=(ft == 7))
                mm = pe.matmul(pQ[0:1, 0:cn], onescf[:, :], wrkQ[slot][:, 0:cn],
                               start=(ft == 0), stop=(ft == 7))
                q_ev[slot] = k.fin(mm, "PE")
            k.wait(dv, q_ev[(7) % 2])
            dv.tensor_copy(srow[0:1, c0:c0 + cn], pS[0:1, 0:cn])
            e = k.fin(dv.tensor_copy(qrow[0:1, c0:c0 + cn], pQ[0:1, 0:cn]), "DV")
            k.wait(pe, e)

        arow = sbA("arow", (1, XL), F32)
        brow = sbA("brow", (1, XL), F32)
        trow = sbA("trow", (1, XL), F32)
        dv.tensor_scalar_mul(srow[:, :], srow[:, :], 1.0 / D)
        dv.tensor_tensor(trow[:, :], srow[:, :], srow[:, :], ALU.mult)
        dv.scalar_tensor_tensor(trow[:, :], qrow[:, :], 1.0 / D, trow[:, :],
                                ALU.mult, ALU.subtract)
        ev = k.fin(dv.tensor_scalar_add(trow[:, :], trow[:, :], 1e-5), "DV")
        k.wait(ac, ev)
        ev = k.fin(ac.activation(trow[:, :], trow[:, :], AF.Sqrt), "AC")
        k.wait(dv, ev)
        dv.reciprocal(arow[:, :], trow[:, :])
        dv.tensor_tensor(brow[:, :], srow[:, :], arow[:, :], ALU.mult)
        ev = k.fin(dv.tensor_scalar_mul(brow[:, :], brow[:, :], -1.0), "DV")

        Abb = sbA("Abb", (128, XL), F32)
        Bbb = sbA("Bbb", (128, XL), F32)
        pbcA = psA("pbcA", (128, 512))
        pbcB = psA("pbcB", (128, 512))
        k.wait(pe, ev)
        bc_ev = []
        for (c0, cn) in _chunks(XL):
            e1 = k.fin(pe.matmul(pbcA[:, 0:cn], onesrf[0:1, :], arow[0:1, c0:c0 + cn],
                                 start=True, stop=True), "PE")
            e2 = k.fin(pe.matmul(pbcB[:, 0:cn], onesrf[0:1, :], brow[0:1, c0:c0 + cn],
                                 start=True, stop=True), "PE")
            k.wait(ac, e2)
            ac.copy(Abb[:, c0:c0 + cn], pbcA[:, 0:cn])
            e3 = k.fin(ac.copy(Bbb[:, c0:c0 + cn], pbcB[:, 0:cn]), "AC")
            bc_ev.append(e3)
            k.wait(pe, e3)
        wrk = sbA("wrkLN", (128, XL), F32)
        k.wait_all(dv, bc_ev)
        for ft in range(8):
            dv.tensor_tensor(wrk[:, :], Xv[ft], Abb[:, :], ALU.mult)
            k.fin(dv.tensor_tensor(xnv[ft], wrk[:, :], Bbb[:, :], ALU.add), "DV")
        xn_done = k.now("DV")
        if "xn" in DBG:
            k.wait(sync, xn_done)
            for ft in range(8):
                k.dma(sync.dma_start(out=g["xn"].rearrange("(a p) n -> a p n", p=128)[ft],
                                     in_=xnv[ft]), "DL")
        st_ev = k.now("DL")
    k.barrier([sync, gp, pe, dv, ac])

    if "B" not in stages:
        k.wait(sync, k.now("DL"))
        k.wait(gp, k.now("DG"))
        return
    es_de = k.es.enter_context(ExitStack())
    s_sb = es_de.enter_context(nc.sbuf_tensor("s_sb", [128, 8 * L0N], BF16))
    sv = _ftv(s_sb, L0N)
    es_bc = k.es.enter_context(ExitStack())
    qk = es_bc.enter_context(nc.sbuf_tensor("qk", [128, 8 * XL], BF16))
    qkv_ = _ftv(qk, XL)
    v_tm = es_bc.enter_context(nc.sbuf_tensor("v_tm", [128, 19 * 520], BF16))
    vtv = _ftv(v_tm, 520)
    with ExitStack() as esB:
        sbB = lambda name, shape, dtype=F32: esB.enter_context(nc.sbuf_tensor(name, list(shape), dtype))
        psB = lambda name, shape, dtype=F32: esB.enter_context(nc.psum_tensor(name, list(shape), dtype))
        wq = sbB("wq", (128, 8 * 1536), BF16)
        wqv = _ftv(wq, 1536)
        wstg = [sbB("wstgB0", (128, 1536), F32)]
        ropeC = sbB("ropeCb", (128, XL), F32)
        ropeS = sbB("ropeSb", (128, XL), F32)
        w_ev = []
        wqkv_d = g["Wqkv"].rearrange("(a p) n -> a p n", p=128)
        stg_ev = [None]
        for kt in range(8):
            k.wait(sync, stg_ev[0])
            e = k.dma(sync.dma_start(out=wstg[0][:, :], in_=wqkv_d[kt]), "DL")
            k.wait(ac, e)
            stg_ev[0] = ee = k.fin(ac.copy(wqv[kt], wstg[0][:, :]), "AC")
            w_ev.append(ee)
        if "wqd" in DBG:
            k.wait_all(sync, w_ev)
            wqd = g["wqd"].rearrange("(a p) n -> a p n", p=128)
            for kt in range(8):
                k.dma(sync.dma_start(out=wqd[kt], in_=wqv[kt]), "DL")
        rc_ev = k.dma(sync.dma_start(out=ropeC[:, :], in_=g["ropeC"]), "DL")
        rs_ev = k.dma(sync.dma_start(out=ropeS[:, :], in_=g["ropeS"]), "DL")
        ones_ev = k.fin(dv.memset(v_tm[:, :], 1.0), "DV")

        pP = [psB(f"pP{i}", (128, 512)) for i in range(2)]
        ev_evac = [None, None]
        for of in range(8):
            for ci, (c0, cn) in enumerate(_chunks(XL)):
                slot = (of * 5 + ci) % 2
                k.wait_all(pe, w_ev + [xn_done, ev_evac[slot]])
                if of == 0 and ci == 0:
                    _warm(pe, pP[1][:, 0:512], ident[:, :], xnv[0][:, 0:512], 12)
                for kt in range(8):
                    mm = pe.matmul(pP[slot][:, 0:cn], wqv[kt][:, 128 * of:128 * of + 128],
                                   xnv[kt][:, c0:c0 + cn], start=(kt == 0), stop=(kt == 7))
                pe_ev = k.fin(mm, "PE")
                k.wait(ac, pe_ev)
                ev_evac[slot] = k.fin(ac.copy(qkv_[of][:, c0:c0 + cn], pP[slot][:, 0:cn]), "AC")
        qk_done = [k.now("AC")]
        for tt in range(19):
            slot = tt % 2
            k.wait_all(pe, [ev_evac[slot], ones_ev])
            for kt in range(8):
                mm = pe.matmul(pP[slot][:, 0:512], xnv[kt][:, 128 * tt:128 * tt + 128],
                               wqv[kt][:, 1024:1536], start=(kt == 0), stop=(kt == 7))
            pe_ev = k.fin(mm, "PE")
            k.wait(ac, pe_ev)
            ev_evac[slot] = k.fin(
                ac.copy(vtv[tt].rearrange("p (h c) -> p h c", c=65)[:, :, 0:64],
                        pP[slot][:, 0:512].rearrange("p (h c) -> p h c", c=64)), "AC")
        v_done = k.now("AC")
        qsw = sbB("qsw", (128, XL), BF16)
        w1 = sbB("ropew1", (128, XL), F32)
        w2 = sbB("ropew2", (128, XL), F32)
        rope_ev = []
        for of in range(8):
            k.wait_all(dv, qk_done + [rc_ev, rs_ev])
            for (a, b2) in [(0, 32), (64, 96)]:
                dv.tensor_copy(qsw[a:a + 32, :], qkv_[of][b2:b2 + 32, :])
                dv.tensor_copy(qsw[b2:b2 + 32, :], qkv_[of][a:a + 32, :])
            dv.tensor_tensor(w1[:, :], qkv_[of], ropeC[:, :], ALU.mult)
            dv.tensor_tensor(w2[:, :], qsw[:, :], ropeS[:, :], ALU.mult)
            rope_ev.append(k.fin(dv.tensor_tensor(qkv_[of], w1[:, :], w2[:, :], ALU.add), "DV"))
        rope_done = k.now("DV")
        if "qkd" in DBG:
            k.wait(sync, rope_done)
            qkd = g["qkd"].rearrange("(a p) n -> a p n", p=128)
            for of in range(8):
                k.dma(sync.dma_start(out=qkd[of], in_=qkv_[of]), "DL")
        if "vtd" in DBG:
            k.wait(sync, v_done)
            vtd = g["vtd"].rearrange("(a p) n -> a p n", p=128)
            for tt in range(19):
                k.dma(sync.dma_start(out=vtd[tt], in_=vtv[tt]), "DL")

    if "C" not in stages:
        k.wait(sync, k.now("DL"))
        k.wait(gp, k.now("DG"))
        return
    k.barrier([sync, gp, pe, dv, ac])
    es_cd = k.es.enter_context(ExitStack())
    s_in = es_cd.enter_context(nc.sbuf_tensor("s_in", [128, 4 * L0N], BF16))
    siv = _ftv(s_in, L0N)
    with ExitStack() as esC:
        sbC = lambda name, shape, dtype=F32: esC.enter_context(nc.sbuf_tensor(name, list(shape), dtype))
        psC = lambda name, shape, dtype=F32: esC.enter_context(nc.psum_tensor(name, list(shape), dtype))
        psS = [psC(f"psS{i}", (128, 384)) for i in range(2)]
        psO = [psC(f"psO{i}", (128, 128)) for i in range(2)]
        psTr = psC("psTr", (64, 1024), BF16)
        Pt = [sbC(f"Pt{i}", (128, 384), BF16) for i in range(2)]
        if "dbgo" in DBG:
            nc.lookup_dbgo = [sbC("dbgob", (128, 200), F32)]
        if "dbgp" in DBG:
            nc.lookup_dbgp = [sbC("dbgpb", (128, 384), F32), sbC("dbgpb2", (128, 384), F32)]
        atm = [sbC(f"atm{i}", (128, 64), BF16) for i in range(2)]
        rc = [sbC(f"rc{i}", (128, 1), F32) for i in range(2)]
        rc2 = [sbC(f"rc2{i}", (128, 1), F32) for i in range(2)]
        evS = [None, None]
        evP = [None, None]
        evO = [None, None]
        evA = [None, None]
        evT = [None] * 8
        first = [rope_done, v_done] + qk_done
        k.wait_all(pe, first)
        _warm(pe, psS[1][:, 0:384], ident[:, :], xnv[0][:, 0:384], 14)
        inst = 0
        for qt in range(NQT):
            for h in range(H0):
                sS, sO = inst % 2, inst % 2
                sT = inst % 8
                ftk = 4 + h // 2
                ftq = h // 2
                rb = (h % 2) * 64
                k.wait_all(pe, first + [evS[sS]])
                first = []
                if h == 0:
                    _warm(pe, psS[sS][:, 0:384], ident[:, :], xnv[0][:, 0:384], 14)
                for s in range(3):
                    mm = pe.matmul(psS[sS][:, 128 * s:128 * s + 128],
                                   qkv_[ftk][rb:rb + 64, 128 * (qt + s):128 * (qt + s) + 128],
                                   qkv_[ftq][rb:rb + 64, L0S + 128 * qt:L0S + 128 * qt + 128],
                                   start=True, stop=True)
                e_mm = k.fin(mm, "PE")
                k.wait_all(ac, [e_mm, evP[sS]])
                e_exp = k.fin(ac.activation(Pt[sS][:, :], psS[sS][:, :], AF.Exp,
                                            scale=SCALE0), "AC")
                evS[sS] = e_exp
                k.wait(dv, e_exp)
                dv.scalar_tensor_tensor(Pt[sS][:, 0:128], Pt[sS][:, 0:128],
                                        validc[:, qt:qt + 1], mSL[:, :], ALU.mult, ALU.mult)
                dv.tensor_scalar(Pt[sS][:, 128:256], Pt[sS][:, 128:256],
                                 validc[:, qt + 1:qt + 2], None, ALU.mult)
                e_msk = k.fin(dv.scalar_tensor_tensor(Pt[sS][:, 256:384], Pt[sS][:, 256:384],
                                                      validc[:, qt + 2:qt + 3], mUI[:, :],
                                                      ALU.mult, ALU.mult), "DV")
                k.wait_all(pe, [e_msk, evO[sO]])
                for s in range(3):
                    mm = pe.matmul(psO[sO][:, 0:65], Pt[sS][:, 128 * s:128 * s + 128],
                                   vtv[qt + s][:, 65 * h:65 * h + 65],
                                   start=(s == 0), stop=(s == 2))
                evP[sS] = e_pv = k.fin(mm, "PE")
                k.wait_all(dv, [e_pv, evA[sO]])
                e1 = k.fin(dv.tensor_scalar_add(rc[sO][:, :], psO[sO][:, 64:65], 1e-30), "DV")
                k.wait(dv, e1)
                e2 = k.fin(dv.reciprocal(rc2[sO][:, :], rc[sO][:, :]), "DV")
                k.wait(dv, e2)
                e_nrm = k.fin(dv.tensor_scalar(atm[sO][:, :], psO[sO][:, 0:64],
                                               rc2[sO][:, 0:1], None, ALU.mult), "DV")
                evO[sO] = e_nrm
                if "dbgp" in DBG and inst == 43:
                    dv.tensor_copy(nc.lookup_dbgp[0][:, 0:384], Pt[sS][:, 0:384])
                    dbge2 = k.fin(dv.tensor_copy(nc.lookup_dbgp[1][:, 0:384], psS[sS][:, 0:384]), "DV")
                    k.wait(sync, dbge2)
                    k.dma(sync.dma_start(out=g["dbgp"][0:128, :], in_=nc.lookup_dbgp[0][:, :]), "DL")
                    k.dma(sync.dma_start(out=g["dbgp"][128:256, :], in_=nc.lookup_dbgp[1][:, :]), "DL")
                if "dbgo" in DBG and inst == 43:
                    dv.tensor_copy(nc.lookup_dbgo[0][:, 0:128], psO[sO][:, 0:128])
                    dv.tensor_copy(nc.lookup_dbgo[0][:, 128:129], rc[sO][:, 0:1])
                    dv.tensor_copy(nc.lookup_dbgo[0][:, 129:130], rc2[sO][:, 0:1])
                    dbge = k.fin(dv.tensor_copy(nc.lookup_dbgo[0][:, 130:194], atm[sO][:, 0:64]), "DV")
                    k.wait(sync, dbge)
                    k.dma(sync.dma_start(out=g["dbgo"], in_=nc.lookup_dbgo[0][:, :]), "DL")
                    e_nrm = dbge
                k.wait_all(pe, [e_nrm, evT[sT]])
                e_tr = k.fin(pe.transpose(psTr[0:64, 128 * sT:128 * sT + 128],
                                          atm[sO][:, :], ident[:, :]), "PE")
                evA[sO] = e_tr
                k.wait(ac, e_tr)
                evT[sT] = k.fin(ac.copy(siv[ftq][rb:rb + 64, 128 * qt:128 * qt + 128],
                                        psTr[0:64, 128 * sT:128 * sT + 128]), "AC")
                inst += 1
        attn_done = k.now("AC")
        if "sind" in DBG:
            k.wait(sync, attn_done)
            sind = g["sind"].rearrange("(a p) n -> a p n", p=128)
            for ft1 in range(4):
                k.dma(sync.dma_start(out=sind[ft1], in_=siv[ft1]), "DL")
    k.barrier([sync, gp, pe, dv, ac])

    if "D" not in stages:
        k.wait(sync, k.now("DL"))
        k.wait(gp, k.now("DG"))
        return
    with ExitStack() as esD:
        sbD = lambda name, shape, dtype=F32: esD.enter_context(nc.sbuf_tensor(name, list(shape), dtype))
        psD = lambda name, shape, dtype=F32: esD.enter_context(nc.psum_tensor(name, list(shape), dtype))
        wo = sbD("wo", (128, 4 * 1024), BF16)
        wov = _ftv(wo, 1024)
        wstgD = [sbD(f"wstgD{i}", (128, 1024), F32) for i in range(2)]
        wo_d = g["Wout"].rearrange("(a p) n -> a p n", p=128)
        wo_ev = []
        stg_ev = [None, None]
        for kt in range(4):
            sl = kt % 2
            k.wait(sync, stg_ev[sl])
            e = k.dma(sync.dma_start(out=wstgD[sl][:, :], in_=wo_d[kt]), "DL")
            k.wait(ac, e)
            stg_ev[sl] = ee = k.fin(ac.copy(wov[kt], wstgD[sl][:, :]), "AC")
            wo_ev.append(ee)
        pD = [psD(f"pD{i}", (128, 512)) for i in range(2)]
        ev_evac = [None, None]
        for of in range(8):
            for ci, (c0, cn) in enumerate(_chunks(L0N)):
                slot = (of * 5 + ci) % 2
                k.wait_all(pe, wo_ev + [attn_done, ev_evac[slot]])
                for kt in range(4):
                    mm = pe.matmul(pD[slot][:, 0:cn], wov[kt][:, 128 * of:128 * of + 128],
                                   siv[kt][:, c0:c0 + cn], start=(kt == 0), stop=(kt == 3))
                e = k.fin(mm, "PE")
                k.wait(ac, e)
                ev_evac[slot] = k.fin(ac.copy(sv[of][:, c0:c0 + cn], pD[slot][:, 0:cn]), "AC")
        s_done = k.now("AC")
        if "s0" in DBG:
            k.wait(sync, s_done)
            for of in range(8):
                k.dma(sync.dma_start(out=g["s0"].rearrange("(a p) n -> a p n", p=128)[of],
                                     in_=sv[of]), "DL")
    es_cd.close()
    es_bc.close()
    k.barrier([sync, gp, pe, dv, ac])

    if "E" not in stages:
        k.wait(sync, k.now("DL"))
        k.wait(gp, k.now("DG"))
        return
    with ExitStack() as esE:
        sbE = lambda name, shape, dtype=F32: esE.enter_context(nc.sbuf_tensor(name, list(shape), dtype))
        psE = lambda name, shape, dtype=F32: esE.enter_context(nc.psum_tensor(name, list(shape), dtype))
        acc = sbE("accF", (128, 8 * L0N), BF16)
        accv = _ftv(acc, L0N)
        wg = sbE("wgf", (128, 8 * 1536), BF16)
        wu = sbE("wuf", (128, 8 * 1536), BF16)
        wd = sbE("wdf", (128, 12 * 1024), BF16)
        wgv, wuv, wdv = _ftv(wg, 1536), _ftv(wu, 1536), _ftv(wd, 1024)
        hbuf = sbE("hbuf", (128, 12 * 512), BF16)
        hv = _ftv(hbuf, 512)
        sg = [sbE(f"sgf{i}", (128, 512), BF16) for i in range(2)]
        pG = [psE(f"pG{i}", (128, 512)) for i in range(2)]
        pU = [psE(f"pU{i}", (128, 512)) for i in range(2)]
        pDn = [psE(f"pDn{i}", (128, 512)) for i in range(2)]
        wgd = g["wgT"].rearrange("(a p) n -> a p n", p=128)
        wud = g["wuT"].rearrange("(a p) n -> a p n", p=128)
        wdd = g["wdT"].rearrange("(a p) n -> a p n", p=128)
        es_stg = ExitStack()
        wstgE = [es_stg.enter_context(nc.sbuf_tensor("wstgE0", [128, 1536], I32))]
        hh_ev = []
        stg_ev = [None]
        cnv = 0
        for hh in range(2):
            k.wait_all(ac, hh_ev)
            wl_ev = []
            for dst, dsrc in ([(wgv[kt], wgd[kt][:, 1536 * hh:1536 * (hh + 1)]) for kt in range(8)]
                              + [(wuv[kt], wud[kt][:, 1536 * hh:1536 * (hh + 1)]) for kt in range(8)]
                              + [(wdv[kt][:, 0:1024], wdd[12 * hh + kt]) for kt in range(12)]):
                sl = 0
                cnv += 1
                k.wait(sync, stg_ev[sl])
                e = k.dma(sync.dma_start(out=wstgE[sl][:, 0:dsrc.shape[1]], in_=dsrc), "DL")
                k.wait(ac, e)
                stg_ev[sl] = ee = k.fin(ac.copy(dst, wstgE[sl][:, 0:dsrc.shape[1]]), "AC")
                wl_ev.append(ee)
            ev_sg = [None, None]
            sgfree = [None, None]
            ev_h = [None, None]
            ev_u = [None] * 12
            for ci, (c0, cn) in enumerate(_chunks(L0N)):
                for ht in range(12):
                    sl = ht % 2
                    k.wait_all(pe, wl_ev + [xn_done, ev_sg[sl], ev_h[sl]])
                    if ht == 0:
                        _warm(pe, pDn[(ci + 1) % 2][:, 0:512], ident[:, :],
                              xnv[0][:, 0:512], 12)
                    for kt in range(8):
                        mm = pe.matmul(pG[sl][:, 0:cn], wgv[kt][:, 128 * ht:128 * ht + 128],
                                       xnv[kt][:, L0S + c0:L0S + c0 + cn],
                                       start=(kt == 0), stop=(kt == 7))
                    e_g = k.fin(mm, "PE")
                    k.wait_all(ac, [e_g, sgfree[sl]])
                    ev_sg[sl] = k.fin(ac.activation(sg[sl][:, 0:cn], pG[sl][:, 0:cn],
                                                    AF.Silu, scale=1.0 / 127.0), "AC")
                    for kt in range(8):
                        mm = pe.matmul(pU[sl][:, 0:cn], wuv[kt][:, 128 * ht:128 * ht + 128],
                                       xnv[kt][:, L0S + c0:L0S + c0 + cn],
                                       start=(kt == 0), stop=(kt == 7))
                    e_um = k.fin(mm, "PE")
                    k.wait_all(dv, [e_um, ev_sg[sl]])
                    ev_u[ht] = ev_h[sl] = k.fin(
                        dv.tensor_tensor(hv[ht][:, 0:cn], sg[sl][:, 0:cn],
                                         pU[sl][:, 0:cn], ALU.mult), "DV")
                    sgfree[sl] = ev_h[sl]
                for of in range(8):
                    slot = of % 2
                    k.wait_all(pe, ev_u)
                    for kt in range(12):
                        mm = pe.matmul(pDn[slot][:, 0:cn], wdv[kt][:, 128 * of:128 * of + 128],
                                       hv[kt][:, 0:cn], start=(kt == 0), stop=(kt == 11))
                    e_d = k.fin(mm, "PE")
                    if hh == 0:
                        k.wait(ac, e_d)
                        e = k.fin(ac.copy(accv[of][:, c0:c0 + cn], pDn[slot][:, 0:cn]), "AC")
                    else:
                        k.wait(dv, e_d)
                        e = k.fin(dv.tensor_tensor(accv[of][:, c0:c0 + cn],
                                                   accv[of][:, c0:c0 + cn],
                                                   pDn[slot][:, 0:cn], ALU.add), "DV")
                    k.wait(pe, e)
            hh_ev = [k.now("PE"), k.now("AC"), k.now("DV")]
        ffn_done = (k.now("DV"), k.now("AC"))
        es_stg.close()
        k.barrier([sync, dv, ac])
        xs = [sbE(f"xsE{i}", (128, 512), F32) for i in range(2)]
        gt = [sbE(f"gtE{i}", (128, 512), BF16) for i in range(2)]
        x1t = [sbE(f"x1tE{i}", (128, 512), F32) for i in range(2)]
        x1d = g["x1sp"].rearrange("(a p) n -> a p n", p=128)
        st_ev = [None, None]
        gt_ev = [None, None]
        for of in range(8):
            for ci, (c0, cn) in enumerate(_chunks(L0N)):
                slot = (of * 5 + ci) % 2
                k.wait(sync, st_ev[slot])
                e_x = k.dma(sync.dma_start(out=xs[slot][:, 0:cn],
                                           in_=xTv[of][:, L0S + c0:L0S + c0 + cn]), "DL")
                k.wait_all(ac, [ffn_done[0], ffn_done[1], gt_ev[slot]])
                e_gt = k.fin(ac.activation(gt[slot][:, 0:cn], accv[of][:, c0:c0 + cn],
                                           AF.Silu, scale=1.0 / 16129.0), "AC")
                k.wait_all(dv, [e_gt, e_x])
                dv.tensor_tensor(gt[slot][:, 0:cn], gt[slot][:, 0:cn],
                                 sv[of][:, c0:c0 + cn], ALU.mult)
                gt_ev[slot] = e_c = k.fin(
                    dv.tensor_tensor(x1t[slot][:, 0:cn], xs[slot][:, 0:cn],
                                     gt[slot][:, 0:cn], ALU.add), "DV")
                k.wait(sync, e_c)
                st_ev[slot] = k.dma(sync.dma_start(out=x1d[of][:, c0:c0 + cn],
                                                   in_=x1t[slot][:, 0:cn]), "DL")
    es_de.close()
    x1_done = k.now("DL")
    k.barrier([sync, gp, pe, dv, ac])

    if "F" not in stages:
        k.wait(sync, k.now("DL"))
        k.wait(gp, k.now("DG"))
        return

    xn1v = [t[:, 0:L1N] for t in xnv]
    x1dv = g["x1sp"].rearrange("(a p) n -> a p n", p=128)

    with ExitStack() as esF1:
        sbF = lambda name, shape, dtype=F32: esF1.enter_context(nc.sbuf_tensor(name, list(shape), dtype))
        psF = lambda name, shape, dtype=F32: esF1.enter_context(nc.psum_tensor(name, list(shape), dtype))
        X1 = sbF("X1", (128, 8 * L1N), F32)
        X1v = _ftv(X1, L1N)
        x1_ev = []
        for ft in range(8):
            x1_ev.append(k.dma(sync.dma_start(out=X1v[ft], in_=x1dv[ft][:, 64:64 + L1N]), "DL"))
        wrkQ = [sbF(f"wrkQF{i}", (128, 512), F32) for i in range(2)]
        pS = psF("pSF", (1, 512))
        pQ = psF("pQF", (1, 512))
        pbcF1 = psF("pbcF1", (128, 512))
        pbcF2 = psF("pbcF2", (128, 512))
        srow = sbF("srowF", (1, L1N), F32)
        qrow = sbF("qrowF", (1, L1N), F32)
        q_ev = [None, None]
        for ci, (c0, cn) in enumerate(_chunks(L1N)):
            for ft in range(8):
                slot = ft % 2
                k.wait(ac, x1_ev[ft])
                k.wait(ac, q_ev[slot])
                sq = k.fin(ac.activation(wrkQ[slot][:, 0:cn], X1v[ft][:, c0:c0 + cn],
                                         AF.Square), "AC")
                k.wait(pe, sq)
                pe.matmul(pS[0:1, 0:cn], onescf[:, :], X1v[ft][:, c0:c0 + cn],
                          start=(ft == 0), stop=(ft == 7))
                mm = pe.matmul(pQ[0:1, 0:cn], onescf[:, :], wrkQ[slot][:, 0:cn],
                               start=(ft == 0), stop=(ft == 7))
                q_ev[slot] = k.fin(mm, "PE")
            k.wait(dv, q_ev[1])
            dv.tensor_copy(srow[0:1, c0:c0 + cn], pS[0:1, 0:cn])
            e = k.fin(dv.tensor_copy(qrow[0:1, c0:c0 + cn], pQ[0:1, 0:cn]), "DV")
            k.wait(pe, e)
        arow = sbF("arowF", (1, L1N), F32)
        brow = sbF("browF", (1, L1N), F32)
        dv.tensor_scalar_mul(srow[:, :], srow[:, :], 1.0 / D)
        dv.tensor_tensor(brow[:, :], srow[:, :], srow[:, :], ALU.mult)
        dv.scalar_tensor_tensor(brow[:, :], qrow[:, :], 1.0 / D, brow[:, :],
                                ALU.mult, ALU.subtract)
        ev = k.fin(dv.tensor_scalar_add(brow[:, :], brow[:, :], 1e-5), "DV")
        k.wait(ac, ev)
        ev = k.fin(ac.activation(brow[:, :], brow[:, :], AF.Sqrt), "AC")
        k.wait(dv, ev)
        e2 = k.fin(dv.reciprocal(arow[:, :], brow[:, :]), "DV")
        k.wait(dv, e2)
        dv.tensor_tensor(brow[:, :], srow[:, :], arow[:, :], ALU.mult)
        ev = k.fin(dv.tensor_scalar_mul(brow[:, :], brow[:, :], -1.0), "DV")
        Abb = sbF("AbbF", (128, L1N), F32)
        Bbb = sbF("BbbF", (128, L1N), F32)
        k.wait(pe, ev)
        bc_ev = []
        for (c0, cn) in _chunks(L1N):
            k.fin(pe.matmul(pbcF1[:, 0:cn], onesrf[0:1, :], arow[0:1, c0:c0 + cn],
                            start=True, stop=True), "PE")
            e2 = k.fin(pe.matmul(pbcF2[:, 0:cn], onesrf[0:1, :], brow[0:1, c0:c0 + cn],
                                 start=True, stop=True), "PE")
            k.wait(ac, e2)
            ac.copy(Abb[:, c0:c0 + cn], pbcF1[:, 0:cn])
            e3 = k.fin(ac.copy(Bbb[:, c0:c0 + cn], pbcF2[:, 0:cn]), "AC")
            bc_ev.append(e3)
            k.wait(pe, e3)
        wrk = sbF("wrkLNF", (128, L1N), F32)
        k.wait_all(dv, bc_ev)
        for ft in range(8):
            dv.tensor_tensor(wrk[:, :], X1v[ft], Abb[:, :], ALU.mult)
            k.fin(dv.tensor_tensor(xn1v[ft], wrk[:, :], Bbb[:, :], ALU.add), "DV")
        xn1_done = k.now("DV")
        if "xn1" in DBG:
            k.wait(sync, xn1_done)
            for ft in range(8):
                k.dma(sync.dma_start(out=g["xn1"].rearrange("(a p) n -> a p n", p=128)[ft],
                                     in_=xn1v[ft]), "DL")
    k.barrier([sync, gp, pe, dv, ac])

    es_l1 = k.es.enter_context(ExitStack())
    q_in = es_l1.enter_context(nc.sbuf_tensor("q_in", [128, 8 * L1N], BF16))
    k_in = es_l1.enter_context(nc.sbuf_tensor("k_in", [128, 8 * L1N], BF16))
    v_in = es_l1.enter_context(nc.sbuf_tensor("v_in", [128, 8 * L1N], BF16))
    qiv, kiv, viv = _ftv(q_in, L1N), _ftv(k_in, L1N), _ftv(v_in, L1N)
    convw = es_l1.enter_context(nc.sbuf_tensor("convwF", [128, 3 * 8 * 4], F32))
    convv_ = _ftv(convw, 4)
    m3 = es_l1.enter_context(nc.sbuf_tensor("m3F", [128, 3], F32))
    selh2 = es_l1.enter_context(nc.sbuf_tensor("selh2F", [128, 8 * 16], BF16))
    selh2v = _ftv(selh2, 16)
    selhb = es_l1.enter_context(nc.sbuf_tensor("selhbF", [16, 8 * 128], BF16))
    selhbv = _ftv(selhb, 128)
    selhf = es_l1.enter_context(nc.sbuf_tensor("selhfF", [16, 8 * 128], F32))
    selhfv = _ftv(selhf, 128)
    cbc = es_l1.enter_context(nc.sbuf_tensor("cbcF", [128, 9], F32))
    es_mid = k.es.enter_context(ExitStack())
    abq = es_mid.enter_context(nc.sbuf_tensor("abqF", [48, L1N], F32))

    ld_ev = []
    for tsb, name in [(m3, "mask3"), (selh2, "selh2"), (selhb, "selh"),
                      (selhf, "selhf"), (cbc, "ctrlb2")]:
        ld_ev.append(k.dma(sync.dma_start(out=tsb[:, :], in_=g[name]), "DL"))
    for ci_, name in [(0, "convq"), (1, "convk"), (2, "convv")]:
        cd = g[name].rearrange("(a p) n -> a p n", p=128)
        for ft in range(8):
            ld_ev.append(k.dma(sync.dma_start(out=convv_[8 * ci_ + ft], in_=cd[ft]), "DL"))

    with ExitStack() as esF2:
        sbF2 = lambda name, shape, dtype=F32: esF2.enter_context(nc.sbuf_tensor(name, list(shape), dtype))
        psF2 = lambda name, shape, dtype=F32: esF2.enter_context(nc.psum_tensor(name, list(shape), dtype))
        og = sbF2("ogF", (128, 8 * L1N), BF16)
        ogv = _ftv(og, L1N)
        wstg = sbF2("wstgF", (128, 128), F32)
        wti = [sbF2(f"wtiF{i}", (128, 8 * 128), BF16) for i in range(2)]
        pF = [psF2(f"pFF{i}", (128, 512)) for i in range(2)]
        qkvT_d = g["qkvT"].rearrange("(a p) n -> a p n", p=128)
        ctrlT_d = g["ctrlT"].rearrange("(a p) n -> a p n", p=128)
        stg_ev = None
        wfree = [None, None]
        ev_evac = [None, None]
        for of in range(33):
            wsl = of % 2
            wtiv = _ftv(wti[wsl], 128)
            ncw = 128
            cnv = []
            for kt in range(8):
                k.wait_all(sync, [stg_ev])
                if of < 24:
                    dsrc = qkvT_d[kt][:, 128 * of:128 * of + 128]
                else:
                    c0w = 128 * (of - 24)
                    dsrc = ctrlT_d[kt][:, c0w:c0w + 128]
                e = k.dma(sync.dma_start(out=wstg[:, 0:ncw], in_=dsrc), "DL")
                k.wait_all(ac, [e, wfree[wsl]])
                stg_ev = ee = k.fin(ac.copy(wtiv[kt][:, 0:ncw], wstg[:, 0:ncw]), "AC")
                cnv.append(ee)
            for ci, (c0, cn) in enumerate(_chunks(L1N)):
                slot = (of * 5 + ci) % 2
                k.wait_all(pe, cnv + [xn1_done, ev_evac[slot]])
                if ci == 0:
                    _warm(pe, pF[slot][:, 0:512], ident[:, :], xn1v[0][:, 0:512], 12)
                for kt in range(8):
                    mm = pe.matmul(pF[slot][0:ncw, 0:cn], wtiv[kt][:, 0:ncw],
                                   xn1v[kt][:, c0:c0 + cn], start=(kt == 0), stop=(kt == 7))
                e_mm = k.fin(mm, "PE")
                if ci == 4:
                    wfree[wsl] = e_mm
                k.wait(ac, e_mm)
                if of < 24:
                    dst = (qiv, kiv, viv)[of // 8][of % 8]
                    ev_evac[slot] = k.fin(ac.copy(dst[:, c0:c0 + cn], pF[slot][:, 0:cn]), "AC")
                else:
                    t = of - 24
                    evs = None
                    if t < 8:
                        evs = ac.activation(ogv[t][0:64, c0:c0 + cn],
                                            pF[slot][64:128, 0:cn], AF.Sigmoid,
                                            bias=cbc[64:128, t:t + 1])
                    if t >= 1:
                        evs = ac.activation(ogv[t - 1][64:128, c0:c0 + cn],
                                            pF[slot][0:64, 0:cn], AF.Sigmoid,
                                            bias=cbc[0:64, t:t + 1])
                    if t == 0:
                        evs = ac.activation(abq[0:16, c0:c0 + cn], pF[slot][0:16, 0:cn],
                                            AF.Sigmoid, bias=cbc[0:16, t:t + 1])
                        evs = ac.activation(abq[32:48, c0:c0 + cn], pF[slot][32:48, 0:cn],
                                            AF.Sigmoid, bias=cbc[32:48, t:t + 1])
                    ev_evac[slot] = k.fin(evs, "AC")
        proj_done = [k.now("AC")]
        k.wait_all(sync, proj_done)
        og_d = g["ogsp"].rearrange("(a p) n -> a p n", p=128)
        xn1_d = g["xn1sv"].rearrange("(a p) n -> a p n", p=128)
        sp_ev = []
        for ft in range(8):
            sp_ev.append(k.dma(sync.dma_start(out=og_d[ft], in_=ogv[ft][:, SOFF:SOFF + SPAN]), "DL"))
            sp_ev.append(k.dma(sync.dma_start(out=xn1_d[ft], in_=xn1v[ft]), "DL"))
        k.wait_all(sync, sp_ev)
        k.barrier([sync, gp, pe, dv, ac])
    es_xn.close()
    k.barrier([sync, gp, pe, dv, ac])

    kbt = es_l1.enter_context(nc.sbuf_tensor("kbt", [128, SPAN], BF16, side="right"))
    rows = es_mid.enter_context(nc.sbuf_tensor("rowsF", [16, 5 * SPAN], F32))
    rowv = _ftv(rows, SPAN)
    rbf = es_mid.enter_context(nc.sbuf_tensor("rbfF", [16, SPAN], BF16))
    sct = es_l1.enter_context(nc.sbuf_tensor("sctF", [128, 8 * 32], F32, side="right"))
    sctv = _ftv(sct, 32)

    with ExitStack() as esF3:
        sbF3 = lambda name, shape, dtype=F32: esF3.enter_context(nc.sbuf_tensor(name, list(shape), dtype))
        psF3 = lambda name, shape, dtype=F32: esF3.enter_context(nc.psum_tensor(name, list(shape), dtype))
        k.wait_all(dv, proj_done)
        e = k.fin(dv.tensor_scalar(abq[0:16, SOFF:SOFF + SPAN], abq[0:16, SOFF:SOFF + SPAN],
                                   0.98, 0.01, ALU.mult, ALU.add), "DV")
        ab_done = [e]

        es_acc = ExitStack()
        acc = es_acc.enter_context(nc.sbuf_tensor("accF3", [128, SPAN], F32))
        conv_done = []
        k.wait_all(dv, ld_ev)
        for ci_, tiles in [(0, qiv), (1, kiv), (2, viv)]:
            for ft in range(8):
                tl = tiles[ft]
                k.wait_all(dv, proj_done + conv_done[-1:])
                dv.tensor_tensor(tl[:, 61:64], tl[:, 61:64], m3[:, :], ALU.mult)
                dv.tensor_scalar(acc[:, :], tl[:, 61:61 + SPAN],
                                 convv_[8 * ci_ + ft][:, 0:1], None, ALU.mult)
                for j in range(1, 4):
                    dv.scalar_tensor_tensor(acc[:, :], tl[:, 61 + j:61 + j + SPAN],
                                            convv_[8 * ci_ + ft][:, j:j + 1], acc[:, :],
                                            ALU.mult, ALU.add)
                e = k.fin(dv.engine_nop(), "DV")
                k.wait(ac, e)
                conv_done.append(k.fin(ac.activation(tl[:, SOFF:SOFF + SPAN], acc[:, :],
                                                     AF.Silu), "AC"))
        conv_all = k.now("AC")
        k.wait(dv, conv_all)
        e = k.fin(dv.tensor_copy(rowv[4][0:16, :], abq[32:48, SOFF:SOFF + SPAN]), "DV")
        beta_done = e
        es_acc.close()

        sqb = sbF3("sqbF", (128, SPAN), BF16)
        pR = psF3("pRF", (16, SPAN))
        for which, tiles, dstrow in [(0, qiv, 2), (1, kiv, 3)]:
            sq_ev = None
            for ft in range(8):
                k.wait_all(ac, [conv_all, sq_ev])
                e_sq = k.fin(ac.activation(sqb[:, :], tiles[ft][:, SOFF:SOFF + SPAN],
                                           AF.Square), "AC")
                k.wait(pe, e_sq)
                for (c0, cn) in _chunks(SPAN):
                    mm = pe.matmul(pR[0:16, c0:c0 + cn], selh2v[ft][:, 0:16],
                                   sqb[:, c0:c0 + cn], start=(ft == 0), stop=(ft == 7))
                sq_ev = k.fin(mm, "PE")
            e = sq_ev
            k.wait(dv, e)
            e = k.fin(dv.tensor_copy(rowv[1][0:16, :], pR[0:16, 0:SPAN]), "DV")
            k.wait(pe, e)
            k.wait(ac, e)
            e = k.fin(ac.activation(rowv[1][0:16, :], rowv[1][0:16, :], AF.Sqrt), "AC")
            k.wait(dv, e)
            e = k.fin(dv.tensor_scalar_max(rowv[1][0:16, :], rowv[1][0:16, :], 1e-6), "DV")
            k.wait(dv, e)
            e = k.fin(dv.reciprocal(rowv[dstrow][0:16, :], rowv[1][0:16, :]), "DV")
            k.wait(dv, e)
            l2n_done = e
        k.wait_all(ac, ab_done + [l2n_done])
        e = k.fin(ac.activation(rowv[1][0:16, :], abq[0:16, SOFF:SOFF + SPAN], AF.Ln), "AC")
        k.wait(dv, e)
        cvv = rowv[1][0:16, :].rearrange("p (c u) -> p c u", u=CH)
        for sshift in [1, 2, 4, 8, 16, 32]:
            dv.tensor_tensor(cvv[:, :, sshift:CH], cvv[:, :, sshift:CH],
                             cvv[:, :, 0:CH - sshift], ALU.add)
        e = k.fin(dv.engine_nop(), "DV")
        k.wait(ac, e)
        e_gam = k.fin(ac.activation(rowv[0][0:16, :], rowv[1][0:16, :], AF.Exp), "AC")
        k.wait(dv, e_gam)
        gcv2 = rowv[0][0:16, :].rearrange("p (c u) -> p c u", u=CH)
        e_gc = k.fin(dv.tensor_copy(rowv[1][0:16, 0:NCH], gcv2[:, :, CH - 1]), "DV")

        cbt = sbF3("cbtF", (128, SPAN), BF16)
        pB = [psF3(f"pBF{i}", (128, 512)) for i in range(2)]
        k.wait(pe, e_gc)
        _warm(pe, pB[1][:, 0:512], ident[:, :], cbt[:, 0:512], 12)
        for p in range(8):
            e_mm = k.fin(pe.matmul(pB[0][:, 0:NCH], selhfv[p][:, :], rowv[1][0:16, 0:NCH],
                                   start=True, stop=True), "PE")
            k.wait(ac, e_mm)
            e_sc = k.fin(ac.copy(sctv[p], pB[0][:, 0:NCH]), "AC")
            k.wait(pe, e_sc)
        sct_done = k.now("AC")
        pe_sct_done = k.now("PE")
        RC = rowv[1]
        last_mm = None
        mul_done = []
        k.wait_all(dv, [beta_done])
        k.wait(dv, pe_sct_done)
        for r_ in [3, 2, 1, 4]:
            if r_ == 3:
                dv.tensor_tensor(RC[0:16, :], rowv[0][0:16, :], rowv[3][0:16, :], ALU.mult)
                e = k.fin(dv.tensor_tensor(RC[0:16, :], RC[0:16, :],
                                           rowv[4][0:16, :], ALU.mult), "DV")
            elif r_ == 2:
                e = k.fin(dv.reciprocal(RC[0:16, :], rowv[0][0:16, :]), "DV")
                k.wait(dv, e)
                e = k.fin(dv.tensor_tensor(RC[0:16, :], RC[0:16, :],
                                           rowv[3][0:16, :], ALU.mult), "DV")
            elif r_ == 1:
                e = k.fin(dv.tensor_tensor(RC[0:16, :], rowv[0][0:16, :],
                                           rowv[2][0:16, :], ALU.mult), "DV")
            else:
                e = k.fin(dv.tensor_copy(RC[0:16, :], rowv[4][0:16, :]), "DV")
            k.wait_all(dv, [e, last_mm])
            e_rb = k.fin(dv.tensor_copy(rbf[0:16, 0:SPAN], RC[0:16, :]), "DV")
            iskb = (r_ == 3)
            tiles = (None, qiv, kiv, kiv, viv)[r_]
            for ft in range(8):
                ev_evac2 = None
                for ci, (c0, cn) in enumerate(_chunks(SPAN)):
                    slot = ci % 2
                    k.wait_all(pe, [e_rb, ev_evac2, sct_done, last_mm])
                    last_mm = e_mm = k.fin(pe.matmul(pB[slot][:, 0:cn], selhbv[ft][:, :],
                                                     rbf[0:16, c0:c0 + cn],
                                                     start=True, stop=True), "PE")
                    k.wait_all(ac, [e_mm] + (mul_done[-1:] if ci == 0 else []))
                    ev_evac2 = k.fin(ac.copy(cbt[:, c0:c0 + cn], pB[slot][:, 0:cn]), "AC")
                k.wait_all(dv, [ev_evac2] + mul_done[-1:])
                if iskb:
                    e_m = k.fin(dv.tensor_tensor(kbt[:, :], tiles[ft][:, SOFF:SOFF + SPAN],
                                                 cbt[:, :], ALU.mult), "DV")
                    k.wait(sync, e_m)
                    e_kb = k.dma(sync.dma_start(out=g["kbsv"].rearrange("(a p) n -> a p n", p=128)[ft],
                                                in_=kbt[:, :]), "DL")
                    mul_done.append(e_kb)
                else:
                    e_m = k.fin(dv.tensor_tensor(tiles[ft][:, SOFF:SOFF + SPAN],
                                                 tiles[ft][:, SOFF:SOFF + SPAN],
                                                 cbt[:, :], ALU.mult), "DV")
                mul_done.append(e_m)
        scaled_done = k.now("DV")
        if "l1t" in DBG:
            k.wait(sync, scaled_done)
            l1t_d = g["l1t"].rearrange("(a w) n -> a w n", w=128)
            for ft in range(8):
                k.dma(sync.dma_start(out=l1t_d[ft], in_=kiv[ft][:, SOFF:SOFF + SPAN]), "DL")
                k.dma(sync.dma_start(out=l1t_d[8 + ft], in_=g["kbsv"].rearrange("(a p) n -> a p n", p=128)[ft]), "DL")
                k.dma(sync.dma_start(out=l1t_d[16 + ft], in_=viv[ft][:, SOFF:SOFF + SPAN]), "DL")
            k.dma(sync.dma_start(out=g["l1r"][0:16, 0:SPAN], in_=rowv[0][0:16, :]), "DL")
            k.dma(sync.dma_start(out=g["l1r"][16:32, 0:SPAN], in_=abq[0:16, SOFF:SOFF + SPAN]), "DL")
            k.dma(sync.dma_start(out=g["l1r"][32:48, 0:SPAN], in_=abq[32:48, SOFF:SOFF + SPAN]), "DL")
        k.wait(sync, k.now("DL"))
    es_mid.close()
    k.barrier([sync, gp, pe, dv, ac])

    if "G" not in stages:
        k.wait(sync, k.now("DL"))
        k.wait(gp, k.now("DG"))
        return
    emit_stageG(nc, k, g, ident, qiv, kiv, viv, sctv, DBG, glim)
    es_l1.close()
    k.barrier([sync, gp, pe, dv, ac])

    if "H" not in stages:
        k.wait(sync, k.now("DL"))
        k.wait(gp, k.now("DG"))
        return
    emit_stageH(nc, k, g, ident, DBG)
    k.wait(sync, k.now("DL"))
    k.wait(gp, k.now("DG"))



def emit_stageG(nc, k, g, ident, qiv, kiv, viv, sctv, DBG, glim=(NPAIR, NCH // 4)):
    sync, gp, pe, dv, ac = nc.sync, nc.gpsimd, nc.tensor, nc.vector, nc.scalar
    qs = [t[:, SOFF:SOFF + SPAN] for t in qiv]
    ks = [t[:, SOFF:SOFF + SPAN] for t in kiv]
    vs = [t[:, SOFF:SOFF + SPAN] for t in viv]
    kb_d = g["kbsv"].rearrange("(a p) n -> a p n", p=128)
    with ExitStack() as esG:
        sb = lambda name, shape, dtype=F32: esG.enter_context(nc.sbuf_tensor(name, list(shape), dtype))
        ps = lambda name, shape, dtype=F32: esG.enter_context(nc.psum_tensor(name, list(shape), dtype))
        msl4 = sb("msl4G", (128, 512), BF16)
        msu4 = sb("msu4G", (128, 512), BF16)
        mui4 = sb("mui4G", (128, 512), BF16)
        mbd8 = sb("mbd8G", (128, 1024), BF16)
        ld_ev = []
        for t, name in [(msl4, "msl4"), (msu4, "msu4"), (mui4, "mui4"), (mbd8, "mbd8")]:
            ld_ev.append(k.dma(sync.dma_start(out=t[:, :], in_=g[name]), "DL"))
        qP = sb("qPG", (128, NCH * 128), BF16)
        khP = sb("khPG", (128, NCH * 128), BF16)
        kbP = sb("kbPG", (128, NCH * 128), BF16)
        UF = sb("UFG", (128, NCH * 256), BF16)
        FT = sb("FTG", (128, NCH * 128), BF16)
        KTM = sb("KTMG", (128, NCH * 128), BF16)
        ptg = [sb(f"ptgG{i}", (128, 512), BF16) for i in range(2)]
        kbl = sb("kblG", (128, SPAN), BF16)
        MPRE = sb("MPREG", (128, (NCH + 1) * 128), BF16)
        ST0b = sb("ST0G", (128, (NCH + 1) * 128), BF16)
        MT2 = sb("MT2G", (128, 2 * 128), BF16)
        WMC = sb("WMCG", (128, 2 * 128), BF16)
        NAg = sb("NAgG", (128, 512), BF16)
        NATg = sb("NATgG", (128, 512), BF16)
        PgA = sb("PgAG", (128, 512), BF16)
        PgAT = sb("PgATG", (128, 512), BF16)
        negNc = sb("negNcG", (128, 128), BF16)
        MCf = sb("MCfG", (128, 128), F32)
        CCf = sb("CCfG", (128, 128), F32)
        psA = ps("psAG", (128, 512))
        psB = ps("psBG", (128, 512))
        psT = ps("psTG", (128, 1024), BF16)
        psC = [ps(f"psCG{i}", (128, 128)) for i in range(2)]
        psD = [ps(f"psDG{i}", (128, 128)) for i in range(2)]

        e_z = None
        for t in (qP, khP, kbP):
            e_z = k.fin(dv.memset(t[:, :], 0.0), "DV")
        k.wait(pe, e_z)
        for j in range(4):
            pe.transpose(psT[64:128, 256 * j:256 * j + 64],
                         qP[64:128, 0:64], ident[64:128, 64:128])
            pe.transpose(psT[0:64, 256 * j + 64:256 * j + 128],
                         qP[0:64, 64:128], ident[0:64, 0:64])

        ufd = g["ufsp"]; ftd = g["ftsp"]; ktd = g["ktsp"]; ptd = g["ptsp"]
        mpd = g["mpsp"]; std = g["stsp"]; mcd = g["mcsp"]; ccd = g["ccsp"]
        qtd = g["qtsp"]
        spill_ev = []
        k.wait_all(dv, ld_ev + [e_z])

        def pview(t):
            v3 = t.rearrange("p (c u) -> p c u", u=128)
            return v3

        for p in range(glim[0]):
            PH = glim[3] if len(glim) > 3 else 31
            e_kb = k.dma(sync.dma_start(out=kbl[:, :], in_=kb_d[p]), "DL")
            k.wait_all(dv, spill_ev + [e_kb])
            k.wait_all(ac, spill_ev)
            pb_ev = []
            for (dst, srcv) in [(qP, qs), (khP, ks), (kbP, None)]:
                d3 = pview(dst[:, :])
                if srcv is None:
                    se = kbl[0:64, :].rearrange("p (c u) -> p c u", u=CH)
                    so = kbl[64:128, :].rearrange("p (c u) -> p c u", u=CH)
                else:
                    se = srcv[p][0:64, :].rearrange("p (c u) -> p c u", u=CH)
                    so = srcv[p][64:128, :].rearrange("p (c u) -> p c u", u=CH)
                dv.tensor_copy(d3[0:64, :, 0:64], se)
                pb_ev.append(k.fin(dv.tensor_copy(d3[64:128, :, 64:128], so), "DV"))
            dv.tensor_copy(MPRE[:, 0:128], ident[:, :])
            dv.tensor_copy(MT2[:, 0:128], ident[:, :])
            e_init = k.fin(dv.memset(ST0b[:, 0:128], 0.0), "DV")
            ev_psA = None
            ev_psB = None
            ev_psT = None
            ev_chain = e_init
            pt_sp = [None, None]
            for gi in range(glim[1]):
                cs = [4 * gi + j for j in range(4)]
                if not (PH & 1):
                    break
                k.wait_all(pe, [ev_psA, e_init] + pb_ev)
                _warm(pe, psA[:, :], ident[:, :], msl4[:, :], 12)
                for j, c in enumerate(cs):
                    mm = pe.matmul(psA[:, 128 * j:128 * j + 128], kbP[:, 128 * c:128 * c + 128],
                                   khP[:, 128 * c:128 * c + 128], start=True, stop=True)
                e = k.fin(mm, "PE")
                k.wait(dv, e)
                ev_psA = k.fin(dv.tensor_tensor(NAg[:, :], psA[:, :], msl4[:, :], ALU.mult), "DV")
                k.wait_all(pe, [ev_psB])
                for j, c in enumerate(cs):
                    mm = pe.matmul(psB[:, 128 * j:128 * j + 128], khP[:, 128 * c:128 * c + 128],
                                   kbP[:, 128 * c:128 * c + 128], start=True, stop=True)
                e = k.fin(mm, "PE")
                k.wait(dv, e)
                ev_psB = k.fin(dv.tensor_tensor(NATg[:, :], psB[:, :], msu4[:, :], ALU.mult), "DV")
                k.wait_all(pe, [ev_psA])
                for j, c in enumerate(cs):
                    mm = pe.matmul(psA[:, 128 * j:128 * j + 128], khP[:, 128 * c:128 * c + 128],
                                   qP[:, 128 * c:128 * c + 128], start=True, stop=True)
                e = k.fin(mm, "PE")
                k.wait_all(dv, [e, pt_sp[gi % 2]])
                e_ptw = k.fin(dv.tensor_tensor(ptg[gi % 2][:, :], psA[:, :],
                                               mui4[:, :], ALU.mult), "DV")
                ev_psA = e_ptw
                k.wait(sync, e_ptw)
                pt_sp[gi % 2] = k.dma(sync.dma_start(out=ptd[p][:, 512 * gi:512 * gi + 512],
                                                     in_=ptg[gi % 2][:, :]), "DL")
                if not (PH & 2):
                    continue
                k.wait_all(pe, [ev_psT])
                for j, c in enumerate(cs):
                    pe.transpose(psT[0:64, 256 * j:256 * j + 64],
                                 vs[p][0:64, CH * c:CH * c + CH], ident[0:64, 0:64])
                    pe.transpose(psT[64:128, 256 * j + 64:256 * j + 128],
                                 vs[p][64:128, CH * c:CH * c + CH], ident[64:128, 64:128])
                    mm = pe.transpose(psT[:, 256 * j + 128:256 * j + 256],
                                      kbP[:, 128 * c:128 * c + 128], ident[:, :])
                e = k.fin(mm, "PE")
                k.wait(dv, e)
                e_uf0 = k.fin(dv.tensor_tensor(UF[:, 1024 * gi:1024 * gi + 1024], psT[:, :],
                                               mbd8[:, :], ALU.mult), "DV")
                k.wait_all(pe, [e_uf0])
                for j, c in enumerate(cs):
                    mm = pe.transpose(psT[:, 128 * j:128 * j + 128],
                                      khP[:, 128 * c:128 * c + 128], ident[:, :])
                e = k.fin(mm, "PE")
                k.wait(ac, e)
                ev_psT = k.fin(ac.copy(KTM[:, 512 * gi:512 * gi + 512], psT[:, 0:512]), "AC")
                if not (PH & 4):
                    continue
                powers = [(PgA, PgAT), (NAg, NATg), (PgA, PgAT), (NAg, NATg)]
                curP, curT = NAg, NATg
                e_apl = e_uf0
                for lev in range(6):
                    for rnd in range(2):
                        k.wait_all(pe, [ev_psB, e_apl, ev_psT])
                        for j in (0, 1):
                            c = cs[2 * rnd + j]
                            mm = pe.matmul(psB[:, 256 * j:256 * j + 256],
                                           curT[:, 128 * (2 * rnd + j):128 * (2 * rnd + j) + 128],
                                           UF[:, 256 * c:256 * c + 256], start=True, stop=True)
                        e = k.fin(mm, "PE")
                        k.wait(dv, e)
                        ev_psB = k.fin(dv.tensor_tensor(
                            UF[:, 1024 * gi + 512 * rnd:1024 * gi + 512 * rnd + 512],
                            UF[:, 1024 * gi + 512 * rnd:1024 * gi + 512 * rnd + 512],
                            psB[:, :], ALU.add), "DV")
                        e_apl = ev_psB
                    if lev == 5:
                        break
                    if lev < 4:
                        nxtP, nxtT = powers[lev]
                        k.wait_all(pe, [ev_psA, ev_psB])
                        for j in range(4):
                            mm = pe.matmul(psA[:, 128 * j:128 * j + 128], curT[:, 128 * j:128 * j + 128],
                                           curP[:, 128 * j:128 * j + 128], start=True, stop=True)
                        e = k.fin(mm, "PE")
                        k.wait(ac, e)
                        ev_psA = e_p = k.fin(ac.copy(nxtP[:, :], psA[:, :]), "AC")
                        k.wait_all(pe, [ev_psB, e_p])
                        for j in range(4):
                            mm = pe.matmul(psB[:, 128 * j:128 * j + 128], curP[:, 128 * j:128 * j + 128],
                                           curT[:, 128 * j:128 * j + 128], start=True, stop=True)
                        e = k.fin(mm, "PE")
                        k.wait(ac, e)
                        ev_psB = k.fin(ac.copy(nxtT[:, :], psB[:, :]), "AC")
                        curP, curT = nxtP, nxtT
                    else:
                        dstT = PgA if curP is not PgA else PgB
                        k.wait_all(pe, [ev_psA])
                        for j in range(4):
                            mm = pe.matmul(psA[:, 128 * j:128 * j + 128], curP[:, 128 * j:128 * j + 128],
                                           curT[:, 128 * j:128 * j + 128], start=True, stop=True)
                        e = k.fin(mm, "PE")
                        k.wait(ac, e)
                        ev_psA = k.fin(ac.copy(dstT[:, :], psA[:, :]), "AC")
                        curT = dstT
                if not (PH & 8):
                    continue
                k.wait_all(pe, [ev_psT, ev_psB])
                for j, c in enumerate(cs):
                    mm = pe.transpose(psT[:, 128 * j:128 * j + 128],
                                      UF[:, 256 * c + 128:256 * c + 256], ident[:, :])
                e = k.fin(mm, "PE")
                k.wait(ac, e)
                ev_psT = k.fin(ac.activation(FT[:, 512 * gi:512 * gi + 512], psT[:, 0:512],
                                             AF.Copy, scale=-1.0), "AC")
                for j, c in (enumerate(cs) if (len(glim) < 3 or glim[2]) else []):
                    so, sn = c % 2, (c + 1) % 2
                    k.wait_all(pe, [ev_psB, ev_psT, ev_chain]
                               + ([ev_psA] if j == 0 else []))
                    if j == 0:
                        _warm(pe, psA[:, :], ident[:, :], msl4[:, :], 12)
                    e = k.fin(pe.matmul(psC[0][:, :], UF[:, 256 * c + 128:256 * c + 256],
                                        KTM[:, 128 * c:128 * c + 128],
                                        start=True, stop=True), "PE")
                    k.wait(dv, e)
                    e_nc = k.fin(dv.tensor_scalar_mul(negNc[:, :], psC[0][:, :], -1.0), "DV")
                    k.wait_all(pe, [e_nc])
                    pe.matmul(psC[1][:, :], ident[:, :], MT2[:, 128 * so:128 * so + 128],
                              start=True, stop=False)
                    e = k.fin(pe.matmul(psC[1][:, :], negNc[:, :],
                                        MT2[:, 128 * so:128 * so + 128],
                                        start=False, stop=True), "PE")
                    k.wait(ac, e)
                    e_mt = k.fin(ac.activation(MT2[:, 128 * sn:128 * sn + 128], psC[1][:, :],
                                               AF.Copy, scale=sctv[p][:, c:c + 1]), "AC")
                    pe.matmul(psC[0][:, :], ident[:, :], MPRE[:, 128 * c:128 * c + 128],
                              start=True, stop=False)
                    e = k.fin(pe.matmul(psC[0][:, :], MT2[:, 128 * so:128 * so + 128],
                                        negNc[:, :], start=False, stop=True), "PE")
                    k.wait(ac, e)
                    e_mp = k.fin(ac.activation(MPRE[:, 128 * (c + 1):128 * (c + 2)], psC[0][:, :],
                                               AF.Copy, scale=sctv[p][:, c:c + 1]), "AC")
                    if c == NCH - 1:
                        e_mp = k.fin(ac.activation(MCf[:, :], psC[0][:, :], AF.Copy,
                                                   scale=sctv[p][:, c:c + 1]), "AC")
                    k.wait_all(pe, [e_mt, e_mp])
                    pe.matmul(psD[0][:, :], ident[:, :], UF[:, 256 * c:256 * c + 128],
                              start=True, stop=False)
                    e = k.fin(pe.matmul(psD[0][:, :], FT[:, 128 * c:128 * c + 128],
                                        ST0b[:, 128 * c:128 * c + 128],
                                        start=False, stop=True), "PE")
                    k.wait(ac, e)
                    e_wm = k.fin(ac.copy(WMC[:, 128 * so:128 * so + 128], psD[0][:, :]), "AC")
                    k.wait_all(pe, [e_wm])
                    pe.matmul(psD[1][:, :], ident[:, :], ST0b[:, 128 * c:128 * c + 128],
                              start=True, stop=False)
                    e = k.fin(pe.matmul(psD[1][:, :], KTM[:, 128 * c:128 * c + 128],
                                        WMC[:, 128 * so:128 * so + 128],
                                        start=False, stop=True), "PE")
                    k.wait(ac, e)
                    e_st = k.fin(ac.activation(ST0b[:, 128 * (c + 1):128 * (c + 2)], psD[1][:, :],
                                               AF.Copy, scale=sctv[p][:, c:c + 1]), "AC")
                    if c == NCH - 1:
                        e_st = k.fin(ac.activation(CCf[:, :], psD[1][:, :], AF.Copy,
                                                   scale=sctv[p][:, c:c + 1]), "AC")
                    ev_chain = e_st
                    k.wait(pe, e_st)
            if not (PH & 16):
                continue
            k.wait_all(sync, [ev_chain, ev_psA, ev_psB, ev_psT, k.now("DV"), k.now("AC")])
            spill_ev = [
                k.dma(sync.dma_start(out=ufd[p], in_=UF[:, :]), "DL"),
                k.dma(sync.dma_start(out=ftd[p], in_=FT[:, :]), "DL"),
                k.dma(sync.dma_start(out=ktd[p], in_=KTM[:, :]), "DL"),
                k.dma(sync.dma_start(out=mpd[p], in_=MPRE[:, 0:NCH * 128]), "DL"),
                k.dma(sync.dma_start(out=std[p], in_=ST0b[:, 0:NCH * 128]), "DL"),
                k.dma(sync.dma_start(out=mcd[p], in_=MCf[:, :]), "DL"),
                k.dma(sync.dma_start(out=ccd[p], in_=CCf[:, :]), "DL"),
                k.dma(sync.dma_start(out=qtd[p], in_=qP[:, :]), "DL"),
            ]
        k.wait_all(sync, spill_ev)


def emit_stageH(nc, k, g, ident, DBG):
    sync, gp, pe, dv, ac = nc.sync, nc.gpsimd, nc.tensor, nc.vector, nc.scalar
    with ExitStack() as esH:
        sb = lambda name, shape, dtype=F32: esH.enter_context(nc.sbuf_tensor(name, list(shape), dtype))
        ps = lambda name, shape, dtype=F32: esH.enter_context(nc.psum_tensor(name, list(shape), dtype))
        xh = sb("xhH", (128, 8 * SPAN), BF16)
        xhv = _ftv(xh, SPAN)
        acc = sb("accH", (128, 8 * SPAN), F32)
        accv = _ftv(acc, SPAN)
        hT = sb("hTH", (128, 4 * SPAN), BF16)
        hv = _ftv(hT, SPAN)
        WUP, WGA, WDO = 0, 4096, 8192
        wbf = [sb(f"wbfH{i}", (128, 12288), BF16) for i in range(2)]
        stg = sb("stgH", (128, 4 * 1024), F32)
        cbE = sb("cbEH", (128, SPAN), BF16)
        cmbT = sb("cmbTH", (16, SPAN), BF16)
        routbf = sb("routbfH", (128, 8 * 16), BF16)
        rstg = sb("rstgH", (128, 8 * 16), F32)
        rbias = sb("rbiasH", (128, 16), F32)
        zrow = sb("zrowH", (128, 5 * 16), F32)
        mrow = sb("mrowH", (128, 8), F32)
        cmbb = sb("cmbbH", (128, 16), BF16)
        sgb = [sb(f"sgbH{i}", (128, 512), BF16) for i in range(2)]
        sgc = [sb(f"sgcH{i}", (128, 512), BF16) for i in range(2)]
        sel_e = sb("seleH", (16, 16 * 128), BF16)
        gbf = sb("gbfH", (128, SPAN), BF16)
        psG = [ps(f"psGH{i}", (128, 512)) for i in range(2)]
        psU = [ps(f"psUH{i}", (128, 512)) for i in range(2)]
        psD = [ps(f"psDH{i}", (128, 512)) for i in range(2)]

        routv = _ftv(routbf, 16)
        rstgv = _ftv(rstg, 16)

        xn1_d = g["xn1sv"].rearrange("(a p) n -> a p n", p=128)
        ld = []
        for ft in range(8):
            ld.append(k.dma(sync.dma_start(out=xhv[ft], in_=xn1_d[ft][:, SOFF:SOFF + SPAN]), "DL"))
        rout_d = g["routT"].rearrange("(a p) n -> a p n", p=128)
        for ft in range(8):
            ld.append(k.dma(sync.dma_start(out=rstgv[ft], in_=rout_d[ft]), "DL"))
        ld.append(k.dma(sync.dma_start(out=rbias[:, :], in_=g["rbias"]), "DL"))
        ld.append(k.dma(sync.dma_start(out=sel_e[:, :], in_=g["sele"]), "DL"))
        k.wait_all(ac, ld)
        cv = k.fin(ac.copy(routbf[:, :], rstg[:, :]), "AC")

        esR2 = ExitStack()
        psZ = esR2.enter_context(nc.psum_tensor("psZH", [128, 16], F32))
        psT2 = esR2.enter_context(nc.psum_tensor("psT2H", [16, 128], BF16))
        Z0, I1, Z2, I2, CM = 0, 16, 32, 48, 64
        e_ev = None
        e_tr = None
        k.wait_all(pe, [cv])
        _warm(pe, psG[0][:, :], ident[:, :], xhv[0][:, 0:512], 12)
        for tc in range(16):
            k.wait_all(pe, [e_ev])
            for ft in range(8):
                mm = pe.matmul(psZ[:, :], xhv[ft][:, 128 * tc:128 * tc + 128],
                               routv[ft], start=(ft == 0), stop=(ft == 7))
            e_mm = k.fin(mm, "PE")
            k.wait(dv, e_mm)
            e_ev = k.fin(dv.tensor_tensor(zrow[:, Z0:Z0 + 16], psZ[:, :], rbias[:, :],
                                          ALU.add), "DV")
            k.wait(dv, e_ev)
            e = k.fin(dv.reduce_max(mrow[:, 0:1], zrow[:, Z0:Z0 + 16], axis=AXL.X), "DV")
            k.wait(dv, e)
            e = k.fin(dv.tensor_scalar(zrow[:, I1:I1 + 16], zrow[:, Z0:Z0 + 16],
                                       mrow[:, 0:1], None, ALU.is_ge), "DV")
            k.wait(dv, e)
            e = k.fin(dv.scalar_tensor_tensor(zrow[:, Z2:Z2 + 16], zrow[:, I1:I1 + 16],
                                              -1e30, zrow[:, Z0:Z0 + 16],
                                              ALU.mult, ALU.add), "DV")
            k.wait(dv, e)
            e = k.fin(dv.reduce_max(mrow[:, 1:2], zrow[:, Z2:Z2 + 16], axis=AXL.X), "DV")
            k.wait(dv, e)
            e = k.fin(dv.tensor_scalar(zrow[:, I2:I2 + 16], zrow[:, Z2:Z2 + 16],
                                       mrow[:, 1:2], None, ALU.is_ge), "DV")
            k.wait(dv, e)
            e = k.fin(dv.tensor_tensor(mrow[:, 2:3], mrow[:, 0:1], mrow[:, 1:2],
                                       ALU.subtract), "DV")
            k.wait(dv, e)
            e = k.fin(dv.tensor_scalar_mul(mrow[:, 3:4], mrow[:, 2:3], -1.0), "DV")
            k.wait(ac, e)
            ac.activation(mrow[:, 4:5], mrow[:, 2:3], AF.Sigmoid)
            e_p = k.fin(ac.activation(mrow[:, 5:6], mrow[:, 3:4], AF.Sigmoid), "AC")
            k.wait(dv, e_p)
            e = k.fin(dv.tensor_scalar(zrow[:, CM:CM + 16], zrow[:, I1:I1 + 16],
                                       mrow[:, 4:5], None, ALU.mult), "DV")
            k.wait(dv, e)
            e = k.fin(dv.scalar_tensor_tensor(zrow[:, CM:CM + 16], zrow[:, I2:I2 + 16],
                                              mrow[:, 5:6], zrow[:, CM:CM + 16],
                                              ALU.mult, ALU.add), "DV")
            k.wait(ac, e)
            e = k.fin(ac.copy(cmbb[:, :], zrow[:, CM:CM + 16]), "AC")
            k.wait_all(pe, [e, e_tr])
            e_mt = k.fin(pe.transpose(psT2[:, :], cmbb[:, :], ident[:, :]), "PE")
            k.wait(ac, e_mt)
            e_tr = k.fin(ac.copy(cmbT[0:16, 128 * tc:128 * tc + 128], psT2[0:16, :]), "AC")
        cmb_done = k.now("AC")
        k.barrier([pe, dv, ac])
        esR2.close()
        psC = ps("psCH", (128, 512))

        eup_r = g["eupT"].rearrange("e (a p) n -> (e a) p n", p=128)
        ega_r = g["egaT"].rearrange("e (a p) n -> (e a) p n", p=128)
        edo_r = g["edoT"].rearrange("e (a p) n -> (e a) p n", p=128)
        shu_r = g["shuT"].rearrange("(a p) n -> a p n", p=128)
        shg_r = g["shgT"].rearrange("(a p) n -> a p n", p=128)
        shd_r = g["shdT"].rearrange("(a p) n -> a p n", p=128)

        def wtiles(e):
            out = []
            if e < 16:
                for ft in range(8):
                    out.append((WUP + 512 * ft, 512, eup_r[8 * e + ft]))
                    out.append((WGA + 512 * ft, 512, ega_r[8 * e + ft]))
                for i in range(4):
                    out.append((WDO + 1024 * i, 1024, edo_r[4 * e + i]))
            else:
                hh = e - 16
                for ft in range(8):
                    out.append((WUP + 512 * ft, 512, shu_r[ft][:, 512 * hh:512 * hh + 512]))
                    out.append((WGA + 512 * ft, 512, shg_r[ft][:, 512 * hh:512 * hh + 512]))
                for i in range(4):
                    out.append((WDO + 1024 * i, 1024, shd_r[4 * hh + i]))
            return out

        stg_free = [None] * 4

        def make_loader(e, pe_free):
            tiles = wtiles(e)
            dst = wbf[e % 2]
            state = {"i": 0, "evs": []}

            def step():
                i = state["i"]
                if i >= len(tiles):
                    return
                c0, w, src = tiles[i]
                s = i % 4
                k.wait_all(sync, [stg_free[s]])
                e_d = k.dma(sync.dma_start(out=stg[:, 1024 * s:1024 * s + w], in_=src), "DL")
                k.wait_all(ac, [e_d] + ([pe_free] if i == 0 else []))
                ev = k.fin(ac.copy(dst[:, c0:c0 + w], stg[:, 1024 * s:1024 * s + w]), "AC")
                stg_free[s] = ev
                state["evs"].append(ev)
                state["i"] = i + 1

            return step, state

        step0, st0 = make_loader(0, None)
        for _ in range(20):
            step0()
        pe_done = [None, None]
        ld_state = st0
        sg_ev = [None, None]
        sgfree = [None, None]
        h_ev = [None, None]
        e_cb = None
        psD_ev = [None, None]

        for e in range(18):
            wv = wbf[e % 2]
            wup = [wv[:, WUP + 512 * ft:WUP + 512 * ft + 512] for ft in range(8)]
            wga = [wv[:, WGA + 512 * ft:WGA + 512 * ft + 512] for ft in range(8)]
            wdo = [wv[:, WDO + 1024 * i:WDO + 1024 * i + 1024] for i in range(4)]
            _warm(pe, psG[0][:, :], ident[:, :], xhv[0][:, 0:512], 12)
            ld_evs = list(ld_state["evs"])
            if e < 17:
                nstep, ld_state = make_loader(e + 1, pe_done[(e + 1) % 2])
            else:
                nstep = lambda: None
            if e < 16:
                k.wait_all(pe, [cmb_done, e_cb])
                for hf in range(4):
                    k.wait_all(pe, [e_cb])
                    e_mm = k.fin(pe.matmul(psC[:, :], sel_e[:, 128 * e:128 * e + 128],
                                           cmbT[0:16, 512 * hf:512 * hf + 512],
                                           start=True, stop=True), "PE")
                    k.wait_all(ac, [e_mm])
                    e_cb = k.fin(ac.copy(cbE[:, 512 * hf:512 * hf + 512], psC[:, :]), "AC")
            for hid in range(4):
                for tq in range(4):
                    off = 512 * tq
                    sl = (4 * hid + tq) % 2
                    k.wait_all(pe, ld_evs + [sg_ev[sl], h_ev[sl]])
                    for ft in range(8):
                        mm = pe.matmul(psG[sl][:, :], wga[ft][:, 128 * hid:128 * hid + 128],
                                       xhv[ft][:, off:off + 512],
                                       start=(ft == 0), stop=(ft == 7))
                    e_g = k.fin(mm, "PE")
                    k.wait_all(ac, [e_g, sgfree[sl]])
                    sg_ev[sl] = k.fin(ac.activation(sgb[sl][:, :], psG[sl][:, :],
                                                    AF.Silu), "AC")
                    for ft in range(8):
                        mm = pe.matmul(psU[sl][:, :], wup[ft][:, 128 * hid:128 * hid + 128],
                                       xhv[ft][:, off:off + 512],
                                       start=(ft == 0), stop=(ft == 7))
                    e_u = k.fin(mm, "PE")
                    if e < 16:
                        k.wait_all(dv, [sg_ev[sl], e_cb])
                        e_sc = k.fin(dv.tensor_tensor(sgc[sl][:, :], sgb[sl][:, :],
                                                      cbE[:, off:off + 512], ALU.mult), "DV")
                        sct = sgc[sl]
                    else:
                        k.wait_all(dv, [sg_ev[sl]])
                        e_sc = None
                        sct = sgb[sl]
                    k.wait_all(dv, [e_u, e_sc])
                    h_ev[sl] = k.fin(dv.tensor_tensor(hv[hid][:, off:off + 512],
                                                      psU[sl][:, :], sct[:, :],
                                                      ALU.mult), "DV")
                    sgfree[sl] = h_ev[sl]
                    nstep()
            k.wait_all(pe, [k.now("DV")])
            for tq in range(4):
                off = 512 * tq
                for do2 in range(8):
                    sl = do2 % 2
                    k.wait_all(pe, [psD_ev[sl]])
                    for hid in range(4):
                        mm = pe.matmul(psD[sl][:, :], wdo[hid][:, 128 * do2:128 * do2 + 128],
                                       hv[hid][:, off:off + 512],
                                       start=(hid == 0), stop=(hid == 3))
                    e_d = k.fin(mm, "PE")
                    if e == 0:
                        k.wait(ac, e_d)
                        psD_ev[sl] = k.fin(ac.copy(accv[do2][:, off:off + 512],
                                                   psD[sl][:, :]), "AC")
                    else:
                        k.wait(dv, e_d)
                        psD_ev[sl] = k.fin(dv.tensor_tensor(accv[do2][:, off:off + 512],
                                                            accv[do2][:, off:off + 512],
                                                            psD[sl][:, :], ALU.add), "DV")
                    if tq % 2 == 0:
                        nstep()
            pe_done[e % 2] = k.now("PE")
        g1_d = g["g1sp"].rearrange("(a p) n -> a p n", p=128)
        acc_done = [k.now("DV"), k.now("AC")]
        e_gd = None
        for do2 in range(8):
            k.wait_all(ac, acc_done + [e_gd])
            e_s = k.fin(ac.activation(gbf[:, :], accv[do2][:, :], AF.Silu), "AC")
            k.wait(sync, e_s)
            e_gd = k.dma(sync.dma_start(out=g1_d[do2], in_=gbf[:, :]), "DL")
        k.wait(sync, k.now("DL"))

def build_launch2():
    nc = bass.Bass("TRN2", target_bir_lowering=False)
    g = {}

    def din(name, shape, dtype=F32):
        g[name] = nc.dram_tensor(name, list(shape), dtype, kind="ExternalInput").ap()

    din("rlM", (3 * 128, 1024)); din("rlC", (3 * 128, 1024))
    din("uf2", (NPAIR * 128, NCH * 256), BF16)
    din("ft2", (NPAIR * 128, NCH * 128), BF16)
    din("pt2", (NPAIR * 128, NCH * 128), BF16)
    din("mp2", (NPAIR * 128, NCH * 128), BF16)
    din("st2", (NPAIR * 128, NCH * 128), BF16)
    din("qt2", (NPAIR * 128, NCH * 128), BF16)
    din("x1f", (D, SPAN))
    din("og2", (D, SPAN), BF16)
    din("g12", (D, SPAN), BF16)
    din("owT", (D, D))
    din("outb2", (128, 8)); din("nwcol2", (128, 1))
    din("sel2", (128, 2), BF16); din("sel2f", (2, 128))
    din("ident2", (128, 128), BF16)
    g["ysp"] = nc.dram_tensor("ysp", [D, SPAN], F32, kind="ExternalOutput").ap()

    k = Ctx(nc)
    with k.es, nc.Block():
        emit_launch2(nc, k, g)
    return nc


def emit_launch2(nc, k, g):
    sync, gp, pe, dv, ac = nc.sync, nc.gpsimd, nc.tensor, nc.vector, nc.scalar
    es = k.es
    sb = lambda name, shape, dtype=F32: es.enter_context(nc.sbuf_tensor(name, list(shape), dtype))
    ps = lambda name, shape, dtype=F32: es.enter_context(nc.psum_tensor(name, list(shape), dtype))

    ident = sb("identL2", (128, 128), BF16)
    sel2 = sb("sel2L2", (128, 2), BF16)
    sel2f = sb("sel2fL2", (2, 128), F32)
    nwc = sb("nwcL2", (128, 1), F32)
    outb = sb("outbL2", (128, 8), F32)
    rdF = sb("rdFL2", (128, 8 * SPAN), BF16)
    rdFv = _ftv(rdF, SPAN)
    Sst = sb("SstL2", (128, 8 * 128), BF16)
    Sstv = _ftv(Sst, 128)
    owb = sb("owbL2", (128, 8 * 1024), BF16)
    owbv = _ftv(owb, 1024)

    ld0 = []
    ld0.append(k.dma(sync.dma_start(out=ident[:, :], in_=g["ident2"]), "DL"))
    ld0.append(k.dma(sync.dma_start(out=sel2[:, :], in_=g["sel2"]), "DL"))
    ld0.append(k.dma(sync.dma_start(out=sel2f[:, :], in_=g["sel2f"]), "DL"))
    ld0.append(k.dma(sync.dma_start(out=nwc[:, :], in_=g["nwcol2"]), "DL"))
    ld0.append(k.dma(sync.dma_start(out=outb[:, :], in_=g["outb2"]), "DL"))

    with ExitStack() as esR:
        rlm = esR.enter_context(nc.sbuf_tensor("rlmL2", [128, 3 * 1024], F32))
        rlc = esR.enter_context(nc.sbuf_tensor("rlcL2", [128, 3 * 1024], F32))
        rlmb = esR.enter_context(nc.sbuf_tensor("rlmbL2", [128, 3 * 1024], BF16))
        rlcb = esR.enter_context(nc.sbuf_tensor("rlcbL2", [128, 3 * 1024], BF16))
        psRel = esR.enter_context(nc.psum_tensor("psRelL2", [128, 128], F32))
        rlM_d = g["rlM"].rearrange("(j p) n -> j p n", p=128)
        rlC_d = g["rlC"].rearrange("(j p) n -> j p n", p=128)
        rlmv = _ftv(rlm, 1024)
        rlcv = _ftv(rlc, 1024)
        rlmbv = _ftv(rlmb, 1024)
        rlcbv = _ftv(rlcb, 1024)
        ldr = []
        for j in range(3):
            ldr.append(k.dma(sync.dma_start(out=rlmv[j], in_=rlM_d[j]), "DL"))
            ldr.append(k.dma(sync.dma_start(out=rlcv[j], in_=rlC_d[j]), "DL"))
        k.wait_all(ac, ld0 + ldr)
        ac.copy(rlmb[:, :], rlm[:, :])
        cvr = k.fin(ac.copy(rlcb[:, :], rlc[:, :]), "AC")
        k.wait(dv, cvr)
        e_prev = None
        for p in range(8):
            e_s = k.fin(dv.tensor_copy(Sstv[p], rlcbv[0][:, 128 * p:128 * p + 128]), "DV")
            for j in (1, 2):
                k.wait_all(pe, [e_s, e_prev])
                pe.matmul(psRel[:, :], rlmbv[j][:, 128 * p:128 * p + 128], Sstv[p],
                          start=True, stop=False)
                e_mm = k.fin(pe.matmul(psRel[:, :], ident[:, :],
                                       rlcbv[j][:, 128 * p:128 * p + 128],
                                       start=False, stop=True), "PE")
                k.wait(ac, e_mm)
                e_s = e_prev = k.fin(ac.copy(Sstv[p], psRel[:, :]), "AC")
        relay_done = [k.now("AC"), k.now("DV")]
    k.barrier([sync, gp, pe, dv, ac])

    psB2 = es.enter_context(nc.psum_tensor("psB2L2", [128, 512], F32))
    ow_d = g["owT"].rearrange("(a p) n -> a p n", p=128)
    with ExitStack() as esM:
        sbM = lambda name, shape, dtype=F32: esM.enter_context(nc.sbuf_tensor(name, list(shape), dtype))
        psM = lambda name, shape, dtype=F32: esM.enter_context(nc.psum_tensor(name, list(shape), dtype))
        owstg = sbM("owstgL2", (128, 2 * 1024), F32)
        stg_ev = [None, None]
        for a in range(8):
            s = a % 2
            k.wait_all(sync, [stg_ev[s]])
            e_d = k.dma(sync.dma_start(out=owstg[:, 1024 * s:1024 * s + 1024], in_=ow_d[a]), "DL")
            k.wait(ac, e_d)
            stg_ev[s] = k.fin(ac.copy(owbv[a], owstg[:, 1024 * s:1024 * s + 1024]), "AC")
        ow_done = k.now("AC")

        uf_d = g["uf2"].rearrange("(a p) (c u) -> a p c u", p=128, u=256)
        ft_d = g["ft2"].rearrange("(a p) n -> a p n", p=128)
        pt_d = g["pt2"].rearrange("(a p) n -> a p n", p=128)
        mp_d = g["mp2"].rearrange("(a p) n -> a p n", p=128)
        st_d = g["st2"].rearrange("(a p) n -> a p n", p=128)
        qt_d = g["qt2"].rearrange("(a p) n -> a p n", p=128)
        NC8 = NCH * 128
        spl = [sbM(f"splL2{i}", (128, 6 * NC8), BF16) for i in range(2)]
        UFo, FTo, PTo, MPo, STo, QTo = (i * NC8 for i in range(6))
        TST = sbM("TSTL2", (128, NC8), BF16)
        WmB = sbM("WmBL2", (128, NC8), BF16)
        rdT = sbM("rdTL2", (128, NC8), BF16)
        sqt = sbM("sqtL2", (128, SPAN), BF16)
        rs = sbM("rsL2", (2, SPAN), F32)
        rs2 = sbM("rs2L2", (2, SPAN), F32)
        rsbc = sbM("rsbcL2", (128, SPAN), BF16)
        psA = psM("psAL2", (128, 512))
        psB = psM("psBL2", (128, 512))
        psR = psM("psRL2", (128, 512))
        psT = psM("psTL2", (128, 512), BF16)
        psS = psM("psSL2", (2, 512))

        def load_pair(p, s, free_ev):
            evs = []
            dst = spl[s]
            k.wait_all(sync, [free_ev])
            evs.append(k.dma(sync.dma_start(
                out=dst[:, UFo:UFo + NC8].rearrange("p (c u) -> p c u", u=128),
                in_=uf_d[p][:, :, 0:128]), "DL"))
            for off, src in [(FTo, ft_d), (PTo, pt_d), (MPo, mp_d),
                             (STo, st_d), (QTo, qt_d)]:
                evs.append(k.dma(sync.dma_start(out=dst[:, off:off + NC8], in_=src[p]), "DL"))
            return evs

        pair_free = [None, None]
        rms_pe = None
        prev_nm = None
        ld_ev = load_pair(0, 0, None)
        k.wait_all(pe, relay_done + [ow_done])
        for p in range(8):
            s = p % 2
            w = spl[s]
            if p < 7:
                nxt_ev = load_pair(p + 1, 1 - s, pair_free[1 - s])
            eA = eB = eR = eT = None
            k.wait_all(pe, ld_ev)
            _warm(pe, psA[:, :], ident[:, :], rdF[:, 0:512], 12)
            for cg in range(8):
                k.wait_all(pe, [eA])
                for j in range(4):
                    c = 4 * cg + j
                    pe.matmul(psA[:, 128 * j:128 * j + 128],
                              w[:, MPo + 128 * c:MPo + 128 * c + 128], Sstv[p],
                              start=True, stop=False)
                    mm = pe.matmul(psA[:, 128 * j:128 * j + 128], ident[:, :],
                                   w[:, STo + 128 * c:STo + 128 * c + 128],
                                   start=False, stop=True)
                e = k.fin(mm, "PE")
                k.wait(ac, e)
                eA = k.fin(ac.copy(TST[:, 512 * cg:512 * cg + 512], psA[:, :]), "AC")
                k.wait_all(pe, [eB, eA])
                for j in range(4):
                    c = 4 * cg + j
                    pe.matmul(psB[:, 128 * j:128 * j + 128], ident[:, :],
                              w[:, UFo + 128 * c:UFo + 128 * c + 128],
                              start=True, stop=False)
                    mm = pe.matmul(psB[:, 128 * j:128 * j + 128],
                                   w[:, FTo + 128 * c:FTo + 128 * c + 128],
                                   TST[:, 128 * c:128 * c + 128],
                                   start=False, stop=True)
                e = k.fin(mm, "PE")
                k.wait(ac, e)
                eB = k.fin(ac.copy(WmB[:, 512 * cg:512 * cg + 512], psB[:, :]), "AC")
                k.wait_all(pe, [eR, eB])
                for j in range(4):
                    c = 4 * cg + j
                    pe.matmul(psR[:, 128 * j:128 * j + 128],
                              w[:, PTo + 128 * c:PTo + 128 * c + 128],
                              WmB[:, 128 * c:128 * c + 128],
                              start=True, stop=False)
                    mm = pe.matmul(psR[:, 128 * j:128 * j + 128],
                                   w[:, QTo + 128 * c:QTo + 128 * c + 128],
                                   TST[:, 128 * c:128 * c + 128],
                                   start=False, stop=True)
                e = k.fin(mm, "PE")
                k.wait(ac, e)
                eR = k.fin(ac.copy(rdT[:, 512 * cg:512 * cg + 512], psR[:, :]), "AC")
                k.wait_all(pe, [eT, eR])
                for j in range(4):
                    c = 4 * cg + j
                    mm = pe.transpose(psT[:, 128 * j:128 * j + 128],
                                      rdT[:, 128 * c:128 * c + 128], ident[:, :])
                e = k.fin(mm, "PE")
                k.wait(dv, e)
                pv = psT[:, :].rearrange("p (c u) -> p c u", u=128)
                rv = rdFv[p].rearrange("p (g c u) -> p g c u", g=8, u=64)
                dv.tensor_copy(rv[0:64, cg, :, :], pv[0:64, :, 0:64])
                eT = k.fin(dv.tensor_copy(rv[64:128, cg, :, :], pv[64:128, :, 64:128]), "DV")
            pair_free[s] = k.now("PE")
            k.wait_all(ac, [eT, rms_pe])
            e_sq = k.fin(ac.activation(sqt[:, :], rdFv[p], AF.Square), "AC")
            eS = None
            for tq in range(4):
                off = 512 * tq
                k.wait_all(pe, [e_sq, eS])
                e_mm = k.fin(pe.matmul(psS[0:2, :], sel2[:, :], sqt[:, off:off + 512],
                                       start=True, stop=True), "PE")
                k.wait(dv, e_mm)
                eS = k.fin(dv.tensor_scalar(rs[0:2, off:off + 512], psS[0:2, :],
                                            1.0 / 64.0, 1e-6, ALU.mult, ALU.add), "DV")
            k.wait(ac, eS)
            eS = k.fin(ac.activation(rs2[0:2, :], rs[0:2, :], AF.Sqrt), "AC")
            k.wait(dv, eS)
            eS = k.fin(dv.reciprocal(rs[0:2, :], rs2[0:2, :]), "DV")
            eB2 = None
            for tq in range(4):
                off = 512 * tq
                k.wait_all(pe, [eS, eB2])
                e_mm = k.fin(pe.matmul(psB2[:, :], sel2f[:, :], rs[0:2, off:off + 512],
                                       start=True, stop=True), "PE")
                k.wait_all(ac, [e_mm, prev_nm])
                eB2 = k.fin(ac.activation(rsbc[:, off:off + 512], psB2[:, :],
                                          AF.Copy, scale=nwc[:, 0:1]), "AC")
            rms_pe = k.now("PE")
            k.wait_all(dv, [eB2])
            prev_nm = e_nm = k.fin(dv.tensor_tensor(rdFv[p], rdFv[p], rsbc[:, :],
                                                    ALU.mult), "DV")
            k.wait(pe, e_nm)
            ld_ev = nxt_ev if p < 7 else []
        scan_done = [k.now("DV"), k.now("AC")]
    k.barrier([sync, gp, pe, dv, ac])

    with ExitStack() as esO:
        sbO = lambda name, shape, dtype=F32: esO.enter_context(nc.sbuf_tensor(name, list(shape), dtype))
        x1_d = g["x1f"].rearrange("(a p) n -> a p n", p=128)
        og_d = g["og2"].rearrange("(a p) n -> a p n", p=128)
        g1_d = g["g12"].rearrange("(a p) n -> a p n", p=128)
        y_d = g["ysp"].rearrange("(a p) n -> a p n", p=128)
        x1t = [sbO(f"x1tL2{i}", (128, SPAN), F32) for i in range(2)]
        ogt = [sbO(f"ogtL2{i}", (128, SPAN), BF16) for i in range(2)]
        g1t = [sbO(f"g1tL2{i}", (128, SPAN), BF16) for i in range(2)]
        prj = sbO("prjL2", (128, 512), F32)
        yb = sbO("ybL2", (128, SPAN), F32)
        dc_free = [None, None]
        ldx = {}

        def load_dc(dc, free_evs):
            s = dc % 2
            evs = []
            k.wait_all(sync, free_evs if free_evs else [])
            evs.append(k.dma(sync.dma_start(out=x1t[s][:, :], in_=x1_d[dc]), "DL"))
            evs.append(k.dma(sync.dma_start(out=ogt[s][:, :], in_=og_d[dc]), "DL"))
            evs.append(k.dma(sync.dma_start(out=g1t[s][:, :], in_=g1_d[dc]), "DL"))
            return evs

        ldx[0] = load_dc(0, None)
        e_pr = None
        e_y = None
        y_dma = None
        for dc in range(8):
            s = dc % 2
            if dc < 7:
                ldx[dc + 1] = load_dc(dc + 1, [dc_free[1 - s]])
            k.wait_all(pe, scan_done)
            if dc == 0:
                _warm(pe, psB2[:, :], ident[:, :], rdFv[0][:, 0:512], 12)
            for tq in range(4):
                off = 512 * tq
                k.wait_all(pe, [e_pr] + ldx[dc])
                for p in range(8):
                    mm = pe.matmul(psB2[:, :], owbv[p][:, 128 * dc:128 * dc + 128],
                                   rdFv[p][:, off:off + 512],
                                   start=(p == 0), stop=(p == 7))
                e_mm = k.fin(mm, "PE")
                k.wait_all(dv, [e_mm, y_dma])
                e_pr = k.fin(dv.scalar_tensor_tensor(prj[:, :], psB2[:, :],
                                                     outb[:, dc:dc + 1],
                                                     ogt[s][:, off:off + 512],
                                                     ALU.add, ALU.mult), "DV")
                dv.tensor_tensor(prj[:, :], prj[:, :], g1t[s][:, off:off + 512], ALU.mult)
                e_y = k.fin(dv.tensor_tensor(yb[:, off:off + 512], prj[:, :],
                                             x1t[s][:, off:off + 512], ALU.add), "DV")
            dc_free[s] = e_y
            k.wait(sync, e_y)
            y_dma = k.dma(sync.dma_start(out=y_d[dc], in_=yb[:, :]), "DL")
        k.wait(sync, k.now("DL"))


def host_inputs2(inputs, res1):
    ident = np.eye(128, dtype=BF)
    identf = np.eye(128, dtype=np.float32)
    sel2 = np.zeros((128, 2), dtype=BF)
    sel2[0:64, 0] = 1
    sel2[64:128, 1] = 1
    sel2f = np.zeros((2, 128), np.float32)
    sel2f[0, 0:64] = 1
    sel2f[1, 64:128] = 1
    nw = np.asarray(inputs["L1_attn_norm_w"]).astype(np.float32)
    nwcol = np.tile(nw, 2).reshape(128, 1)
    outb = np.ascontiguousarray(
        np.asarray(inputs["L1_attn_out_b"]).astype(np.float32).reshape(8, 128).T)
    owT = np.ascontiguousarray(np.asarray(inputs["L1_attn_out_w"], np.float32).T)
    in_maps = []
    for c in range(NCORES):
        b, q = divmod(c, 4)
        rlM = np.zeros((3, 128, 1024), np.float32)
        rlC = np.zeros((3, 128, 1024), np.float32)
        for j in range(3):
            if j < q:
                pred = 4 * b + j
                mc = np.asarray(res1.results[pred]["mcsp"], np.float32)
                cc = np.asarray(res1.results[pred]["ccsp"], np.float32)
                rlM[j] = mc.transpose(1, 0, 2).reshape(128, 1024)
                rlC[j] = cc.transpose(1, 0, 2).reshape(128, 1024)
            else:
                rlM[j] = np.tile(identf, (1, 8))
        r = res1.results[c]
        m = {
            "rlM": rlM.reshape(384, 1024), "rlC": rlC.reshape(384, 1024),
            "uf2": np.asarray(r["ufsp"]).reshape(NPAIR * 128, NCH * 256),
            "ft2": np.asarray(r["ftsp"]).reshape(NPAIR * 128, NCH * 128),
            "pt2": np.asarray(r["ptsp"]).reshape(NPAIR * 128, NCH * 128),
            "mp2": np.asarray(r["mpsp"]).reshape(NPAIR * 128, NCH * 128),
            "st2": np.asarray(r["stsp"]).reshape(NPAIR * 128, NCH * 128),
            "qt2": np.asarray(r["qtsp"]).reshape(NPAIR * 128, NCH * 128),
            "x1f": np.ascontiguousarray(np.asarray(r["x1sp"], np.float32)[:, 128:128 + SPAN]),
            "og2": np.asarray(r["ogsp"]),
            "g12": np.asarray(r["g1sp"]),
            "owT": owT, "outb2": outb, "nwcol2": nwcol,
            "sel2": sel2, "sel2f": sel2f, "ident2": ident,
        }
        in_maps.append(m)
    return in_maps


def kernel(**inputs):
    in_maps = host_inputs(inputs)
    nc1 = build_launch1(stages="ABCDEFGH")
    res1 = run_bass_kernel_spmd(nc1, in_maps, core_ids=list(range(NCORES)))
    in_maps2 = host_inputs2(inputs, res1)
    nc2 = build_launch2()
    res2 = run_bass_kernel_spmd(nc2, in_maps2, core_ids=list(range(NCORES)))
    out = np.empty((B, T, D), np.float32)
    for c in range(NCORES):
        b, q = divmod(c, 4)
        out[b, SPAN * q:SPAN * (q + 1), :] = np.asarray(res2.results[c]["ysp"], np.float32).T
    return out


# version 06
# speedup vs baseline: 1.0586x; 1.0586x over previous
import math
from contextlib import ExitStack

import numpy as np
import ml_dtypes
import concourse.bass as bass
import concourse.mybir as mybir
from concourse.bass_utils import run_bass_kernel_spmd

F32 = mybir.dt.float32
BF16 = mybir.dt.bfloat16
I32 = mybir.dt.int32
AF = mybir.ActivationFunctionType
ALU = mybir.AluOpType
AXL = mybir.AxisListType
BF = ml_dtypes.bfloat16

B, T, D = 2, 8192, 1024
NCORES = 8
SPAN = 2048
HALO = 384
XL = SPAN + HALO
L0S = 256
L0N = XL - L0S
NQT = L0N // 128
H0, HD0 = 8, 64
SCALE0 = 1.0 / 8.0
FFNH = 3072
L1S = 320
L1N = XL - L1S
SOFF = HALO - L1S
H1, HD1 = 16, 64
CH = 64
NCH = SPAN // CH
NPAIR = H1 // 2
NEXP, EDIM = 16, 512
SHH = 1024


def _chunks(n, step=512):
    return [(i, min(step, n - i)) for i in range(0, n, step)]


class Ctx:

    def __init__(self, nc):
        self.nc = nc
        self.es = ExitStack()
        self.cnt = {}
        self.sem = {}
        for name in ["DL", "DG", "PE", "DV", "AC", "PO"]:
            self.sem[name] = self.es.enter_context(nc.semaphore(name))
            self.cnt[name] = 0

    def dma(self, inst, s):
        self.cnt[s] += 16
        inst.then_inc(self.sem[s], 16)
        return (s, self.cnt[s])

    def fin(self, inst, s):
        self.cnt[s] += 1
        inst.then_inc(self.sem[s], 1)
        return (s, self.cnt[s])

    def wait(self, eng, ev):
        if ev is not None:
            eng.wait_ge(self.sem[ev[0]], ev[1])

    def wait_all(self, eng, evs):
        seen = {}
        for ev in evs:
            if ev is None:
                continue
            s, v = ev
            seen[s] = max(seen.get(s, 0), v)
        for s, v in seen.items():
            eng.wait_ge(self.sem[s], v)

    def now(self, s):
        return (s, self.cnt[s])

    def barrier(self, engines):
        evs = [self.now(s) for s in ["PE", "DV", "AC", "DL", "DG", "PO"]]
        for eng in engines:
            self.wait_all(eng, evs)


def mt(view):
    return view


def _rope_tables(c):
    start = SPAN * (c % 4)
    pos = np.arange(start - HALO, start - HALO + XL, dtype=np.float64)
    j = np.arange(32, dtype=np.float64)
    inv = 10000.0 ** (-2.0 * j / 64.0)
    ang = pos[None, :] * inv[:, None]
    cos = np.cos(ang).astype(np.float32)
    sin = np.sin(ang).astype(np.float32)
    cc = np.empty((128, XL), np.float32)
    ss = np.empty((128, XL), np.float32)
    for hh in range(2):
        r = 64 * hh
        cc[r:r + 32] = cos
        cc[r + 32:r + 64] = cos
        ss[r:r + 32] = -sin
        ss[r + 32:r + 64] = sin
    return cc, ss


def host_inputs(inputs):
    x = np.asarray(inputs["L0_attn_x"])
    i = np.arange(128)
    sl = (i[:, None] > i[None, :]).astype(BF)
    ui = (i[:, None] <= i[None, :]).astype(BF)
    ident = np.eye(128, dtype=BF)
    onesrow = np.ones((1, 128), dtype=BF)
    onescol = np.ones((128, 1), dtype=BF)
    selh = np.zeros((16, 8 * 128), dtype=BF)
    for ft in range(8):
        selh[2 * ft, ft * 128 + 0:ft * 128 + 64] = 1
        selh[2 * ft + 1, ft * 128 + 64:ft * 128 + 128] = 1
    sele = np.zeros((16, 16 * 128), dtype=BF)
    for e in range(16):
        sele[e, 128 * e:128 * (e + 1)] = 1
    j64 = np.arange(64)
    su64 = (j64[:, None] < j64[None, :]).astype(np.float32)
    sl64 = (j64[:, None] > j64[None, :]).astype(np.float32)
    ui64 = (j64[:, None] <= j64[None, :]).astype(np.float32)
    bd = np.zeros((128, 128), np.float32)
    bd[0:64, 0:64] = 1
    bd[64:128, 64:128] = 1

    def bdm(m64):
        out = np.zeros((128, 128), np.float32)
        out[0:64, 0:64] = m64
        out[64:128, 64:128] = m64
        return out

    selh2 = np.zeros((128, 8 * 16), dtype=BF)
    for ft in range(8):
        selh2[0:64, 16 * ft + 2 * ft] = 1
        selh2[64:128, 16 * ft + 2 * ft + 1] = 1
    selhf_ = np.zeros((16, 8 * 128), dtype=np.float32)
    for ft in range(8):
        selhf_[2 * ft, ft * 128 + 0:ft * 128 + 64] = 1
        selhf_[2 * ft + 1, ft * 128 + 64:ft * 128 + 128] = 1
    ctrl_w_ = np.asarray(inputs["L1_attn_ctrl_w"], np.float32)
    ctrl_b_ = np.asarray(inputs["L1_attn_ctrl_b"], np.float32)
    ctrlP = np.zeros((1152, 1024), np.float32)
    ctrlP[0:16] = ctrl_w_[0:16]
    ctrlP[32:48] = ctrl_w_[16:32]
    ctrlP[64:1088] = ctrl_w_[32:1056]
    ctrlTP = np.ascontiguousarray(ctrlP.T)
    cb_ = np.zeros(1152, np.float32)
    cb_[0:16] = ctrl_b_[0:16]
    cb_[32:48] = ctrl_b_[16:32]
    cb_[64:1088] = ctrl_b_[32:1056]
    ctrlb2v = np.ascontiguousarray(cb_.reshape(9, 128).T)
    mslbdn_ = (-bdm(sl64)).astype(BF)
    msubdn_ = (-bdm(su64)).astype(BF)
    muibd_ = bdm(ui64).astype(BF)
    nw = np.asarray(inputs["L1_attn_norm_w"]).astype(np.float32)
    nwcol = np.tile(nw, 2).reshape(128, 1)
    outb = np.asarray(inputs["L1_attn_out_b"]).astype(np.float32).reshape(1024, 1)
    ctrlb = np.asarray(inputs["L1_attn_ctrl_b"]).astype(np.float32).reshape(1056, 1)
    rbias_row = np.broadcast_to(
        np.asarray(inputs["L1_ffn_router_bias"]).astype(np.float32), (128, 16)).copy()

    common = {
        "Wqkv": np.asarray(inputs["L0_attn_Wqkv"], np.float32),
        "Wout": np.asarray(inputs["L0_attn_Wout"], np.float32),
        "wgT": np.ascontiguousarray(np.asarray(inputs["L0_ffn_w_gate"]).T).astype(np.int32),
        "wuT": np.ascontiguousarray(np.asarray(inputs["L0_ffn_w_up"]).T).astype(np.int32),
        "wdT": np.ascontiguousarray(np.asarray(inputs["L0_ffn_w_down"]).T).astype(np.int32),
        "qkvT": np.ascontiguousarray(np.asarray(inputs["L1_attn_qkv_w"], np.float32).T),
        "ctrlT": ctrlTP,
        "convq": np.asarray(inputs["L1_attn_convq_w"], np.float32),
        "convk": np.asarray(inputs["L1_attn_convk_w"], np.float32),
        "convv": np.asarray(inputs["L1_attn_convv_w"], np.float32),
        "routT": np.ascontiguousarray(np.asarray(inputs["L1_ffn_router_w"], np.float32).T),
        "rbias": rbias_row,
        "eupT": np.asarray(inputs["L1_ffn_up_proj"], np.float32),
        "egaT": np.asarray(inputs["L1_ffn_gate_proj"], np.float32),
        "edoT": np.asarray(inputs["L1_ffn_down_proj"], np.float32),
        "shgT": np.ascontiguousarray(np.asarray(inputs["L1_ffn_sh_gate"], np.float32).T),
        "shuT": np.ascontiguousarray(np.asarray(inputs["L1_ffn_sh_up"], np.float32).T),
        "shdT": np.ascontiguousarray(np.asarray(inputs["L1_ffn_sh_down"], np.float32).T),
        "ctrlb": ctrlb,
        "maskSL": sl, "maskUI": ui, "ident": ident,
        "onesrow": onesrow, "onescol": onescol,
        "selh": selh, "sele": sele,
        "mslbdn": mslbdn_, "msubdn": msubdn_,
        "muibd": muibd_, "mbdn": (-bd).astype(BF),
        "mbd": bd.astype(BF),
        "nwcol": nwcol, "outb": outb,
        "selh2": selh2, "selhf": selhf_, "ctrlb2": ctrlb2v,
        "msl4": np.tile(mslbdn_, (1, 4)), "msu4": np.tile(msubdn_, (1, 4)),
        "mui4": np.tile(muibd_, (1, 4)), "mbd4": np.tile(bd.astype(BF), (1, 4)),
        "mbdn4": np.tile((-bd).astype(BF), (1, 4)),
        "mbd8": np.tile(bd.astype(BF), (1, 8)),
    }
    in_maps = []
    for c in range(NCORES):
        b, q = divmod(c, 4)
        start = SPAN * q
        lo = start - HALO
        xs = np.zeros((XL, D), np.float32)
        src_lo = max(lo, 0)
        xs[src_lo - lo:, :] = x[b, src_lo:start + SPAN, :]
        cc, ss = _rope_tables(c)
        validc = np.ones((128, 19), np.float32)
        mask3 = np.ones((128, 3), np.float32)
        if q == 0:
            validc[:, 0:3] = 0.0
            mask3[:] = 0.0
        m = dict(common)
        m.update({"xT": np.ascontiguousarray(xs.T), "ropeC": cc, "ropeS": ss,
                  "validc": validc, "mask3": mask3})
        in_maps.append(m)
    return in_maps


def build_launch1(dbg=(), stages="ABCDE", glim=(NPAIR, NCH // 4)):
    nc = bass.Bass("TRN2", target_bir_lowering=False)
    g = {}

    def din(name, shape, dtype=F32):
        g[name] = nc.dram_tensor(name, list(shape), dtype, kind="ExternalInput").ap()

    def dout(name, shape, dtype=F32):
        g[name] = nc.dram_tensor(name, list(shape), dtype, kind="ExternalOutput").ap()

    din("xT", (D, XL))
    din("Wqkv", (D, 1536)); din("Wout", (512, D))
    din("wgT", (D, FFNH), I32); din("wuT", (D, FFNH), I32); din("wdT", (FFNH, D), I32)
    din("qkvT", (D, 3 * D)); din("ctrlT", (D, 1152)); din("ctrlb", (1056, 1))
    din("convq", (D, 4)); din("convk", (D, 4)); din("convv", (D, 4))
    din("routT", (D, 16)); din("rbias", (128, 16))
    din("eupT", (NEXP, D, EDIM)); din("egaT", (NEXP, D, EDIM)); din("edoT", (NEXP, EDIM, D))
    din("shgT", (D, SHH)); din("shuT", (D, SHH)); din("shdT", (SHH, D))
    din("ropeC", (128, XL)); din("ropeS", (128, XL))
    din("maskSL", (128, 128), BF16); din("maskUI", (128, 128), BF16)
    din("ident", (128, 128), BF16)
    din("onesrow", (1, 128), BF16); din("onescol", (128, 1), BF16)
    din("selh", (16, 8 * 128), BF16); din("sele", (16, 16 * 128), BF16)
    din("mslbdn", (128, 128), BF16); din("msubdn", (128, 128), BF16)
    din("muibd", (128, 128), BF16); din("mbdn", (128, 128), BF16)
    din("mbd", (128, 128), BF16)
    din("validc", (128, 19)); din("mask3", (128, 3))
    din("nwcol", (128, 1)); din("outb", (D, 1))
    din("selh2", (128, 8 * 16), BF16); din("selhf", (16, 8 * 128))
    din("ctrlb2", (128, 9))
    din("msl4", (128, 512), BF16); din("msu4", (128, 512), BF16)
    din("mui4", (128, 512), BF16); din("mbd4", (128, 512), BF16)
    din("mbdn4", (128, 512), BF16); din("mbd8", (128, 1024), BF16)
    g["xn1sv"] = nc.dram_tensor("xn1sv", [D, L1N], BF16, kind="Internal").ap()
    g["kbsv"] = nc.dram_tensor("kbsv", [D, SPAN], BF16, kind="Internal").ap()

    dout("x1sp", (D, L0N))
    dout("g1sp", (D, SPAN), BF16)
    dout("qtsp", (NPAIR, 128, NCH * 128), BF16)
    dout("ogsp", (D, SPAN), BF16)
    dout("ufsp", (NPAIR, 128, NCH * 256), BF16)
    dout("ftsp", (NPAIR, 128, NCH * 128), BF16)
    dout("ktsp", (NPAIR, 128, NCH * 128), BF16)
    dout("ptsp", (NPAIR, 128, NCH * 128), BF16)
    dout("mpsp", (NPAIR, 128, NCH * 128), BF16)
    dout("stsp", (NPAIR, 128, NCH * 128), BF16)
    dout("mcsp", (NPAIR, 128, 128))
    dout("ccsp", (NPAIR, 128, 128))
    for name, shape, dt in dbg:
        dout(name, shape, dt)

    k = Ctx(nc)
    with k.es, nc.Block():
        emit_launch1(nc, k, g, dbg, stages, glim)
    return nc


def _ftv(t, n):
    v = t.ap().rearrange("p (a n) -> p a n", n=n)
    return [v[:, a2, :] for a2 in range(v.shape[1])]


def emit_launch1(nc, k, g, dbg=(), stages="ABCDE", glim=(NPAIR, NCH // 4)):
    sync, gp, pe, dv, ac = nc.sync, nc.gpsimd, nc.tensor, nc.vector, nc.scalar
    es = k.es
    sb = lambda name, shape, dtype=F32: es.enter_context(nc.sbuf_tensor(name, list(shape), dtype))
    psa = lambda name, shape, dtype=F32: es.enter_context(nc.psum_tensor(name, list(shape), dtype))
    DBG = {name for name, _, _ in dbg}

    ident = sb("identb", (128, 128), BF16)
    onesr = sb("onesrb", (1, 128), BF16)
    onesc = sb("onescb", (128, 1), BF16)
    onesrf = sb("onesrf", (1, 128), F32)
    onescf = sb("onescf", (128, 1), F32)
    mSL = sb("mSLb", (128, 128), BF16)
    mUI = sb("mUIb", (128, 128), BF16)
    validc = sb("validcb", (128, 19), F32)
    const_ev = []
    for t, name in [(ident, "ident"), (onesr, "onesrow"), (onesc, "onescol"),
                    (mSL, "maskSL"), (mUI, "maskUI"), (validc, "validc")]:
        const_ev.append(k.dma(sync.dma_start(out=t[:, :], in_=g[name]), "DL"))
    dv.memset(onesrf[:, :], 1.0)
    ev = k.fin(dv.memset(onescf[:, :], 1.0), "DV")
    const_ev.append(ev)

    es_xn = k.es.enter_context(ExitStack())
    xn = es_xn.enter_context(nc.sbuf_tensor("xnb", [128, 8 * XL], BF16, side="right"))
    xnv = _ftv(xn, XL)

    xTv = g["xT"].rearrange("(a p) n -> a p n", p=128)

    with ExitStack() as esA:
        sbA = lambda name, shape, dtype=F32: esA.enter_context(nc.sbuf_tensor(name, list(shape), dtype))
        psA = lambda name, shape, dtype=F32: esA.enter_context(nc.psum_tensor(name, list(shape), dtype))
        X = sbA("X", (128, 8 * XL), F32)
        Xv = _ftv(X, XL)
        x_ev = []
        for ft in range(8):
            x_ev.append(k.dma(sync.dma_start(out=Xv[ft], in_=xTv[ft]), "DL"))

        wrkQ = [sbA(f"wrkQ{i}", (128, 512), F32) for i in range(2)]
        pS = psA("pS", (1, 512))
        pQ = psA("pQ", (1, 512))
        srow = sbA("srow", (1, XL), F32)
        qrow = sbA("qrow", (1, XL), F32)
        q_ev = [None, None]
        ch = _chunks(XL)
        for ci, (c0, cn) in enumerate(ch):
            stat_done = []
            for ft in range(8):
                slot = ft % 2
                k.wait(ac, x_ev[ft])
                k.wait(ac, q_ev[slot])
                sq = k.fin(ac.activation(wrkQ[slot][:, 0:cn], Xv[ft][:, c0:c0 + cn],
                                         AF.Square), "AC")
                k.wait_all(pe, [sq] + ([const_ev[-1]] if ci == 0 and ft == 0 else []))
                pe.matmul(pS[0:1, 0:cn], onescf[:, :], Xv[ft][:, c0:c0 + cn],
                          start=(ft == 0), stop=(ft == 7))
                mm = pe.matmul(pQ[0:1, 0:cn], onescf[:, :], wrkQ[slot][:, 0:cn],
                               start=(ft == 0), stop=(ft == 7))
                q_ev[slot] = k.fin(mm, "PE")
            k.wait(dv, q_ev[(7) % 2])
            dv.tensor_copy(srow[0:1, c0:c0 + cn], pS[0:1, 0:cn])
            e = k.fin(dv.tensor_copy(qrow[0:1, c0:c0 + cn], pQ[0:1, 0:cn]), "DV")
            k.wait(pe, e)

        arow = sbA("arow", (1, XL), F32)
        brow = sbA("brow", (1, XL), F32)
        trow = sbA("trow", (1, XL), F32)
        dv.tensor_scalar_mul(srow[:, :], srow[:, :], 1.0 / D)
        dv.tensor_tensor(trow[:, :], srow[:, :], srow[:, :], ALU.mult)
        dv.scalar_tensor_tensor(trow[:, :], qrow[:, :], 1.0 / D, trow[:, :],
                                ALU.mult, ALU.subtract)
        ev = k.fin(dv.tensor_scalar_add(trow[:, :], trow[:, :], 1e-5), "DV")
        k.wait(ac, ev)
        ev = k.fin(ac.activation(trow[:, :], trow[:, :], AF.Sqrt), "AC")
        k.wait(dv, ev)
        dv.reciprocal(arow[:, :], trow[:, :])
        dv.tensor_tensor(brow[:, :], srow[:, :], arow[:, :], ALU.mult)
        ev = k.fin(dv.tensor_scalar_mul(brow[:, :], brow[:, :], -1.0), "DV")

        Abb = sbA("Abb", (128, XL), F32)
        Bbb = sbA("Bbb", (128, XL), F32)
        pbcA = psA("pbcA", (128, 512))
        pbcB = psA("pbcB", (128, 512))
        k.wait(pe, ev)
        bc_ev = []
        for (c0, cn) in _chunks(XL):
            e1 = k.fin(pe.matmul(pbcA[:, 0:cn], onesrf[0:1, :], arow[0:1, c0:c0 + cn],
                                 start=True, stop=True), "PE")
            e2 = k.fin(pe.matmul(pbcB[:, 0:cn], onesrf[0:1, :], brow[0:1, c0:c0 + cn],
                                 start=True, stop=True), "PE")
            k.wait(ac, e2)
            ac.copy(Abb[:, c0:c0 + cn], pbcA[:, 0:cn])
            e3 = k.fin(ac.copy(Bbb[:, c0:c0 + cn], pbcB[:, 0:cn]), "AC")
            bc_ev.append(e3)
            k.wait(pe, e3)
        wrk = sbA("wrkLN", (128, XL), F32)
        k.wait_all(dv, bc_ev)
        for ft in range(8):
            dv.tensor_tensor(wrk[:, :], Xv[ft], Abb[:, :], ALU.mult)
            k.fin(dv.tensor_tensor(xnv[ft], wrk[:, :], Bbb[:, :], ALU.add), "DV")
        xn_done = k.now("DV")
        if "xn" in DBG:
            k.wait(sync, xn_done)
            for ft in range(8):
                k.dma(sync.dma_start(out=g["xn"].rearrange("(a p) n -> a p n", p=128)[ft],
                                     in_=xnv[ft]), "DL")
        st_ev = k.now("DL")
    k.barrier([sync, gp, pe, dv, ac])

    if "B" not in stages:
        k.wait(sync, k.now("DL"))
        k.wait(gp, k.now("DG"))
        return
    es_de = k.es.enter_context(ExitStack())
    s_sb = es_de.enter_context(nc.sbuf_tensor("s_sb", [128, 8 * L0N], BF16))
    sv = _ftv(s_sb, L0N)
    es_bc = k.es.enter_context(ExitStack())
    qk = es_bc.enter_context(nc.sbuf_tensor("qk", [128, 8 * XL], BF16))
    qkv_ = _ftv(qk, XL)
    v_tm = es_bc.enter_context(nc.sbuf_tensor("v_tm", [128, 19 * 520], BF16))
    vtv = _ftv(v_tm, 520)
    with ExitStack() as esB:
        sbB = lambda name, shape, dtype=F32: esB.enter_context(nc.sbuf_tensor(name, list(shape), dtype))
        psB = lambda name, shape, dtype=F32: esB.enter_context(nc.psum_tensor(name, list(shape), dtype))
        wq = sbB("wq", (128, 8 * 1536), BF16)
        wqv = _ftv(wq, 1536)
        wstg = [sbB("wstgB0", (128, 1536), F32)]
        ropeC = sbB("ropeCb", (128, XL), F32)
        ropeS = sbB("ropeSb", (128, XL), F32)
        w_ev = []
        wqkv_d = g["Wqkv"].rearrange("(a p) n -> a p n", p=128)
        stg_ev = [None]
        for kt in range(8):
            k.wait(sync, stg_ev[0])
            e = k.dma(sync.dma_start(out=wstg[0][:, :], in_=wqkv_d[kt]), "DL")
            k.wait(ac, e)
            stg_ev[0] = ee = k.fin(ac.copy(wqv[kt], wstg[0][:, :]), "AC")
            w_ev.append(ee)
        if "wqd" in DBG:
            k.wait_all(sync, w_ev)
            wqd = g["wqd"].rearrange("(a p) n -> a p n", p=128)
            for kt in range(8):
                k.dma(sync.dma_start(out=wqd[kt], in_=wqv[kt]), "DL")
        rc_ev = k.dma(sync.dma_start(out=ropeC[:, :], in_=g["ropeC"]), "DL")
        rs_ev = k.dma(sync.dma_start(out=ropeS[:, :], in_=g["ropeS"]), "DL")
        ones_ev = k.fin(dv.memset(v_tm[:, :], 1.0), "DV")

        pP = [psB(f"pP{i}", (128, 512)) for i in range(2)]
        ev_evac = [None, None]
        for of in range(8):
            for ci, (c0, cn) in enumerate(_chunks(XL)):
                slot = (of * 5 + ci) % 2
                k.wait_all(pe, w_ev + [xn_done, ev_evac[slot]])
                for kt in range(8):
                    mm = pe.matmul(pP[slot][:, 0:cn], wqv[kt][:, 128 * of:128 * of + 128],
                                   xnv[kt][:, c0:c0 + cn], start=(kt == 0), stop=(kt == 7))
                pe_ev = k.fin(mm, "PE")
                k.wait(ac, pe_ev)
                ev_evac[slot] = k.fin(ac.copy(qkv_[of][:, c0:c0 + cn], pP[slot][:, 0:cn]), "AC")
        qk_done = [k.now("AC")]
        for tt in range(19):
            slot = tt % 2
            k.wait_all(pe, [ev_evac[slot], ones_ev])
            for kt in range(8):
                mm = pe.matmul(pP[slot][:, 0:512], xnv[kt][:, 128 * tt:128 * tt + 128],
                               wqv[kt][:, 1024:1536], start=(kt == 0), stop=(kt == 7))
            pe_ev = k.fin(mm, "PE")
            k.wait(ac, pe_ev)
            ev_evac[slot] = k.fin(
                ac.copy(vtv[tt].rearrange("p (h c) -> p h c", c=65)[:, :, 0:64],
                        pP[slot][:, 0:512].rearrange("p (h c) -> p h c", c=64)), "AC")
        v_done = k.now("AC")
        qsw = sbB("qsw", (128, XL), BF16)
        w1 = sbB("ropew1", (128, XL), F32)
        w2 = sbB("ropew2", (128, XL), F32)
        rope_ev = []
        for of in range(8):
            k.wait_all(dv, qk_done + [rc_ev, rs_ev])
            for (a, b2) in [(0, 32), (64, 96)]:
                dv.tensor_copy(qsw[a:a + 32, :], qkv_[of][b2:b2 + 32, :])
                dv.tensor_copy(qsw[b2:b2 + 32, :], qkv_[of][a:a + 32, :])
            dv.tensor_tensor(w1[:, :], qkv_[of], ropeC[:, :], ALU.mult)
            dv.tensor_tensor(w2[:, :], qsw[:, :], ropeS[:, :], ALU.mult)
            rope_ev.append(k.fin(dv.tensor_tensor(qkv_[of], w1[:, :], w2[:, :], ALU.add), "DV"))
        rope_done = k.now("DV")
        if "qkd" in DBG:
            k.wait(sync, rope_done)
            qkd = g["qkd"].rearrange("(a p) n -> a p n", p=128)
            for of in range(8):
                k.dma(sync.dma_start(out=qkd[of], in_=qkv_[of]), "DL")
        if "vtd" in DBG:
            k.wait(sync, v_done)
            vtd = g["vtd"].rearrange("(a p) n -> a p n", p=128)
            for tt in range(19):
                k.dma(sync.dma_start(out=vtd[tt], in_=vtv[tt]), "DL")

    if "C" not in stages:
        k.wait(sync, k.now("DL"))
        k.wait(gp, k.now("DG"))
        return
    k.barrier([sync, gp, pe, dv, ac])
    es_cd = k.es.enter_context(ExitStack())
    s_in = es_cd.enter_context(nc.sbuf_tensor("s_in", [128, 4 * L0N], BF16))
    siv = _ftv(s_in, L0N)
    with ExitStack() as esC:
        sbC = lambda name, shape, dtype=F32: esC.enter_context(nc.sbuf_tensor(name, list(shape), dtype))
        psC = lambda name, shape, dtype=F32: esC.enter_context(nc.psum_tensor(name, list(shape), dtype))
        psS = [psC(f"psS{i}", (128, 384)) for i in range(2)]
        psO = [psC(f"psO{i}", (128, 128)) for i in range(2)]
        psTr = psC("psTr", (64, 1024), BF16)
        Pt = [sbC(f"Pt{i}", (128, 384), BF16) for i in range(2)]
        if "dbgo" in DBG:
            nc.lookup_dbgo = [sbC("dbgob", (128, 200), F32)]
        if "dbgp" in DBG:
            nc.lookup_dbgp = [sbC("dbgpb", (128, 384), F32), sbC("dbgpb2", (128, 384), F32)]
        atm = [sbC(f"atm{i}", (128, 64), BF16) for i in range(2)]
        rc = [sbC(f"rc{i}", (128, 1), F32) for i in range(2)]
        rc2 = [sbC(f"rc2{i}", (128, 1), F32) for i in range(2)]
        evS = [None, None]
        evP = [None, None]
        evO = [None, None]
        evA = [None, None]
        evT = [None] * 8
        first = [rope_done, v_done] + qk_done
        inst = 0
        for qt in range(NQT):
            for h in range(H0):
                sS, sO = inst % 2, inst % 2
                sT = inst % 8
                ftk = 4 + h // 2
                ftq = h // 2
                rb = (h % 2) * 64
                k.wait_all(pe, first + [evS[sS]])
                first = []
                for s in range(3):
                    mm = pe.matmul(psS[sS][:, 128 * s:128 * s + 128],
                                   qkv_[ftk][rb:rb + 64, 128 * (qt + s):128 * (qt + s) + 128],
                                   qkv_[ftq][rb:rb + 64, L0S + 128 * qt:L0S + 128 * qt + 128],
                                   start=True, stop=True)
                e_mm = k.fin(mm, "PE")
                k.wait_all(ac, [e_mm, evP[sS]])
                e_exp = k.fin(ac.activation(Pt[sS][:, :], psS[sS][:, :], AF.Exp,
                                            scale=SCALE0), "AC")
                evS[sS] = e_exp
                k.wait(dv, e_exp)
                dv.scalar_tensor_tensor(Pt[sS][:, 0:128], Pt[sS][:, 0:128],
                                        validc[:, qt:qt + 1], mSL[:, :], ALU.mult, ALU.mult)
                dv.tensor_scalar(Pt[sS][:, 128:256], Pt[sS][:, 128:256],
                                 validc[:, qt + 1:qt + 2], None, ALU.mult)
                e_msk = k.fin(dv.scalar_tensor_tensor(Pt[sS][:, 256:384], Pt[sS][:, 256:384],
                                                      validc[:, qt + 2:qt + 3], mUI[:, :],
                                                      ALU.mult, ALU.mult), "DV")
                k.wait_all(pe, [e_msk, evO[sO]])
                for s in range(3):
                    mm = pe.matmul(psO[sO][:, 0:65], Pt[sS][:, 128 * s:128 * s + 128],
                                   vtv[qt + s][:, 65 * h:65 * h + 65],
                                   start=(s == 0), stop=(s == 2))
                evP[sS] = e_pv = k.fin(mm, "PE")
                k.wait_all(dv, [e_pv, evA[sO]])
                e1 = k.fin(dv.tensor_scalar_add(rc[sO][:, :], psO[sO][:, 64:65], 1e-30), "DV")
                k.wait(dv, e1)
                e2 = k.fin(dv.reciprocal(rc2[sO][:, :], rc[sO][:, :]), "DV")
                k.wait(dv, e2)
                e_nrm = k.fin(dv.tensor_scalar(atm[sO][:, :], psO[sO][:, 0:64],
                                               rc2[sO][:, 0:1], None, ALU.mult), "DV")
                evO[sO] = e_nrm
                if "dbgp" in DBG and inst == 43:
                    dv.tensor_copy(nc.lookup_dbgp[0][:, 0:384], Pt[sS][:, 0:384])
                    dbge2 = k.fin(dv.tensor_copy(nc.lookup_dbgp[1][:, 0:384], psS[sS][:, 0:384]), "DV")
                    k.wait(sync, dbge2)
                    k.dma(sync.dma_start(out=g["dbgp"][0:128, :], in_=nc.lookup_dbgp[0][:, :]), "DL")
                    k.dma(sync.dma_start(out=g["dbgp"][128:256, :], in_=nc.lookup_dbgp[1][:, :]), "DL")
                if "dbgo" in DBG and inst == 43:
                    dv.tensor_copy(nc.lookup_dbgo[0][:, 0:128], psO[sO][:, 0:128])
                    dv.tensor_copy(nc.lookup_dbgo[0][:, 128:129], rc[sO][:, 0:1])
                    dv.tensor_copy(nc.lookup_dbgo[0][:, 129:130], rc2[sO][:, 0:1])
                    dbge = k.fin(dv.tensor_copy(nc.lookup_dbgo[0][:, 130:194], atm[sO][:, 0:64]), "DV")
                    k.wait(sync, dbge)
                    k.dma(sync.dma_start(out=g["dbgo"], in_=nc.lookup_dbgo[0][:, :]), "DL")
                    e_nrm = dbge
                k.wait_all(pe, [e_nrm, evT[sT]])
                e_tr = k.fin(pe.transpose(psTr[0:64, 128 * sT:128 * sT + 128],
                                          atm[sO][:, :], ident[:, :]), "PE")
                evA[sO] = e_tr
                k.wait(ac, e_tr)
                evT[sT] = k.fin(ac.copy(siv[ftq][rb:rb + 64, 128 * qt:128 * qt + 128],
                                        psTr[0:64, 128 * sT:128 * sT + 128]), "AC")
                inst += 1
        attn_done = k.now("AC")
        if "sind" in DBG:
            k.wait(sync, attn_done)
            sind = g["sind"].rearrange("(a p) n -> a p n", p=128)
            for ft1 in range(4):
                k.dma(sync.dma_start(out=sind[ft1], in_=siv[ft1]), "DL")
    k.barrier([sync, gp, pe, dv, ac])

    if "D" not in stages:
        k.wait(sync, k.now("DL"))
        k.wait(gp, k.now("DG"))
        return
    with ExitStack() as esD:
        sbD = lambda name, shape, dtype=F32: esD.enter_context(nc.sbuf_tensor(name, list(shape), dtype))
        psD = lambda name, shape, dtype=F32: esD.enter_context(nc.psum_tensor(name, list(shape), dtype))
        wo = sbD("wo", (128, 4 * 1024), BF16)
        wov = _ftv(wo, 1024)
        wstgD = [sbD(f"wstgD{i}", (128, 1024), F32) for i in range(2)]
        wo_d = g["Wout"].rearrange("(a p) n -> a p n", p=128)
        wo_ev = []
        stg_ev = [None, None]
        for kt in range(4):
            sl = kt % 2
            k.wait(sync, stg_ev[sl])
            e = k.dma(sync.dma_start(out=wstgD[sl][:, :], in_=wo_d[kt]), "DL")
            k.wait(ac, e)
            stg_ev[sl] = ee = k.fin(ac.copy(wov[kt], wstgD[sl][:, :]), "AC")
            wo_ev.append(ee)
        pD = [psD(f"pD{i}", (128, 512)) for i in range(2)]
        ev_evac = [None, None]
        for of in range(8):
            for ci, (c0, cn) in enumerate(_chunks(L0N)):
                slot = (of * 5 + ci) % 2
                k.wait_all(pe, wo_ev + [attn_done, ev_evac[slot]])
                for kt in range(4):
                    mm = pe.matmul(pD[slot][:, 0:cn], wov[kt][:, 128 * of:128 * of + 128],
                                   siv[kt][:, c0:c0 + cn], start=(kt == 0), stop=(kt == 3))
                e = k.fin(mm, "PE")
                k.wait(ac, e)
                ev_evac[slot] = k.fin(ac.copy(sv[of][:, c0:c0 + cn], pD[slot][:, 0:cn]), "AC")
        s_done = k.now("AC")
        if "s0" in DBG:
            k.wait(sync, s_done)
            for of in range(8):
                k.dma(sync.dma_start(out=g["s0"].rearrange("(a p) n -> a p n", p=128)[of],
                                     in_=sv[of]), "DL")
    es_cd.close()
    es_bc.close()
    k.barrier([sync, gp, pe, dv, ac])

    if "E" not in stages:
        k.wait(sync, k.now("DL"))
        k.wait(gp, k.now("DG"))
        return
    with ExitStack() as esE:
        sbE = lambda name, shape, dtype=F32: esE.enter_context(nc.sbuf_tensor(name, list(shape), dtype))
        psE = lambda name, shape, dtype=F32: esE.enter_context(nc.psum_tensor(name, list(shape), dtype))
        acc = sbE("accF", (128, 8 * L0N), BF16)
        accv = _ftv(acc, L0N)
        wg = sbE("wgf", (128, 8 * 1536), BF16)
        wu = sbE("wuf", (128, 8 * 1536), BF16)
        wd = sbE("wdf", (128, 12 * 1024), BF16)
        wgv, wuv, wdv = _ftv(wg, 1536), _ftv(wu, 1536), _ftv(wd, 1024)
        hbuf = sbE("hbuf", (128, 12 * 512), BF16)
        hv = _ftv(hbuf, 512)
        sg = sbE("sgf", (128, 512), BF16)
        pG = psE("pG", (128, 512))
        pU = psE("pU", (128, 512))
        pDn = [psE(f"pDn{i}", (128, 512)) for i in range(2)]
        wgd = g["wgT"].rearrange("(a p) n -> a p n", p=128)
        wud = g["wuT"].rearrange("(a p) n -> a p n", p=128)
        wdd = g["wdT"].rearrange("(a p) n -> a p n", p=128)
        es_stg = ExitStack()
        wstgE = [es_stg.enter_context(nc.sbuf_tensor(f"wstgE{i}", [128, 1536], I32))
                 for i in range(2)]
        hh_ev = []
        stg_ev = [None, None]
        for hh in range(2):
            k.wait_all(ac, hh_ev)
            wl_g, wl_u, wl_d = [], [], []
            tiles = ([(wgv[kt], wgd[kt][:, 1536 * hh:1536 * (hh + 1)], wl_g) for kt in range(8)]
                     + [(wuv[kt], wud[kt][:, 1536 * hh:1536 * (hh + 1)], wl_u) for kt in range(8)]
                     + [(wdv[kt][:, 0:1024], wdd[12 * hh + kt], wl_d) for kt in range(12)])
            for ti, (dst, dsrc, bucket) in enumerate(tiles):
                sl = ti % 2
                k.wait(sync, stg_ev[sl])
                e = k.dma(sync.dma_start(out=wstgE[sl][:, 0:dsrc.shape[1]], in_=dsrc), "DL")
                k.wait(ac, e)
                stg_ev[sl] = ee = k.fin(ac.copy(dst, wstgE[sl][:, 0:dsrc.shape[1]]), "AC")
                bucket.append(ee)
            ev_g = None
            last_tt = None
            ev_u = [None] * 12
            for ci, (c0, cn) in enumerate(_chunks(L0N)):
                for ht in range(12):
                    k.wait_all(pe, wl_g + [xn_done, ev_g])
                    for kt in range(8):
                        mm = pe.matmul(pG[:, 0:cn], wgv[kt][:, 128 * ht:128 * ht + 128],
                                       xnv[kt][:, L0S + c0:L0S + c0 + cn],
                                       start=(kt == 0), stop=(kt == 7))
                    e_g = k.fin(mm, "PE")
                    k.wait_all(ac, [e_g, last_tt])
                    e_sg = k.fin(ac.activation(sg[:, 0:cn], pG[:, 0:cn], AF.Silu,
                                               scale=1.0 / 127.0), "AC")
                    ev_g = e_sg
                    k.wait_all(pe, wl_u + [e_sg, last_tt])
                    for kt in range(8):
                        mm = pe.matmul(pU[:, 0:cn], wuv[kt][:, 128 * ht:128 * ht + 128],
                                       xnv[kt][:, L0S + c0:L0S + c0 + cn],
                                       start=(kt == 0), stop=(kt == 7))
                    e_um = k.fin(mm, "PE")
                    k.wait(dv, e_um)
                    ev_u[ht] = last_tt = k.fin(
                        dv.tensor_tensor(hv[ht][:, 0:cn], sg[:, 0:cn],
                                         pU[:, 0:cn], ALU.mult), "DV")
                for of in range(8):
                    slot = of % 2
                    k.wait_all(pe, ev_u + wl_d)
                    for kt in range(12):
                        mm = pe.matmul(pDn[slot][:, 0:cn], wdv[kt][:, 128 * of:128 * of + 128],
                                       hv[kt][:, 0:cn], start=(kt == 0), stop=(kt == 11))
                    e_d = k.fin(mm, "PE")
                    if hh == 0:
                        k.wait(ac, e_d)
                        e = k.fin(ac.copy(accv[of][:, c0:c0 + cn], pDn[slot][:, 0:cn]), "AC")
                    else:
                        k.wait(dv, e_d)
                        e = k.fin(dv.tensor_tensor(accv[of][:, c0:c0 + cn],
                                                   accv[of][:, c0:c0 + cn],
                                                   pDn[slot][:, 0:cn], ALU.add), "DV")
                    k.wait(pe, e)
            hh_ev = [k.now("PE"), k.now("AC"), k.now("DV")]
        ffn_done = (k.now("DV"), k.now("AC"))
        es_stg.close()
        k.barrier([sync, dv, ac])
        xs = [sbE(f"xsE{i}", (128, 512), F32) for i in range(2)]
        gt = [sbE(f"gtE{i}", (128, 512), BF16) for i in range(2)]
        x1t = [sbE(f"x1tE{i}", (128, 512), F32) for i in range(2)]
        x1d = g["x1sp"].rearrange("(a p) n -> a p n", p=128)
        st_ev = [None, None]
        gt_ev = [None, None]
        for of in range(8):
            for ci, (c0, cn) in enumerate(_chunks(L0N)):
                slot = (of * 5 + ci) % 2
                k.wait(sync, st_ev[slot])
                e_x = k.dma(sync.dma_start(out=xs[slot][:, 0:cn],
                                           in_=xTv[of][:, L0S + c0:L0S + c0 + cn]), "DL")
                k.wait_all(ac, [ffn_done[0], ffn_done[1], gt_ev[slot]])
                e_gt = k.fin(ac.activation(gt[slot][:, 0:cn], accv[of][:, c0:c0 + cn],
                                           AF.Silu, scale=1.0 / 16129.0), "AC")
                k.wait_all(dv, [e_gt, e_x])
                dv.tensor_tensor(gt[slot][:, 0:cn], gt[slot][:, 0:cn],
                                 sv[of][:, c0:c0 + cn], ALU.mult)
                gt_ev[slot] = e_c = k.fin(
                    dv.tensor_tensor(x1t[slot][:, 0:cn], xs[slot][:, 0:cn],
                                     gt[slot][:, 0:cn], ALU.add), "DV")
                k.wait(sync, e_c)
                st_ev[slot] = k.dma(sync.dma_start(out=x1d[of][:, c0:c0 + cn],
                                                   in_=x1t[slot][:, 0:cn]), "DL")
    es_de.close()
    x1_done = k.now("DL")
    k.barrier([sync, gp, pe, dv, ac])

    if "F" not in stages:
        k.wait(sync, k.now("DL"))
        k.wait(gp, k.now("DG"))
        return

    xn1v = [t[:, 0:L1N] for t in xnv]
    x1dv = g["x1sp"].rearrange("(a p) n -> a p n", p=128)

    with ExitStack() as esF1:
        sbF = lambda name, shape, dtype=F32: esF1.enter_context(nc.sbuf_tensor(name, list(shape), dtype))
        psF = lambda name, shape, dtype=F32: esF1.enter_context(nc.psum_tensor(name, list(shape), dtype))
        X1 = sbF("X1", (128, 8 * L1N), F32)
        X1v = _ftv(X1, L1N)
        x1_ev = []
        for ft in range(8):
            x1_ev.append(k.dma(sync.dma_start(out=X1v[ft], in_=x1dv[ft][:, 64:64 + L1N]), "DL"))
        wrkQ = [sbF(f"wrkQF{i}", (128, 512), F32) for i in range(2)]
        pS = psF("pSF", (1, 512))
        pQ = psF("pQF", (1, 512))
        pbcF1 = psF("pbcF1", (128, 512))
        pbcF2 = psF("pbcF2", (128, 512))
        srow = sbF("srowF", (1, L1N), F32)
        qrow = sbF("qrowF", (1, L1N), F32)
        q_ev = [None, None]
        for ci, (c0, cn) in enumerate(_chunks(L1N)):
            for ft in range(8):
                slot = ft % 2
                k.wait(ac, x1_ev[ft])
                k.wait(ac, q_ev[slot])
                sq = k.fin(ac.activation(wrkQ[slot][:, 0:cn], X1v[ft][:, c0:c0 + cn],
                                         AF.Square), "AC")
                k.wait(pe, sq)
                pe.matmul(pS[0:1, 0:cn], onescf[:, :], X1v[ft][:, c0:c0 + cn],
                          start=(ft == 0), stop=(ft == 7))
                mm = pe.matmul(pQ[0:1, 0:cn], onescf[:, :], wrkQ[slot][:, 0:cn],
                               start=(ft == 0), stop=(ft == 7))
                q_ev[slot] = k.fin(mm, "PE")
            k.wait(dv, q_ev[1])
            dv.tensor_copy(srow[0:1, c0:c0 + cn], pS[0:1, 0:cn])
            e = k.fin(dv.tensor_copy(qrow[0:1, c0:c0 + cn], pQ[0:1, 0:cn]), "DV")
            k.wait(pe, e)
        arow = sbF("arowF", (1, L1N), F32)
        brow = sbF("browF", (1, L1N), F32)
        dv.tensor_scalar_mul(srow[:, :], srow[:, :], 1.0 / D)
        dv.tensor_tensor(brow[:, :], srow[:, :], srow[:, :], ALU.mult)
        dv.scalar_tensor_tensor(brow[:, :], qrow[:, :], 1.0 / D, brow[:, :],
                                ALU.mult, ALU.subtract)
        ev = k.fin(dv.tensor_scalar_add(brow[:, :], brow[:, :], 1e-5), "DV")
        k.wait(ac, ev)
        ev = k.fin(ac.activation(brow[:, :], brow[:, :], AF.Sqrt), "AC")
        k.wait(dv, ev)
        e2 = k.fin(dv.reciprocal(arow[:, :], brow[:, :]), "DV")
        k.wait(dv, e2)
        dv.tensor_tensor(brow[:, :], srow[:, :], arow[:, :], ALU.mult)
        ev = k.fin(dv.tensor_scalar_mul(brow[:, :], brow[:, :], -1.0), "DV")
        Abb = sbF("AbbF", (128, L1N), F32)
        Bbb = sbF("BbbF", (128, L1N), F32)
        k.wait(pe, ev)
        bc_ev = []
        for (c0, cn) in _chunks(L1N):
            k.fin(pe.matmul(pbcF1[:, 0:cn], onesrf[0:1, :], arow[0:1, c0:c0 + cn],
                            start=True, stop=True), "PE")
            e2 = k.fin(pe.matmul(pbcF2[:, 0:cn], onesrf[0:1, :], brow[0:1, c0:c0 + cn],
                                 start=True, stop=True), "PE")
            k.wait(ac, e2)
            ac.copy(Abb[:, c0:c0 + cn], pbcF1[:, 0:cn])
            e3 = k.fin(ac.copy(Bbb[:, c0:c0 + cn], pbcF2[:, 0:cn]), "AC")
            bc_ev.append(e3)
            k.wait(pe, e3)
        wrk = sbF("wrkLNF", (128, L1N), F32)
        k.wait_all(dv, bc_ev)
        for ft in range(8):
            dv.tensor_tensor(wrk[:, :], X1v[ft], Abb[:, :], ALU.mult)
            k.fin(dv.tensor_tensor(xn1v[ft], wrk[:, :], Bbb[:, :], ALU.add), "DV")
        xn1_done = k.now("DV")
        if "xn1" in DBG:
            k.wait(sync, xn1_done)
            for ft in range(8):
                k.dma(sync.dma_start(out=g["xn1"].rearrange("(a p) n -> a p n", p=128)[ft],
                                     in_=xn1v[ft]), "DL")
    k.barrier([sync, gp, pe, dv, ac])

    es_l1 = k.es.enter_context(ExitStack())
    q_in = es_l1.enter_context(nc.sbuf_tensor("q_in", [128, 8 * L1N], BF16))
    k_in = es_l1.enter_context(nc.sbuf_tensor("k_in", [128, 8 * L1N], BF16))
    v_in = es_l1.enter_context(nc.sbuf_tensor("v_in", [128, 8 * L1N], BF16))
    qiv, kiv, viv = _ftv(q_in, L1N), _ftv(k_in, L1N), _ftv(v_in, L1N)
    convw = es_l1.enter_context(nc.sbuf_tensor("convwF", [128, 3 * 8 * 4], F32))
    convv_ = _ftv(convw, 4)
    m3 = es_l1.enter_context(nc.sbuf_tensor("m3F", [128, 3], F32))
    selh2 = es_l1.enter_context(nc.sbuf_tensor("selh2F", [128, 8 * 16], BF16))
    selh2v = _ftv(selh2, 16)
    selhb = es_l1.enter_context(nc.sbuf_tensor("selhbF", [16, 8 * 128], BF16))
    selhbv = _ftv(selhb, 128)
    selhf = es_l1.enter_context(nc.sbuf_tensor("selhfF", [16, 8 * 128], F32))
    selhfv = _ftv(selhf, 128)
    cbc = es_l1.enter_context(nc.sbuf_tensor("cbcF", [128, 9], F32))
    es_mid = k.es.enter_context(ExitStack())
    abq = es_mid.enter_context(nc.sbuf_tensor("abqF", [48, L1N], F32))

    ld_ev = []
    for tsb, name in [(m3, "mask3"), (selh2, "selh2"), (selhb, "selh"),
                      (selhf, "selhf"), (cbc, "ctrlb2")]:
        ld_ev.append(k.dma(sync.dma_start(out=tsb[:, :], in_=g[name]), "DL"))
    for ci_, name in [(0, "convq"), (1, "convk"), (2, "convv")]:
        cd = g[name].rearrange("(a p) n -> a p n", p=128)
        for ft in range(8):
            ld_ev.append(k.dma(sync.dma_start(out=convv_[8 * ci_ + ft], in_=cd[ft]), "DL"))

    with ExitStack() as esF2:
        sbF2 = lambda name, shape, dtype=F32: esF2.enter_context(nc.sbuf_tensor(name, list(shape), dtype))
        psF2 = lambda name, shape, dtype=F32: esF2.enter_context(nc.psum_tensor(name, list(shape), dtype))
        og = sbF2("ogF", (128, 8 * L1N), BF16)
        ogv = _ftv(og, L1N)
        wstg = sbF2("wstgF", (128, 128), F32)
        wti = [sbF2(f"wtiF{i}", (128, 8 * 128), BF16) for i in range(2)]
        pF = [psF2(f"pFF{i}", (128, 512)) for i in range(2)]
        qkvT_d = g["qkvT"].rearrange("(a p) n -> a p n", p=128)
        ctrlT_d = g["ctrlT"].rearrange("(a p) n -> a p n", p=128)
        stg_ev = None
        wfree = [None, None]
        ev_evac = [None, None]
        for of in range(33):
            wsl = of % 2
            wtiv = _ftv(wti[wsl], 128)
            ncw = 128
            cnv = []
            for kt in range(8):
                k.wait_all(sync, [stg_ev])
                if of < 24:
                    dsrc = qkvT_d[kt][:, 128 * of:128 * of + 128]
                else:
                    c0w = 128 * (of - 24)
                    dsrc = ctrlT_d[kt][:, c0w:c0w + 128]
                e = k.dma(sync.dma_start(out=wstg[:, 0:ncw], in_=dsrc), "DL")
                k.wait_all(ac, [e, wfree[wsl]])
                stg_ev = ee = k.fin(ac.copy(wtiv[kt][:, 0:ncw], wstg[:, 0:ncw]), "AC")
                cnv.append(ee)
            for ci, (c0, cn) in enumerate(_chunks(L1N)):
                slot = (of * 5 + ci) % 2
                k.wait_all(pe, cnv + [xn1_done, ev_evac[slot]])
                for kt in range(8):
                    mm = pe.matmul(pF[slot][0:ncw, 0:cn], wtiv[kt][:, 0:ncw],
                                   xn1v[kt][:, c0:c0 + cn], start=(kt == 0), stop=(kt == 7))
                e_mm = k.fin(mm, "PE")
                if ci == 4:
                    wfree[wsl] = e_mm
                k.wait(ac, e_mm)
                if of < 24:
                    dst = (qiv, kiv, viv)[of // 8][of % 8]
                    ev_evac[slot] = k.fin(ac.copy(dst[:, c0:c0 + cn], pF[slot][:, 0:cn]), "AC")
                else:
                    t = of - 24
                    evs = None
                    if t < 8:
                        evs = ac.activation(ogv[t][0:64, c0:c0 + cn],
                                            pF[slot][64:128, 0:cn], AF.Sigmoid,
                                            bias=cbc[64:128, t:t + 1])
                    if t >= 1:
                        evs = ac.activation(ogv[t - 1][64:128, c0:c0 + cn],
                                            pF[slot][0:64, 0:cn], AF.Sigmoid,
                                            bias=cbc[0:64, t:t + 1])
                    if t == 0:
                        evs = ac.activation(abq[0:16, c0:c0 + cn], pF[slot][0:16, 0:cn],
                                            AF.Sigmoid, bias=cbc[0:16, t:t + 1])
                        evs = ac.activation(abq[32:48, c0:c0 + cn], pF[slot][32:48, 0:cn],
                                            AF.Sigmoid, bias=cbc[32:48, t:t + 1])
                    ev_evac[slot] = k.fin(evs, "AC")
        proj_done = [k.now("AC")]
        k.wait_all(sync, proj_done)
        og_d = g["ogsp"].rearrange("(a p) n -> a p n", p=128)
        xn1_d = g["xn1sv"].rearrange("(a p) n -> a p n", p=128)
        sp_ev = []
        for ft in range(8):
            sp_ev.append(k.dma(sync.dma_start(out=og_d[ft], in_=ogv[ft][:, SOFF:SOFF + SPAN]), "DL"))
            sp_ev.append(k.dma(sync.dma_start(out=xn1_d[ft], in_=xn1v[ft]), "DL"))
        k.wait_all(sync, sp_ev)
        k.barrier([sync, gp, pe, dv, ac])
    es_xn.close()
    k.barrier([sync, gp, pe, dv, ac])

    kbt = es_l1.enter_context(nc.sbuf_tensor("kbt", [128, SPAN], BF16, side="right"))
    rows = es_mid.enter_context(nc.sbuf_tensor("rowsF", [16, 5 * SPAN], F32))
    rowv = _ftv(rows, SPAN)
    rbf = es_mid.enter_context(nc.sbuf_tensor("rbfF", [16, SPAN], BF16))
    sct = es_l1.enter_context(nc.sbuf_tensor("sctF", [128, 8 * 32], F32, side="right"))
    sctv = _ftv(sct, 32)

    with ExitStack() as esF3:
        sbF3 = lambda name, shape, dtype=F32: esF3.enter_context(nc.sbuf_tensor(name, list(shape), dtype))
        psF3 = lambda name, shape, dtype=F32: esF3.enter_context(nc.psum_tensor(name, list(shape), dtype))
        k.wait_all(dv, proj_done)
        e = k.fin(dv.tensor_scalar(abq[0:16, SOFF:SOFF + SPAN], abq[0:16, SOFF:SOFF + SPAN],
                                   0.98, 0.01, ALU.mult, ALU.add), "DV")
        ab_done = [e]

        es_acc = ExitStack()
        acc = es_acc.enter_context(nc.sbuf_tensor("accF3", [128, SPAN], F32))
        conv_done = []
        k.wait_all(dv, ld_ev)
        for ci_, tiles in [(0, qiv), (1, kiv), (2, viv)]:
            for ft in range(8):
                tl = tiles[ft]
                k.wait_all(dv, proj_done + conv_done[-1:])
                dv.tensor_tensor(tl[:, 61:64], tl[:, 61:64], m3[:, :], ALU.mult)
                dv.tensor_scalar(acc[:, :], tl[:, 61:61 + SPAN],
                                 convv_[8 * ci_ + ft][:, 0:1], None, ALU.mult)
                for j in range(1, 4):
                    dv.scalar_tensor_tensor(acc[:, :], tl[:, 61 + j:61 + j + SPAN],
                                            convv_[8 * ci_ + ft][:, j:j + 1], acc[:, :],
                                            ALU.mult, ALU.add)
                e = k.fin(dv.engine_nop(), "DV")
                k.wait(ac, e)
                conv_done.append(k.fin(ac.activation(tl[:, SOFF:SOFF + SPAN], acc[:, :],
                                                     AF.Silu), "AC"))
        conv_all = k.now("AC")
        k.wait(dv, conv_all)
        e = k.fin(dv.tensor_copy(rowv[4][0:16, :], abq[32:48, SOFF:SOFF + SPAN]), "DV")
        beta_done = e
        es_acc.close()

        sqb = sbF3("sqbF", (128, SPAN), BF16)
        pR = psF3("pRF", (16, SPAN))
        for which, tiles, dstrow in [(0, qiv, 2), (1, kiv, 3)]:
            sq_ev = None
            for ft in range(8):
                k.wait_all(ac, [conv_all, sq_ev])
                e_sq = k.fin(ac.activation(sqb[:, :], tiles[ft][:, SOFF:SOFF + SPAN],
                                           AF.Square), "AC")
                k.wait(pe, e_sq)
                for (c0, cn) in _chunks(SPAN):
                    mm = pe.matmul(pR[0:16, c0:c0 + cn], selh2v[ft][:, 0:16],
                                   sqb[:, c0:c0 + cn], start=(ft == 0), stop=(ft == 7))
                sq_ev = k.fin(mm, "PE")
            e = sq_ev
            k.wait(dv, e)
            e = k.fin(dv.tensor_copy(rowv[1][0:16, :], pR[0:16, 0:SPAN]), "DV")
            k.wait(pe, e)
            k.wait(ac, e)
            e = k.fin(ac.activation(rowv[1][0:16, :], rowv[1][0:16, :], AF.Sqrt), "AC")
            k.wait(dv, e)
            e = k.fin(dv.tensor_scalar_max(rowv[1][0:16, :], rowv[1][0:16, :], 1e-6), "DV")
            k.wait(dv, e)
            e = k.fin(dv.reciprocal(rowv[dstrow][0:16, :], rowv[1][0:16, :]), "DV")
            k.wait(dv, e)
            l2n_done = e
        k.wait_all(ac, ab_done + [l2n_done])
        e = k.fin(ac.activation(rowv[1][0:16, :], abq[0:16, SOFF:SOFF + SPAN], AF.Ln), "AC")
        k.wait(dv, e)
        cvv = rowv[1][0:16, :].rearrange("p (c u) -> p c u", u=CH)
        for sshift in [1, 2, 4, 8, 16, 32]:
            dv.tensor_tensor(cvv[:, :, sshift:CH], cvv[:, :, sshift:CH],
                             cvv[:, :, 0:CH - sshift], ALU.add)
        e = k.fin(dv.engine_nop(), "DV")
        k.wait(ac, e)
        e_gam = k.fin(ac.activation(rowv[0][0:16, :], rowv[1][0:16, :], AF.Exp), "AC")
        k.wait(dv, e_gam)
        gcv2 = rowv[0][0:16, :].rearrange("p (c u) -> p c u", u=CH)
        e_gc = k.fin(dv.tensor_copy(rowv[1][0:16, 0:NCH], gcv2[:, :, CH - 1]), "DV")

        cbt = sbF3("cbtF", (128, SPAN), BF16)
        pB = [psF3(f"pBF{i}", (128, 512)) for i in range(2)]
        k.wait(pe, e_gc)
        for p in range(8):
            e_mm = k.fin(pe.matmul(pB[0][:, 0:NCH], selhfv[p][:, :], rowv[1][0:16, 0:NCH],
                                   start=True, stop=True), "PE")
            k.wait(ac, e_mm)
            e_sc = k.fin(ac.copy(sctv[p], pB[0][:, 0:NCH]), "AC")
            k.wait(pe, e_sc)
        sct_done = k.now("AC")
        pe_sct_done = k.now("PE")
        RC = rowv[1]
        last_mm = None
        mul_done = []
        k.wait_all(dv, [beta_done])
        k.wait(dv, pe_sct_done)
        for r_ in [3, 2, 1, 4]:
            if r_ == 3:
                dv.tensor_tensor(RC[0:16, :], rowv[0][0:16, :], rowv[3][0:16, :], ALU.mult)
                e = k.fin(dv.tensor_tensor(RC[0:16, :], RC[0:16, :],
                                           rowv[4][0:16, :], ALU.mult), "DV")
            elif r_ == 2:
                e = k.fin(dv.reciprocal(RC[0:16, :], rowv[0][0:16, :]), "DV")
                k.wait(dv, e)
                e = k.fin(dv.tensor_tensor(RC[0:16, :], RC[0:16, :],
                                           rowv[3][0:16, :], ALU.mult), "DV")
            elif r_ == 1:
                e = k.fin(dv.tensor_tensor(RC[0:16, :], rowv[0][0:16, :],
                                           rowv[2][0:16, :], ALU.mult), "DV")
            else:
                e = k.fin(dv.tensor_copy(RC[0:16, :], rowv[4][0:16, :]), "DV")
            k.wait_all(dv, [e, last_mm])
            e_rb = k.fin(dv.tensor_copy(rbf[0:16, 0:SPAN], RC[0:16, :]), "DV")
            iskb = (r_ == 3)
            tiles = (None, qiv, kiv, kiv, viv)[r_]
            for ft in range(8):
                ev_evac2 = None
                for ci, (c0, cn) in enumerate(_chunks(SPAN)):
                    slot = ci % 2
                    k.wait_all(pe, [e_rb, ev_evac2, sct_done, last_mm])
                    last_mm = e_mm = k.fin(pe.matmul(pB[slot][:, 0:cn], selhbv[ft][:, :],
                                                     rbf[0:16, c0:c0 + cn],
                                                     start=True, stop=True), "PE")
                    k.wait_all(ac, [e_mm] + (mul_done[-1:] if ci == 0 else []))
                    ev_evac2 = k.fin(ac.copy(cbt[:, c0:c0 + cn], pB[slot][:, 0:cn]), "AC")
                k.wait_all(dv, [ev_evac2] + mul_done[-1:])
                if iskb:
                    e_m = k.fin(dv.tensor_tensor(kbt[:, :], tiles[ft][:, SOFF:SOFF + SPAN],
                                                 cbt[:, :], ALU.mult), "DV")
                    k.wait(sync, e_m)
                    e_kb = k.dma(sync.dma_start(out=g["kbsv"].rearrange("(a p) n -> a p n", p=128)[ft],
                                                in_=kbt[:, :]), "DL")
                    mul_done.append(e_kb)
                else:
                    e_m = k.fin(dv.tensor_tensor(tiles[ft][:, SOFF:SOFF + SPAN],
                                                 tiles[ft][:, SOFF:SOFF + SPAN],
                                                 cbt[:, :], ALU.mult), "DV")
                mul_done.append(e_m)
        scaled_done = k.now("DV")
        if "l1t" in DBG:
            k.wait(sync, scaled_done)
            l1t_d = g["l1t"].rearrange("(a w) n -> a w n", w=128)
            for ft in range(8):
                k.dma(sync.dma_start(out=l1t_d[ft], in_=kiv[ft][:, SOFF:SOFF + SPAN]), "DL")
                k.dma(sync.dma_start(out=l1t_d[8 + ft], in_=g["kbsv"].rearrange("(a p) n -> a p n", p=128)[ft]), "DL")
                k.dma(sync.dma_start(out=l1t_d[16 + ft], in_=viv[ft][:, SOFF:SOFF + SPAN]), "DL")
            k.dma(sync.dma_start(out=g["l1r"][0:16, 0:SPAN], in_=rowv[0][0:16, :]), "DL")
            k.dma(sync.dma_start(out=g["l1r"][16:32, 0:SPAN], in_=abq[0:16, SOFF:SOFF + SPAN]), "DL")
            k.dma(sync.dma_start(out=g["l1r"][32:48, 0:SPAN], in_=abq[32:48, SOFF:SOFF + SPAN]), "DL")
        k.wait(sync, k.now("DL"))
    es_mid.close()
    k.barrier([sync, gp, pe, dv, ac])

    if "G" not in stages:
        k.wait(sync, k.now("DL"))
        k.wait(gp, k.now("DG"))
        return
    emit_stageG(nc, k, g, ident, qiv, kiv, viv, sctv, DBG, glim)
    es_l1.close()
    k.barrier([sync, gp, pe, dv, ac])

    if "H" not in stages:
        k.wait(sync, k.now("DL"))
        k.wait(gp, k.now("DG"))
        return
    emit_stageH(nc, k, g, ident, DBG)
    k.wait(sync, k.now("DL"))
    k.wait(gp, k.now("DG"))



def emit_stageG(nc, k, g, ident, qiv, kiv, viv, sctv, DBG, glim=(NPAIR, NCH // 4)):
    sync, gp, pe, dv, ac = nc.sync, nc.gpsimd, nc.tensor, nc.vector, nc.scalar
    qs = [t[:, SOFF:SOFF + SPAN] for t in qiv]
    ks = [t[:, SOFF:SOFF + SPAN] for t in kiv]
    vs = [t[:, SOFF:SOFF + SPAN] for t in viv]
    kb_d = g["kbsv"].rearrange("(a p) n -> a p n", p=128)
    with ExitStack() as esG:
        sb = lambda name, shape, dtype=F32: esG.enter_context(nc.sbuf_tensor(name, list(shape), dtype))
        ps = lambda name, shape, dtype=F32: esG.enter_context(nc.psum_tensor(name, list(shape), dtype))
        msl4 = sb("msl4G", (128, 512), BF16)
        msu4 = sb("msu4G", (128, 512), BF16)
        mui4 = sb("mui4G", (128, 512), BF16)
        mbd8 = sb("mbd8G", (128, 1024), BF16)
        ld_ev = []
        for t, name in [(msl4, "msl4"), (msu4, "msu4"), (mui4, "mui4"), (mbd8, "mbd8")]:
            ld_ev.append(k.dma(sync.dma_start(out=t[:, :], in_=g[name]), "DL"))
        qP = sb("qPG", (128, NCH * 128), BF16)
        khP = sb("khPG", (128, NCH * 128), BF16)
        kbP = sb("kbPG", (128, NCH * 128), BF16)
        UF = sb("UFG", (128, NCH * 256), BF16)
        FT = sb("FTG", (128, NCH * 128), BF16)
        KTM = sb("KTMG", (128, NCH * 128), BF16)
        ptg = [sb(f"ptgG{i}", (128, 512), BF16) for i in range(2)]
        kbl = sb("kblG", (128, SPAN), BF16)
        MPRE = sb("MPREG", (128, (NCH + 1) * 128), BF16)
        ST0b = sb("ST0G", (128, (NCH + 1) * 128), BF16)
        MT2 = sb("MT2G", (128, 2 * 128), BF16)
        WMC = sb("WMCG", (128, 2 * 128), BF16)
        NAg = sb("NAgG", (128, 512), BF16)
        NATg = sb("NATgG", (128, 512), BF16)
        PgA = sb("PgAG", (128, 512), BF16)
        PgAT = sb("PgATG", (128, 512), BF16)
        negNc = sb("negNcG", (128, 128), BF16)
        MCf = sb("MCfG", (128, 128), F32)
        CCf = sb("CCfG", (128, 128), F32)
        psA = ps("psAG", (128, 512))
        psB = ps("psBG", (128, 512))
        psT = ps("psTG", (128, 1024), BF16)
        psC = [ps(f"psCG{i}", (128, 128)) for i in range(2)]
        psD = [ps(f"psDG{i}", (128, 128)) for i in range(2)]

        e_z = None
        for t in (qP, khP, kbP):
            e_z = k.fin(dv.memset(t[:, :], 0.0), "DV")
        k.wait(pe, e_z)
        for j in range(4):
            pe.transpose(psT[64:128, 256 * j:256 * j + 64],
                         qP[64:128, 0:64], ident[64:128, 64:128])
            pe.transpose(psT[0:64, 256 * j + 64:256 * j + 128],
                         qP[0:64, 64:128], ident[0:64, 0:64])

        ufd = g["ufsp"]; ftd = g["ftsp"]; ktd = g["ktsp"]; ptd = g["ptsp"]
        mpd = g["mpsp"]; std = g["stsp"]; mcd = g["mcsp"]; ccd = g["ccsp"]
        qtd = g["qtsp"]
        spill_ev = []
        k.wait_all(dv, ld_ev + [e_z])

        def pview(t):
            v3 = t.rearrange("p (c u) -> p c u", u=128)
            return v3

        for p in range(glim[0]):
            PH = glim[3] if len(glim) > 3 else 31
            e_kb = k.dma(sync.dma_start(out=kbl[:, :], in_=kb_d[p]), "DL")
            k.wait_all(dv, spill_ev + [e_kb])
            k.wait_all(ac, spill_ev)
            pb_ev = []
            for (dst, srcv) in [(qP, qs), (khP, ks), (kbP, None)]:
                d3 = pview(dst[:, :])
                if srcv is None:
                    se = kbl[0:64, :].rearrange("p (c u) -> p c u", u=CH)
                    so = kbl[64:128, :].rearrange("p (c u) -> p c u", u=CH)
                else:
                    se = srcv[p][0:64, :].rearrange("p (c u) -> p c u", u=CH)
                    so = srcv[p][64:128, :].rearrange("p (c u) -> p c u", u=CH)
                dv.tensor_copy(d3[0:64, :, 0:64], se)
                pb_ev.append(k.fin(dv.tensor_copy(d3[64:128, :, 64:128], so), "DV"))
            dv.tensor_copy(MPRE[:, 0:128], ident[:, :])
            dv.tensor_copy(MT2[:, 0:128], ident[:, :])
            e_init = k.fin(dv.memset(ST0b[:, 0:128], 0.0), "DV")
            ev_psA = None
            ev_psB = None
            ev_psT = None
            ev_chain = e_init
            pt_sp = [None, None]
            for gi in range(glim[1]):
                cs = [4 * gi + j for j in range(4)]
                if not (PH & 1):
                    break
                k.wait_all(pe, [ev_psA, e_init] + pb_ev)
                for j, c in enumerate(cs):
                    mm = pe.matmul(psA[:, 128 * j:128 * j + 128], kbP[:, 128 * c:128 * c + 128],
                                   khP[:, 128 * c:128 * c + 128], start=True, stop=True)
                e = k.fin(mm, "PE")
                k.wait(dv, e)
                ev_psA = k.fin(dv.tensor_tensor(NAg[:, :], psA[:, :], msl4[:, :], ALU.mult), "DV")
                k.wait_all(pe, [ev_psB])
                for j, c in enumerate(cs):
                    mm = pe.matmul(psB[:, 128 * j:128 * j + 128], khP[:, 128 * c:128 * c + 128],
                                   kbP[:, 128 * c:128 * c + 128], start=True, stop=True)
                e = k.fin(mm, "PE")
                k.wait(dv, e)
                ev_psB = k.fin(dv.tensor_tensor(NATg[:, :], psB[:, :], msu4[:, :], ALU.mult), "DV")
                k.wait_all(pe, [ev_psA])
                for j, c in enumerate(cs):
                    mm = pe.matmul(psA[:, 128 * j:128 * j + 128], khP[:, 128 * c:128 * c + 128],
                                   qP[:, 128 * c:128 * c + 128], start=True, stop=True)
                e = k.fin(mm, "PE")
                k.wait_all(dv, [e, pt_sp[gi % 2]])
                e_ptw = k.fin(dv.tensor_tensor(ptg[gi % 2][:, :], psA[:, :],
                                               mui4[:, :], ALU.mult), "DV")
                ev_psA = e_ptw
                k.wait(sync, e_ptw)
                pt_sp[gi % 2] = k.dma(sync.dma_start(out=ptd[p][:, 512 * gi:512 * gi + 512],
                                                     in_=ptg[gi % 2][:, :]), "DL")
                if not (PH & 2):
                    continue
                k.wait_all(pe, [ev_psT])
                for j, c in enumerate(cs):
                    pe.transpose(psT[0:64, 256 * j:256 * j + 64],
                                 vs[p][0:64, CH * c:CH * c + CH], ident[0:64, 0:64])
                    pe.transpose(psT[64:128, 256 * j + 64:256 * j + 128],
                                 vs[p][64:128, CH * c:CH * c + CH], ident[64:128, 64:128])
                    mm = pe.transpose(psT[:, 256 * j + 128:256 * j + 256],
                                      kbP[:, 128 * c:128 * c + 128], ident[:, :])
                e = k.fin(mm, "PE")
                k.wait(dv, e)
                e_uf0 = k.fin(dv.tensor_tensor(UF[:, 1024 * gi:1024 * gi + 1024], psT[:, :],
                                               mbd8[:, :], ALU.mult), "DV")
                k.wait_all(pe, [e_uf0])
                for j, c in enumerate(cs):
                    mm = pe.transpose(psT[:, 128 * j:128 * j + 128],
                                      khP[:, 128 * c:128 * c + 128], ident[:, :])
                e = k.fin(mm, "PE")
                k.wait(ac, e)
                ev_psT = k.fin(ac.copy(KTM[:, 512 * gi:512 * gi + 512], psT[:, 0:512]), "AC")
                if not (PH & 4):
                    continue
                powers = [(PgA, PgAT), (NAg, NATg), (PgA, PgAT), (NAg, NATg)]
                curP, curT = NAg, NATg
                e_apl = e_uf0
                for lev in range(6):
                    for rnd in range(2):
                        k.wait_all(pe, [ev_psB, e_apl, ev_psT])
                        for j in (0, 1):
                            c = cs[2 * rnd + j]
                            mm = pe.matmul(psB[:, 256 * j:256 * j + 256],
                                           curT[:, 128 * (2 * rnd + j):128 * (2 * rnd + j) + 128],
                                           UF[:, 256 * c:256 * c + 256], start=True, stop=True)
                        e = k.fin(mm, "PE")
                        k.wait(dv, e)
                        ev_psB = k.fin(dv.tensor_tensor(
                            UF[:, 1024 * gi + 512 * rnd:1024 * gi + 512 * rnd + 512],
                            UF[:, 1024 * gi + 512 * rnd:1024 * gi + 512 * rnd + 512],
                            psB[:, :], ALU.add), "DV")
                        e_apl = ev_psB
                    if lev == 5:
                        break
                    if lev < 4:
                        nxtP, nxtT = powers[lev]
                        k.wait_all(pe, [ev_psA, ev_psB])
                        for j in range(4):
                            mm = pe.matmul(psA[:, 128 * j:128 * j + 128], curT[:, 128 * j:128 * j + 128],
                                           curP[:, 128 * j:128 * j + 128], start=True, stop=True)
                        e = k.fin(mm, "PE")
                        k.wait(ac, e)
                        ev_psA = e_p = k.fin(ac.copy(nxtP[:, :], psA[:, :]), "AC")
                        k.wait_all(pe, [ev_psB, e_p])
                        for j in range(4):
                            mm = pe.matmul(psB[:, 128 * j:128 * j + 128], curP[:, 128 * j:128 * j + 128],
                                           curT[:, 128 * j:128 * j + 128], start=True, stop=True)
                        e = k.fin(mm, "PE")
                        k.wait(ac, e)
                        ev_psB = k.fin(ac.copy(nxtT[:, :], psB[:, :]), "AC")
                        curP, curT = nxtP, nxtT
                    else:
                        dstT = PgA if curP is not PgA else PgB
                        k.wait_all(pe, [ev_psA])
                        for j in range(4):
                            mm = pe.matmul(psA[:, 128 * j:128 * j + 128], curP[:, 128 * j:128 * j + 128],
                                           curT[:, 128 * j:128 * j + 128], start=True, stop=True)
                        e = k.fin(mm, "PE")
                        k.wait(ac, e)
                        ev_psA = k.fin(ac.copy(dstT[:, :], psA[:, :]), "AC")
                        curT = dstT
                if not (PH & 8):
                    continue
                k.wait_all(pe, [ev_psT, ev_psB])
                for j, c in enumerate(cs):
                    mm = pe.transpose(psT[:, 128 * j:128 * j + 128],
                                      UF[:, 256 * c + 128:256 * c + 256], ident[:, :])
                e = k.fin(mm, "PE")
                k.wait(ac, e)
                ev_psT = k.fin(ac.activation(FT[:, 512 * gi:512 * gi + 512], psT[:, 0:512],
                                             AF.Copy, scale=-1.0), "AC")
                for j, c in (enumerate(cs) if (len(glim) < 3 or glim[2]) else []):
                    so, sn = c % 2, (c + 1) % 2
                    k.wait_all(pe, [ev_psB, ev_psT, ev_chain])
                    e = k.fin(pe.matmul(psC[0][:, :], UF[:, 256 * c + 128:256 * c + 256],
                                        KTM[:, 128 * c:128 * c + 128],
                                        start=True, stop=True), "PE")
                    k.wait(dv, e)
                    e_nc = k.fin(dv.tensor_scalar_mul(negNc[:, :], psC[0][:, :], -1.0), "DV")
                    k.wait_all(pe, [e_nc])
                    pe.matmul(psC[1][:, :], ident[:, :], MT2[:, 128 * so:128 * so + 128],
                              start=True, stop=False)
                    e = k.fin(pe.matmul(psC[1][:, :], negNc[:, :],
                                        MT2[:, 128 * so:128 * so + 128],
                                        start=False, stop=True), "PE")
                    k.wait(ac, e)
                    e_mt = k.fin(ac.activation(MT2[:, 128 * sn:128 * sn + 128], psC[1][:, :],
                                               AF.Copy, scale=sctv[p][:, c:c + 1]), "AC")
                    pe.matmul(psC[0][:, :], ident[:, :], MPRE[:, 128 * c:128 * c + 128],
                              start=True, stop=False)
                    e = k.fin(pe.matmul(psC[0][:, :], MT2[:, 128 * so:128 * so + 128],
                                        negNc[:, :], start=False, stop=True), "PE")
                    k.wait(ac, e)
                    e_mp = k.fin(ac.activation(MPRE[:, 128 * (c + 1):128 * (c + 2)], psC[0][:, :],
                                               AF.Copy, scale=sctv[p][:, c:c + 1]), "AC")
                    if c == NCH - 1:
                        e_mp = k.fin(ac.activation(MCf[:, :], psC[0][:, :], AF.Copy,
                                                   scale=sctv[p][:, c:c + 1]), "AC")
                    k.wait_all(pe, [e_mt, e_mp])
                    pe.matmul(psD[0][:, :], ident[:, :], UF[:, 256 * c:256 * c + 128],
                              start=True, stop=False)
                    e = k.fin(pe.matmul(psD[0][:, :], FT[:, 128 * c:128 * c + 128],
                                        ST0b[:, 128 * c:128 * c + 128],
                                        start=False, stop=True), "PE")
                    k.wait(ac, e)
                    e_wm = k.fin(ac.copy(WMC[:, 128 * so:128 * so + 128], psD[0][:, :]), "AC")
                    k.wait_all(pe, [e_wm])
                    pe.matmul(psD[1][:, :], ident[:, :], ST0b[:, 128 * c:128 * c + 128],
                              start=True, stop=False)
                    e = k.fin(pe.matmul(psD[1][:, :], KTM[:, 128 * c:128 * c + 128],
                                        WMC[:, 128 * so:128 * so + 128],
                                        start=False, stop=True), "PE")
                    k.wait(ac, e)
                    e_st = k.fin(ac.activation(ST0b[:, 128 * (c + 1):128 * (c + 2)], psD[1][:, :],
                                               AF.Copy, scale=sctv[p][:, c:c + 1]), "AC")
                    if c == NCH - 1:
                        e_st = k.fin(ac.activation(CCf[:, :], psD[1][:, :], AF.Copy,
                                                   scale=sctv[p][:, c:c + 1]), "AC")
                    ev_chain = e_st
                    k.wait(pe, e_st)
            if not (PH & 16):
                continue
            k.wait_all(sync, [ev_chain, ev_psA, ev_psB, ev_psT, k.now("DV"), k.now("AC")])
            spill_ev = [
                k.dma(sync.dma_start(out=ufd[p], in_=UF[:, :]), "DL"),
                k.dma(sync.dma_start(out=ftd[p], in_=FT[:, :]), "DL"),
                k.dma(sync.dma_start(out=ktd[p], in_=KTM[:, :]), "DL"),
                k.dma(sync.dma_start(out=mpd[p], in_=MPRE[:, 0:NCH * 128]), "DL"),
                k.dma(sync.dma_start(out=std[p], in_=ST0b[:, 0:NCH * 128]), "DL"),
                k.dma(sync.dma_start(out=mcd[p], in_=MCf[:, :]), "DL"),
                k.dma(sync.dma_start(out=ccd[p], in_=CCf[:, :]), "DL"),
                k.dma(sync.dma_start(out=qtd[p], in_=qP[:, :]), "DL"),
            ]
        k.wait_all(sync, spill_ev)


def emit_stageH(nc, k, g, ident, DBG):
    sync, gp, pe, dv, ac = nc.sync, nc.gpsimd, nc.tensor, nc.vector, nc.scalar
    with ExitStack() as esH:
        sb = lambda name, shape, dtype=F32: esH.enter_context(nc.sbuf_tensor(name, list(shape), dtype))
        ps = lambda name, shape, dtype=F32: esH.enter_context(nc.psum_tensor(name, list(shape), dtype))
        xh = sb("xhH", (128, 8 * SPAN), BF16)
        xhv = _ftv(xh, SPAN)
        acc = sb("accH", (128, 8 * SPAN), F32)
        accv = _ftv(acc, SPAN)
        hT = sb("hTH", (128, 4 * SPAN), BF16)
        hv = _ftv(hT, SPAN)
        WUP, WGA, WDO = 0, 4096, 8192
        wbf = [sb(f"wbfH{i}", (128, 12288), BF16) for i in range(2)]
        stg = sb("stgH", (128, 4 * 1024), F32)
        cbE = sb("cbEH", (128, SPAN), BF16)
        cmbT = sb("cmbTH", (16, SPAN), BF16)
        routbf = sb("routbfH", (128, 8 * 16), BF16)
        rstg = sb("rstgH", (128, 8 * 16), F32)
        rbias = sb("rbiasH", (128, 16), F32)
        zrow = sb("zrowH", (128, 5 * 16), F32)
        mrow = sb("mrowH", (128, 8), F32)
        cmbb = sb("cmbbH", (128, 16), BF16)
        sgb = [sb(f"sgbH{i}", (128, 512), BF16) for i in range(2)]
        sgc = [sb(f"sgcH{i}", (128, 512), BF16) for i in range(2)]
        sel_e = sb("seleH", (16, 16 * 128), BF16)
        gbf = sb("gbfH", (128, SPAN), BF16)
        psG = [ps(f"psGH{i}", (128, 512)) for i in range(2)]
        psU = [ps(f"psUH{i}", (128, 512)) for i in range(2)]
        psD = [ps(f"psDH{i}", (128, 512)) for i in range(2)]

        routv = _ftv(routbf, 16)
        rstgv = _ftv(rstg, 16)

        xn1_d = g["xn1sv"].rearrange("(a p) n -> a p n", p=128)
        ld = []
        for ft in range(8):
            ld.append(k.dma(sync.dma_start(out=xhv[ft], in_=xn1_d[ft][:, SOFF:SOFF + SPAN]), "DL"))
        rout_d = g["routT"].rearrange("(a p) n -> a p n", p=128)
        for ft in range(8):
            ld.append(k.dma(sync.dma_start(out=rstgv[ft], in_=rout_d[ft]), "DL"))
        ld.append(k.dma(sync.dma_start(out=rbias[:, :], in_=g["rbias"]), "DL"))
        ld.append(k.dma(sync.dma_start(out=sel_e[:, :], in_=g["sele"]), "DL"))
        k.wait_all(ac, ld)
        cv = k.fin(ac.copy(routbf[:, :], rstg[:, :]), "AC")

        esR2 = ExitStack()
        psZ = esR2.enter_context(nc.psum_tensor("psZH", [128, 16], F32))
        psT2 = esR2.enter_context(nc.psum_tensor("psT2H", [16, 128], BF16))
        Z0, I1, Z2, I2, CM = 0, 16, 32, 48, 64
        e_ev = None
        e_tr = None
        k.wait_all(pe, [cv])
        for tc in range(16):
            k.wait_all(pe, [e_ev])
            for ft in range(8):
                mm = pe.matmul(psZ[:, :], xhv[ft][:, 128 * tc:128 * tc + 128],
                               routv[ft], start=(ft == 0), stop=(ft == 7))
            e_mm = k.fin(mm, "PE")
            k.wait(dv, e_mm)
            e_ev = k.fin(dv.tensor_tensor(zrow[:, Z0:Z0 + 16], psZ[:, :], rbias[:, :],
                                          ALU.add), "DV")
            k.wait(dv, e_ev)
            e = k.fin(dv.reduce_max(mrow[:, 0:1], zrow[:, Z0:Z0 + 16], axis=AXL.X), "DV")
            k.wait(dv, e)
            e = k.fin(dv.tensor_scalar(zrow[:, I1:I1 + 16], zrow[:, Z0:Z0 + 16],
                                       mrow[:, 0:1], None, ALU.is_ge), "DV")
            k.wait(dv, e)
            e = k.fin(dv.scalar_tensor_tensor(zrow[:, Z2:Z2 + 16], zrow[:, I1:I1 + 16],
                                              -1e30, zrow[:, Z0:Z0 + 16],
                                              ALU.mult, ALU.add), "DV")
            k.wait(dv, e)
            e = k.fin(dv.reduce_max(mrow[:, 1:2], zrow[:, Z2:Z2 + 16], axis=AXL.X), "DV")
            k.wait(dv, e)
            e = k.fin(dv.tensor_scalar(zrow[:, I2:I2 + 16], zrow[:, Z2:Z2 + 16],
                                       mrow[:, 1:2], None, ALU.is_ge), "DV")
            k.wait(dv, e)
            e = k.fin(dv.tensor_tensor(mrow[:, 2:3], mrow[:, 0:1], mrow[:, 1:2],
                                       ALU.subtract), "DV")
            k.wait(dv, e)
            e = k.fin(dv.tensor_scalar_mul(mrow[:, 3:4], mrow[:, 2:3], -1.0), "DV")
            k.wait(ac, e)
            ac.activation(mrow[:, 4:5], mrow[:, 2:3], AF.Sigmoid)
            e_p = k.fin(ac.activation(mrow[:, 5:6], mrow[:, 3:4], AF.Sigmoid), "AC")
            k.wait(dv, e_p)
            e = k.fin(dv.tensor_scalar(zrow[:, CM:CM + 16], zrow[:, I1:I1 + 16],
                                       mrow[:, 4:5], None, ALU.mult), "DV")
            k.wait(dv, e)
            e = k.fin(dv.scalar_tensor_tensor(zrow[:, CM:CM + 16], zrow[:, I2:I2 + 16],
                                              mrow[:, 5:6], zrow[:, CM:CM + 16],
                                              ALU.mult, ALU.add), "DV")
            k.wait(ac, e)
            e = k.fin(ac.copy(cmbb[:, :], zrow[:, CM:CM + 16]), "AC")
            k.wait_all(pe, [e, e_tr])
            e_mt = k.fin(pe.transpose(psT2[:, :], cmbb[:, :], ident[:, :]), "PE")
            k.wait(ac, e_mt)
            e_tr = k.fin(ac.copy(cmbT[0:16, 128 * tc:128 * tc + 128], psT2[0:16, :]), "AC")
        cmb_done = k.now("AC")
        k.barrier([pe, dv, ac])
        esR2.close()
        psC = ps("psCH", (128, 512))

        eup_r = g["eupT"].rearrange("e (a p) n -> (e a) p n", p=128)
        ega_r = g["egaT"].rearrange("e (a p) n -> (e a) p n", p=128)
        edo_r = g["edoT"].rearrange("e (a p) n -> (e a) p n", p=128)
        shu_r = g["shuT"].rearrange("(a p) n -> a p n", p=128)
        shg_r = g["shgT"].rearrange("(a p) n -> a p n", p=128)
        shd_r = g["shdT"].rearrange("(a p) n -> a p n", p=128)

        def wtiles(e):
            out = []
            if e < 16:
                for ft in range(8):
                    out.append((WUP + 512 * ft, 512, eup_r[8 * e + ft]))
                    out.append((WGA + 512 * ft, 512, ega_r[8 * e + ft]))
                for i in range(4):
                    out.append((WDO + 1024 * i, 1024, edo_r[4 * e + i]))
            else:
                hh = e - 16
                for ft in range(8):
                    out.append((WUP + 512 * ft, 512, shu_r[ft][:, 512 * hh:512 * hh + 512]))
                    out.append((WGA + 512 * ft, 512, shg_r[ft][:, 512 * hh:512 * hh + 512]))
                for i in range(4):
                    out.append((WDO + 1024 * i, 1024, shd_r[4 * hh + i]))
            return out

        stg_free = [None] * 4

        def make_loader(e, pe_free):
            tiles = wtiles(e)
            dst = wbf[e % 2]
            state = {"i": 0, "evs": []}

            def step():
                i = state["i"]
                if i >= len(tiles):
                    return
                c0, w, src = tiles[i]
                s = i % 4
                k.wait_all(sync, [stg_free[s]])
                e_d = k.dma(sync.dma_start(out=stg[:, 1024 * s:1024 * s + w], in_=src), "DL")
                k.wait_all(ac, [e_d] + ([pe_free] if i == 0 else []))
                ev = k.fin(ac.copy(dst[:, c0:c0 + w], stg[:, 1024 * s:1024 * s + w]), "AC")
                stg_free[s] = ev
                state["evs"].append(ev)
                state["i"] = i + 1

            return step, state

        step0, st0 = make_loader(0, None)
        for _ in range(20):
            step0()
        pe_done = [None, None]
        ld_state = st0
        sg_ev = [None, None]
        sgfree = [None, None]
        h_ev = [None, None]
        e_cb = None
        psD_ev = [None, None]

        for e in range(18):
            wv = wbf[e % 2]
            wup = [wv[:, WUP + 512 * ft:WUP + 512 * ft + 512] for ft in range(8)]
            wga = [wv[:, WGA + 512 * ft:WGA + 512 * ft + 512] for ft in range(8)]
            wdo = [wv[:, WDO + 1024 * i:WDO + 1024 * i + 1024] for i in range(4)]
            ld_evs = list(ld_state["evs"])
            if e < 17:
                nstep, ld_state = make_loader(e + 1, pe_done[(e + 1) % 2])
            else:
                nstep = lambda: None
            if e < 16:
                k.wait_all(pe, [cmb_done, e_cb])
                for hf in range(4):
                    k.wait_all(pe, [e_cb])
                    e_mm = k.fin(pe.matmul(psC[:, :], sel_e[:, 128 * e:128 * e + 128],
                                           cmbT[0:16, 512 * hf:512 * hf + 512],
                                           start=True, stop=True), "PE")
                    k.wait_all(ac, [e_mm])
                    e_cb = k.fin(ac.copy(cbE[:, 512 * hf:512 * hf + 512], psC[:, :]), "AC")
            for hid in range(4):
                for tq in range(4):
                    off = 512 * tq
                    sl = (4 * hid + tq) % 2
                    k.wait_all(pe, ld_evs + [sg_ev[sl]])
                    for ft in range(8):
                        mm = pe.matmul(psG[sl][:, :], wga[ft][:, 128 * hid:128 * hid + 128],
                                       xhv[ft][:, off:off + 512],
                                       start=(ft == 0), stop=(ft == 7))
                    e_g = k.fin(mm, "PE")
                    k.wait_all(ac, [e_g, sgfree[sl]])
                    sg_ev[sl] = k.fin(ac.activation(sgb[sl][:, :], psG[sl][:, :],
                                                    AF.Silu), "AC")
                    k.wait_all(pe, [h_ev[sl]])
                    for ft in range(8):
                        mm = pe.matmul(psU[sl][:, :], wup[ft][:, 128 * hid:128 * hid + 128],
                                       xhv[ft][:, off:off + 512],
                                       start=(ft == 0), stop=(ft == 7))
                    e_u = k.fin(mm, "PE")
                    if e < 16:
                        k.wait_all(dv, [sg_ev[sl], e_cb])
                        e_sc = k.fin(dv.tensor_tensor(sgc[sl][:, :], sgb[sl][:, :],
                                                      cbE[:, off:off + 512], ALU.mult), "DV")
                        sct = sgc[sl]
                    else:
                        k.wait_all(dv, [sg_ev[sl]])
                        e_sc = None
                        sct = sgb[sl]
                    k.wait_all(dv, [e_u, e_sc])
                    h_ev[sl] = k.fin(dv.tensor_tensor(hv[hid][:, off:off + 512],
                                                      psU[sl][:, :], sct[:, :],
                                                      ALU.mult), "DV")
                    sgfree[sl] = h_ev[sl]
                    nstep()
            k.wait_all(pe, [k.now("DV")])
            for tq in range(4):
                off = 512 * tq
                for do2 in range(8):
                    sl = do2 % 2
                    k.wait_all(pe, [psD_ev[sl]])
                    for hid in range(4):
                        mm = pe.matmul(psD[sl][:, :], wdo[hid][:, 128 * do2:128 * do2 + 128],
                                       hv[hid][:, off:off + 512],
                                       start=(hid == 0), stop=(hid == 3))
                    e_d = k.fin(mm, "PE")
                    if e == 0:
                        k.wait(ac, e_d)
                        psD_ev[sl] = k.fin(ac.copy(accv[do2][:, off:off + 512],
                                                   psD[sl][:, :]), "AC")
                    else:
                        k.wait(dv, e_d)
                        psD_ev[sl] = k.fin(dv.tensor_tensor(accv[do2][:, off:off + 512],
                                                            accv[do2][:, off:off + 512],
                                                            psD[sl][:, :], ALU.add), "DV")
                    if tq % 2 == 0:
                        nstep()
            pe_done[e % 2] = k.now("PE")
        g1_d = g["g1sp"].rearrange("(a p) n -> a p n", p=128)
        acc_done = [k.now("DV"), k.now("AC")]
        e_gd = None
        for do2 in range(8):
            k.wait_all(ac, acc_done + [e_gd])
            e_s = k.fin(ac.activation(gbf[:, :], accv[do2][:, :], AF.Silu), "AC")
            k.wait(sync, e_s)
            e_gd = k.dma(sync.dma_start(out=g1_d[do2], in_=gbf[:, :]), "DL")
        k.wait(sync, k.now("DL"))

def build_launch2():
    nc = bass.Bass("TRN2", target_bir_lowering=False)
    g = {}

    def din(name, shape, dtype=F32):
        g[name] = nc.dram_tensor(name, list(shape), dtype, kind="ExternalInput").ap()

    din("rlM", (3 * 128, 1024)); din("rlC", (3 * 128, 1024))
    din("uf2", (NPAIR * 128, NCH * 256), BF16)
    din("ft2", (NPAIR * 128, NCH * 128), BF16)
    din("pt2", (NPAIR * 128, NCH * 128), BF16)
    din("mp2", (NPAIR * 128, NCH * 128), BF16)
    din("st2", (NPAIR * 128, NCH * 128), BF16)
    din("qt2", (NPAIR * 128, NCH * 128), BF16)
    din("x1f", (D, SPAN))
    din("og2", (D, SPAN), BF16)
    din("g12", (D, SPAN), BF16)
    din("owT", (D, D))
    din("outb2", (128, 8)); din("nwcol2", (128, 1))
    din("sel2", (128, 2), BF16); din("sel2f", (2, 128))
    din("ident2", (128, 128), BF16)
    g["ysp"] = nc.dram_tensor("ysp", [D, SPAN], F32, kind="ExternalOutput").ap()

    k = Ctx(nc)
    with k.es, nc.Block():
        emit_launch2(nc, k, g)
    return nc


def emit_launch2(nc, k, g):
    sync, gp, pe, dv, ac = nc.sync, nc.gpsimd, nc.tensor, nc.vector, nc.scalar
    es = k.es
    sb = lambda name, shape, dtype=F32: es.enter_context(nc.sbuf_tensor(name, list(shape), dtype))
    ps = lambda name, shape, dtype=F32: es.enter_context(nc.psum_tensor(name, list(shape), dtype))

    ident = sb("identL2", (128, 128), BF16)
    sel2 = sb("sel2L2", (128, 2), BF16)
    sel2f = sb("sel2fL2", (2, 128), F32)
    nwc = sb("nwcL2", (128, 1), F32)
    outb = sb("outbL2", (128, 8), F32)
    rdF = sb("rdFL2", (128, 8 * SPAN), BF16)
    rdFv = _ftv(rdF, SPAN)
    Sst = sb("SstL2", (128, 8 * 128), BF16)
    Sstv = _ftv(Sst, 128)
    owb = sb("owbL2", (128, 8 * 1024), BF16)
    owbv = _ftv(owb, 1024)

    ld0 = []
    ld0.append(k.dma(sync.dma_start(out=ident[:, :], in_=g["ident2"]), "DL"))
    ld0.append(k.dma(sync.dma_start(out=sel2[:, :], in_=g["sel2"]), "DL"))
    ld0.append(k.dma(sync.dma_start(out=sel2f[:, :], in_=g["sel2f"]), "DL"))
    ld0.append(k.dma(sync.dma_start(out=nwc[:, :], in_=g["nwcol2"]), "DL"))
    ld0.append(k.dma(sync.dma_start(out=outb[:, :], in_=g["outb2"]), "DL"))

    with ExitStack() as esR:
        rlm = esR.enter_context(nc.sbuf_tensor("rlmL2", [128, 3 * 1024], F32))
        rlc = esR.enter_context(nc.sbuf_tensor("rlcL2", [128, 3 * 1024], F32))
        rlmb = esR.enter_context(nc.sbuf_tensor("rlmbL2", [128, 3 * 1024], BF16))
        rlcb = esR.enter_context(nc.sbuf_tensor("rlcbL2", [128, 3 * 1024], BF16))
        psRel = esR.enter_context(nc.psum_tensor("psRelL2", [128, 128], F32))
        rlM_d = g["rlM"].rearrange("(j p) n -> j p n", p=128)
        rlC_d = g["rlC"].rearrange("(j p) n -> j p n", p=128)
        rlmv = _ftv(rlm, 1024)
        rlcv = _ftv(rlc, 1024)
        rlmbv = _ftv(rlmb, 1024)
        rlcbv = _ftv(rlcb, 1024)
        ldr = []
        for j in range(3):
            ldr.append(k.dma(sync.dma_start(out=rlmv[j], in_=rlM_d[j]), "DL"))
            ldr.append(k.dma(sync.dma_start(out=rlcv[j], in_=rlC_d[j]), "DL"))
        k.wait_all(ac, ld0 + ldr)
        ac.copy(rlmb[:, :], rlm[:, :])
        cvr = k.fin(ac.copy(rlcb[:, :], rlc[:, :]), "AC")
        k.wait(dv, cvr)
        e_prev = None
        for p in range(8):
            e_s = k.fin(dv.tensor_copy(Sstv[p], rlcbv[0][:, 128 * p:128 * p + 128]), "DV")
            for j in (1, 2):
                k.wait_all(pe, [e_s, e_prev])
                pe.matmul(psRel[:, :], rlmbv[j][:, 128 * p:128 * p + 128], Sstv[p],
                          start=True, stop=False)
                e_mm = k.fin(pe.matmul(psRel[:, :], ident[:, :],
                                       rlcbv[j][:, 128 * p:128 * p + 128],
                                       start=False, stop=True), "PE")
                k.wait(ac, e_mm)
                e_s = e_prev = k.fin(ac.copy(Sstv[p], psRel[:, :]), "AC")
        relay_done = [k.now("AC"), k.now("DV")]
    k.barrier([sync, gp, pe, dv, ac])

    psB2 = es.enter_context(nc.psum_tensor("psB2L2", [128, 512], F32))
    ow_d = g["owT"].rearrange("(a p) n -> a p n", p=128)
    with ExitStack() as esM:
        sbM = lambda name, shape, dtype=F32: esM.enter_context(nc.sbuf_tensor(name, list(shape), dtype))
        psM = lambda name, shape, dtype=F32: esM.enter_context(nc.psum_tensor(name, list(shape), dtype))
        owstg = sbM("owstgL2", (128, 2 * 1024), F32)
        stg_ev = [None, None]
        for a in range(8):
            s = a % 2
            k.wait_all(sync, [stg_ev[s]])
            e_d = k.dma(sync.dma_start(out=owstg[:, 1024 * s:1024 * s + 1024], in_=ow_d[a]), "DL")
            k.wait(ac, e_d)
            stg_ev[s] = k.fin(ac.copy(owbv[a], owstg[:, 1024 * s:1024 * s + 1024]), "AC")
        ow_done = k.now("AC")

        uf_d = g["uf2"].rearrange("(a p) (c u) -> a p c u", p=128, u=256)
        ft_d = g["ft2"].rearrange("(a p) n -> a p n", p=128)
        pt_d = g["pt2"].rearrange("(a p) n -> a p n", p=128)
        mp_d = g["mp2"].rearrange("(a p) n -> a p n", p=128)
        st_d = g["st2"].rearrange("(a p) n -> a p n", p=128)
        qt_d = g["qt2"].rearrange("(a p) n -> a p n", p=128)
        NC8 = NCH * 128
        spl = [sbM(f"splL2{i}", (128, 6 * NC8), BF16) for i in range(2)]
        UFo, FTo, PTo, MPo, STo, QTo = (i * NC8 for i in range(6))
        TST = sbM("TSTL2", (128, NC8), BF16)
        WmB = sbM("WmBL2", (128, NC8), BF16)
        rdT = sbM("rdTL2", (128, NC8), BF16)
        sqt = sbM("sqtL2", (128, SPAN), BF16)
        rs = sbM("rsL2", (2, SPAN), F32)
        rs2 = sbM("rs2L2", (2, SPAN), F32)
        rsbc = sbM("rsbcL2", (128, SPAN), BF16)
        psA = psM("psAL2", (128, 512))
        psB = psM("psBL2", (128, 512))
        psR = psM("psRL2", (128, 512))
        psT = psM("psTL2", (128, 512), BF16)
        psS = psM("psSL2", (2, 512))

        def load_pair(p, s, free_ev):
            evs = []
            dst = spl[s]
            k.wait_all(sync, [free_ev])
            evs.append(k.dma(sync.dma_start(
                out=dst[:, UFo:UFo + NC8].rearrange("p (c u) -> p c u", u=128),
                in_=uf_d[p][:, :, 0:128]), "DL"))
            for off, src in [(FTo, ft_d), (PTo, pt_d), (MPo, mp_d),
                             (STo, st_d), (QTo, qt_d)]:
                evs.append(k.dma(sync.dma_start(out=dst[:, off:off + NC8], in_=src[p]), "DL"))
            return evs

        pair_free = [None, None]
        rms_pe = None
        prev_nm = None
        ld_ev = load_pair(0, 0, None)
        k.wait_all(pe, relay_done + [ow_done])
        for p in range(8):
            s = p % 2
            w = spl[s]
            if p < 7:
                nxt_ev = load_pair(p + 1, 1 - s, pair_free[1 - s])
            eA = eB = eR = eT = None
            k.wait_all(pe, ld_ev)
            for cg in range(8):
                k.wait_all(pe, [eA])
                for j in range(4):
                    c = 4 * cg + j
                    pe.matmul(psA[:, 128 * j:128 * j + 128],
                              w[:, MPo + 128 * c:MPo + 128 * c + 128], Sstv[p],
                              start=True, stop=False)
                    mm = pe.matmul(psA[:, 128 * j:128 * j + 128], ident[:, :],
                                   w[:, STo + 128 * c:STo + 128 * c + 128],
                                   start=False, stop=True)
                e = k.fin(mm, "PE")
                k.wait(ac, e)
                eA = k.fin(ac.copy(TST[:, 512 * cg:512 * cg + 512], psA[:, :]), "AC")
                k.wait_all(pe, [eB, eA])
                for j in range(4):
                    c = 4 * cg + j
                    pe.matmul(psB[:, 128 * j:128 * j + 128], ident[:, :],
                              w[:, UFo + 128 * c:UFo + 128 * c + 128],
                              start=True, stop=False)
                    mm = pe.matmul(psB[:, 128 * j:128 * j + 128],
                                   w[:, FTo + 128 * c:FTo + 128 * c + 128],
                                   TST[:, 128 * c:128 * c + 128],
                                   start=False, stop=True)
                e = k.fin(mm, "PE")
                k.wait(ac, e)
                eB = k.fin(ac.copy(WmB[:, 512 * cg:512 * cg + 512], psB[:, :]), "AC")
                k.wait_all(pe, [eR, eB])
                for j in range(4):
                    c = 4 * cg + j
                    pe.matmul(psR[:, 128 * j:128 * j + 128],
                              w[:, PTo + 128 * c:PTo + 128 * c + 128],
                              WmB[:, 128 * c:128 * c + 128],
                              start=True, stop=False)
                    mm = pe.matmul(psR[:, 128 * j:128 * j + 128],
                                   w[:, QTo + 128 * c:QTo + 128 * c + 128],
                                   TST[:, 128 * c:128 * c + 128],
                                   start=False, stop=True)
                e = k.fin(mm, "PE")
                k.wait(ac, e)
                eR = k.fin(ac.copy(rdT[:, 512 * cg:512 * cg + 512], psR[:, :]), "AC")
                k.wait_all(pe, [eT, eR])
                for j in range(4):
                    c = 4 * cg + j
                    mm = pe.transpose(psT[:, 128 * j:128 * j + 128],
                                      rdT[:, 128 * c:128 * c + 128], ident[:, :])
                e = k.fin(mm, "PE")
                k.wait(dv, e)
                pv = psT[:, :].rearrange("p (c u) -> p c u", u=128)
                rv = rdFv[p].rearrange("p (g c u) -> p g c u", g=8, u=64)
                dv.tensor_copy(rv[0:64, cg, :, :], pv[0:64, :, 0:64])
                eT = k.fin(dv.tensor_copy(rv[64:128, cg, :, :], pv[64:128, :, 64:128]), "DV")
            pair_free[s] = k.now("PE")
            k.wait_all(ac, [eT, rms_pe])
            e_sq = k.fin(ac.activation(sqt[:, :], rdFv[p], AF.Square), "AC")
            eS = None
            for tq in range(4):
                off = 512 * tq
                k.wait_all(pe, [e_sq, eS])
                e_mm = k.fin(pe.matmul(psS[0:2, :], sel2[:, :], sqt[:, off:off + 512],
                                       start=True, stop=True), "PE")
                k.wait(dv, e_mm)
                eS = k.fin(dv.tensor_scalar(rs[0:2, off:off + 512], psS[0:2, :],
                                            1.0 / 64.0, 1e-6, ALU.mult, ALU.add), "DV")
            k.wait(ac, eS)
            eS = k.fin(ac.activation(rs2[0:2, :], rs[0:2, :], AF.Sqrt), "AC")
            k.wait(dv, eS)
            eS = k.fin(dv.reciprocal(rs[0:2, :], rs2[0:2, :]), "DV")
            eB2 = None
            for tq in range(4):
                off = 512 * tq
                k.wait_all(pe, [eS, eB2])
                e_mm = k.fin(pe.matmul(psB2[:, :], sel2f[:, :], rs[0:2, off:off + 512],
                                       start=True, stop=True), "PE")
                k.wait_all(ac, [e_mm, prev_nm])
                eB2 = k.fin(ac.activation(rsbc[:, off:off + 512], psB2[:, :],
                                          AF.Copy, scale=nwc[:, 0:1]), "AC")
            rms_pe = k.now("PE")
            k.wait_all(dv, [eB2])
            prev_nm = e_nm = k.fin(dv.tensor_tensor(rdFv[p], rdFv[p], rsbc[:, :],
                                                    ALU.mult), "DV")
            k.wait(pe, e_nm)
            ld_ev = nxt_ev if p < 7 else []
        scan_done = [k.now("DV"), k.now("AC")]
    k.barrier([sync, gp, pe, dv, ac])

    with ExitStack() as esO:
        sbO = lambda name, shape, dtype=F32: esO.enter_context(nc.sbuf_tensor(name, list(shape), dtype))
        x1_d = g["x1f"].rearrange("(a p) n -> a p n", p=128)
        og_d = g["og2"].rearrange("(a p) n -> a p n", p=128)
        g1_d = g["g12"].rearrange("(a p) n -> a p n", p=128)
        y_d = g["ysp"].rearrange("(a p) n -> a p n", p=128)
        x1t = [sbO(f"x1tL2{i}", (128, SPAN), F32) for i in range(2)]
        ogt = [sbO(f"ogtL2{i}", (128, SPAN), BF16) for i in range(2)]
        g1t = [sbO(f"g1tL2{i}", (128, SPAN), BF16) for i in range(2)]
        prj = sbO("prjL2", (128, 512), F32)
        yb = sbO("ybL2", (128, SPAN), F32)
        dc_free = [None, None]
        ldx = {}

        def load_dc(dc, free_evs):
            s = dc % 2
            evs = []
            k.wait_all(sync, free_evs if free_evs else [])
            evs.append(k.dma(sync.dma_start(out=x1t[s][:, :], in_=x1_d[dc]), "DL"))
            evs.append(k.dma(sync.dma_start(out=ogt[s][:, :], in_=og_d[dc]), "DL"))
            evs.append(k.dma(sync.dma_start(out=g1t[s][:, :], in_=g1_d[dc]), "DL"))
            return evs

        ldx[0] = load_dc(0, None)
        e_pr = None
        e_y = None
        y_dma = None
        for dc in range(8):
            s = dc % 2
            if dc < 7:
                ldx[dc + 1] = load_dc(dc + 1, [dc_free[1 - s]])
            k.wait_all(pe, scan_done)
            for tq in range(4):
                off = 512 * tq
                k.wait_all(pe, [e_pr] + ldx[dc])
                for p in range(8):
                    mm = pe.matmul(psB2[:, :], owbv[p][:, 128 * dc:128 * dc + 128],
                                   rdFv[p][:, off:off + 512],
                                   start=(p == 0), stop=(p == 7))
                e_mm = k.fin(mm, "PE")
                k.wait_all(dv, [e_mm, y_dma])
                e_pr = k.fin(dv.scalar_tensor_tensor(prj[:, :], psB2[:, :],
                                                     outb[:, dc:dc + 1],
                                                     ogt[s][:, off:off + 512],
                                                     ALU.add, ALU.mult), "DV")
                dv.tensor_tensor(prj[:, :], prj[:, :], g1t[s][:, off:off + 512], ALU.mult)
                e_y = k.fin(dv.tensor_tensor(yb[:, off:off + 512], prj[:, :],
                                             x1t[s][:, off:off + 512], ALU.add), "DV")
            dc_free[s] = e_y
            k.wait(sync, e_y)
            y_dma = k.dma(sync.dma_start(out=y_d[dc], in_=yb[:, :]), "DL")
        k.wait(sync, k.now("DL"))


def host_inputs2(inputs, res1):
    ident = np.eye(128, dtype=BF)
    identf = np.eye(128, dtype=np.float32)
    sel2 = np.zeros((128, 2), dtype=BF)
    sel2[0:64, 0] = 1
    sel2[64:128, 1] = 1
    sel2f = np.zeros((2, 128), np.float32)
    sel2f[0, 0:64] = 1
    sel2f[1, 64:128] = 1
    nw = np.asarray(inputs["L1_attn_norm_w"]).astype(np.float32)
    nwcol = np.tile(nw, 2).reshape(128, 1)
    outb = np.ascontiguousarray(
        np.asarray(inputs["L1_attn_out_b"]).astype(np.float32).reshape(8, 128).T)
    owT = np.ascontiguousarray(np.asarray(inputs["L1_attn_out_w"], np.float32).T)
    in_maps = []
    for c in range(NCORES):
        b, q = divmod(c, 4)
        rlM = np.zeros((3, 128, 1024), np.float32)
        rlC = np.zeros((3, 128, 1024), np.float32)
        for j in range(3):
            if j < q:
                pred = 4 * b + j
                mc = np.asarray(res1.results[pred]["mcsp"], np.float32)
                cc = np.asarray(res1.results[pred]["ccsp"], np.float32)
                rlM[j] = mc.transpose(1, 0, 2).reshape(128, 1024)
                rlC[j] = cc.transpose(1, 0, 2).reshape(128, 1024)
            else:
                rlM[j] = np.tile(identf, (1, 8))
        r = res1.results[c]
        m = {
            "rlM": rlM.reshape(384, 1024), "rlC": rlC.reshape(384, 1024),
            "uf2": np.asarray(r["ufsp"]).reshape(NPAIR * 128, NCH * 256),
            "ft2": np.asarray(r["ftsp"]).reshape(NPAIR * 128, NCH * 128),
            "pt2": np.asarray(r["ptsp"]).reshape(NPAIR * 128, NCH * 128),
            "mp2": np.asarray(r["mpsp"]).reshape(NPAIR * 128, NCH * 128),
            "st2": np.asarray(r["stsp"]).reshape(NPAIR * 128, NCH * 128),
            "qt2": np.asarray(r["qtsp"]).reshape(NPAIR * 128, NCH * 128),
            "x1f": np.ascontiguousarray(np.asarray(r["x1sp"], np.float32)[:, 128:128 + SPAN]),
            "og2": np.asarray(r["ogsp"]),
            "g12": np.asarray(r["g1sp"]),
            "owT": owT, "outb2": outb, "nwcol2": nwcol,
            "sel2": sel2, "sel2f": sel2f, "ident2": ident,
        }
        in_maps.append(m)
    return in_maps


def kernel(**inputs):
    in_maps = host_inputs(inputs)
    nc1 = build_launch1(stages="ABCDEFGH")
    res1 = run_bass_kernel_spmd(nc1, in_maps, core_ids=list(range(NCORES)))
    in_maps2 = host_inputs2(inputs, res1)
    nc2 = build_launch2()
    res2 = run_bass_kernel_spmd(nc2, in_maps2, core_ids=list(range(NCORES)))
    out = np.empty((B, T, D), np.float32)
    for c in range(NCORES):
        b, q = divmod(c, 4)
        out[b, SPAN * q:SPAN * (q + 1), :] = np.asarray(res2.results[c]["ysp"], np.float32).T
    return out


# version 07
# speedup vs baseline: 1.0773x; 1.0176x over previous
import math
from contextlib import ExitStack

import numpy as np
import ml_dtypes
import concourse.bass as bass
import concourse.mybir as mybir
from concourse.bass_utils import run_bass_kernel_spmd

F32 = mybir.dt.float32
BF16 = mybir.dt.bfloat16
I32 = mybir.dt.int32
AF = mybir.ActivationFunctionType
ALU = mybir.AluOpType
AXL = mybir.AxisListType
BF = ml_dtypes.bfloat16

B, T, D = 2, 8192, 1024
NCORES = 8
SPAN = 2048
HALO = 384
XL = SPAN + HALO
L0S = 256
L0N = XL - L0S
NQT = L0N // 128
H0, HD0 = 8, 64
SCALE0 = 1.0 / 8.0
FFNH = 3072
L1S = 320
L1N = XL - L1S
SOFF = HALO - L1S
H1, HD1 = 16, 64
CH = 64
NCH = SPAN // CH
NPAIR = H1 // 2
NEXP, EDIM = 16, 512
SHH = 1024


def _chunks(n, step=512):
    return [(i, min(step, n - i)) for i in range(0, n, step)]


class Ctx:

    def __init__(self, nc):
        self.nc = nc
        self.es = ExitStack()
        self.cnt = {}
        self.sem = {}
        for name in ["DL", "DG", "PE", "DV", "AC", "PO"]:
            self.sem[name] = self.es.enter_context(nc.semaphore(name))
            self.cnt[name] = 0

    def dma(self, inst, s):
        self.cnt[s] += 16
        inst.then_inc(self.sem[s], 16)
        return (s, self.cnt[s])

    def fin(self, inst, s):
        self.cnt[s] += 1
        inst.then_inc(self.sem[s], 1)
        return (s, self.cnt[s])

    def wait(self, eng, ev):
        if ev is not None:
            eng.wait_ge(self.sem[ev[0]], ev[1])

    def wait_all(self, eng, evs):
        seen = {}
        for ev in evs:
            if ev is None:
                continue
            s, v = ev
            seen[s] = max(seen.get(s, 0), v)
        for s, v in seen.items():
            eng.wait_ge(self.sem[s], v)

    def now(self, s):
        return (s, self.cnt[s])

    def barrier(self, engines):
        evs = [self.now(s) for s in ["PE", "DV", "AC", "DL", "DG", "PO"]]
        for eng in engines:
            self.wait_all(eng, evs)


def mt(view):
    return view


def _rope_tables(c):
    start = SPAN * (c % 4)
    pos = np.arange(start - HALO, start - HALO + XL, dtype=np.float64)
    j = np.arange(32, dtype=np.float64)
    inv = 10000.0 ** (-2.0 * j / 64.0)
    ang = pos[None, :] * inv[:, None]
    cos = np.cos(ang).astype(np.float32)
    sin = np.sin(ang).astype(np.float32)
    cc = np.empty((128, XL), np.float32)
    ss = np.empty((128, XL), np.float32)
    for hh in range(2):
        r = 64 * hh
        cc[r:r + 32] = cos
        cc[r + 32:r + 64] = cos
        ss[r:r + 32] = -sin
        ss[r + 32:r + 64] = sin
    return cc, ss


def host_inputs(inputs):
    x = np.asarray(inputs["L0_attn_x"])
    i = np.arange(128)
    sl = (i[:, None] > i[None, :]).astype(BF)
    ui = (i[:, None] <= i[None, :]).astype(BF)
    ident = np.eye(128, dtype=BF)
    onesrow = np.ones((1, 128), dtype=BF)
    onescol = np.ones((128, 1), dtype=BF)
    selh = np.zeros((16, 8 * 128), dtype=BF)
    for ft in range(8):
        selh[2 * ft, ft * 128 + 0:ft * 128 + 64] = 1
        selh[2 * ft + 1, ft * 128 + 64:ft * 128 + 128] = 1
    sele = np.zeros((16, 16 * 128), dtype=BF)
    for e in range(16):
        sele[e, 128 * e:128 * (e + 1)] = 1
    j64 = np.arange(64)
    su64 = (j64[:, None] < j64[None, :]).astype(np.float32)
    sl64 = (j64[:, None] > j64[None, :]).astype(np.float32)
    ui64 = (j64[:, None] <= j64[None, :]).astype(np.float32)
    bd = np.zeros((128, 128), np.float32)
    bd[0:64, 0:64] = 1
    bd[64:128, 64:128] = 1

    def bdm(m64):
        out = np.zeros((128, 128), np.float32)
        out[0:64, 0:64] = m64
        out[64:128, 64:128] = m64
        return out

    selh2 = np.zeros((128, 8 * 16), dtype=BF)
    for ft in range(8):
        selh2[0:64, 16 * ft + 2 * ft] = 1
        selh2[64:128, 16 * ft + 2 * ft + 1] = 1
    selhf_ = np.zeros((16, 8 * 128), dtype=np.float32)
    for ft in range(8):
        selhf_[2 * ft, ft * 128 + 0:ft * 128 + 64] = 1
        selhf_[2 * ft + 1, ft * 128 + 64:ft * 128 + 128] = 1
    ctrl_w_ = np.asarray(inputs["L1_attn_ctrl_w"], np.float32)
    ctrl_b_ = np.asarray(inputs["L1_attn_ctrl_b"], np.float32)
    ctrlP = np.zeros((1152, 1024), np.float32)
    ctrlP[0:16] = ctrl_w_[0:16]
    ctrlP[32:48] = ctrl_w_[16:32]
    ctrlP[64:1088] = ctrl_w_[32:1056]
    ctrlTP = np.ascontiguousarray(ctrlP.T)
    cb_ = np.zeros(1152, np.float32)
    cb_[0:16] = ctrl_b_[0:16]
    cb_[32:48] = ctrl_b_[16:32]
    cb_[64:1088] = ctrl_b_[32:1056]
    ctrlb2v = np.ascontiguousarray(cb_.reshape(9, 128).T)
    mslbdn_ = (-bdm(sl64)).astype(BF)
    msubdn_ = (-bdm(su64)).astype(BF)
    muibd_ = bdm(ui64).astype(BF)
    nw = np.asarray(inputs["L1_attn_norm_w"]).astype(np.float32)
    nwcol = np.tile(nw, 2).reshape(128, 1)
    outb = np.asarray(inputs["L1_attn_out_b"]).astype(np.float32).reshape(1024, 1)
    ctrlb = np.asarray(inputs["L1_attn_ctrl_b"]).astype(np.float32).reshape(1056, 1)
    rbias_row = np.broadcast_to(
        np.asarray(inputs["L1_ffn_router_bias"]).astype(np.float32), (128, 16)).copy()

    common = {
        "Wqkv": np.asarray(inputs["L0_attn_Wqkv"], np.float32),
        "Wout": np.asarray(inputs["L0_attn_Wout"], np.float32),
        "wgT": np.ascontiguousarray(np.asarray(inputs["L0_ffn_w_gate"]).T).astype(np.int32),
        "wuT": np.ascontiguousarray(np.asarray(inputs["L0_ffn_w_up"]).T).astype(np.int32),
        "wdT": np.ascontiguousarray(np.asarray(inputs["L0_ffn_w_down"]).T).astype(np.int32),
        "qkvT": np.ascontiguousarray(np.asarray(inputs["L1_attn_qkv_w"], np.float32).T),
        "ctrlT": ctrlTP,
        "convq": np.asarray(inputs["L1_attn_convq_w"], np.float32),
        "convk": np.asarray(inputs["L1_attn_convk_w"], np.float32),
        "convv": np.asarray(inputs["L1_attn_convv_w"], np.float32),
        "routT": np.ascontiguousarray(np.asarray(inputs["L1_ffn_router_w"], np.float32).T),
        "rbias": rbias_row,
        "eupT": np.asarray(inputs["L1_ffn_up_proj"], np.float32),
        "egaT": np.asarray(inputs["L1_ffn_gate_proj"], np.float32),
        "edoT": np.asarray(inputs["L1_ffn_down_proj"], np.float32),
        "shgT": np.ascontiguousarray(np.asarray(inputs["L1_ffn_sh_gate"], np.float32).T),
        "shuT": np.ascontiguousarray(np.asarray(inputs["L1_ffn_sh_up"], np.float32).T),
        "shdT": np.ascontiguousarray(np.asarray(inputs["L1_ffn_sh_down"], np.float32).T),
        "ctrlb": ctrlb,
        "maskSL": sl, "maskUI": ui, "ident": ident,
        "onesrow": onesrow, "onescol": onescol,
        "selh": selh, "sele": sele,
        "mslbdn": mslbdn_, "msubdn": msubdn_,
        "muibd": muibd_, "mbdn": (-bd).astype(BF),
        "mbd": bd.astype(BF),
        "nwcol": nwcol, "outb": outb,
        "selh2": selh2, "selhf": selhf_, "ctrlb2": ctrlb2v,
        "msl4": np.tile(mslbdn_, (1, 4)), "msu4": np.tile(msubdn_, (1, 4)),
        "mui4": np.tile(muibd_, (1, 4)), "mbd4": np.tile(bd.astype(BF), (1, 4)),
        "mbdn4": np.tile((-bd).astype(BF), (1, 4)),
        "mbd8": np.tile(bd.astype(BF), (1, 8)),
    }
    in_maps = []
    for c in range(NCORES):
        b, q = divmod(c, 4)
        start = SPAN * q
        lo = start - HALO
        xs = np.zeros((XL, D), np.float32)
        src_lo = max(lo, 0)
        xs[src_lo - lo:, :] = x[b, src_lo:start + SPAN, :]
        cc, ss = _rope_tables(c)
        validc = np.ones((128, 19), np.float32)
        mask3 = np.ones((128, 3), np.float32)
        if q == 0:
            validc[:, 0:3] = 0.0
            mask3[:] = 0.0
        m = dict(common)
        m.update({"xT": np.ascontiguousarray(xs.T), "ropeC": cc, "ropeS": ss,
                  "validc": validc, "mask3": mask3})
        in_maps.append(m)
    return in_maps


def build_launch1(dbg=(), stages="ABCDE", glim=(NPAIR, NCH // 4)):
    nc = bass.Bass("TRN2", target_bir_lowering=False)
    g = {}

    def din(name, shape, dtype=F32):
        g[name] = nc.dram_tensor(name, list(shape), dtype, kind="ExternalInput").ap()

    def dout(name, shape, dtype=F32):
        g[name] = nc.dram_tensor(name, list(shape), dtype, kind="ExternalOutput").ap()

    din("xT", (D, XL))
    din("Wqkv", (D, 1536)); din("Wout", (512, D))
    din("wgT", (D, FFNH), I32); din("wuT", (D, FFNH), I32); din("wdT", (FFNH, D), I32)
    din("qkvT", (D, 3 * D)); din("ctrlT", (D, 1152)); din("ctrlb", (1056, 1))
    din("convq", (D, 4)); din("convk", (D, 4)); din("convv", (D, 4))
    din("routT", (D, 16)); din("rbias", (128, 16))
    din("eupT", (NEXP, D, EDIM)); din("egaT", (NEXP, D, EDIM)); din("edoT", (NEXP, EDIM, D))
    din("shgT", (D, SHH)); din("shuT", (D, SHH)); din("shdT", (SHH, D))
    din("ropeC", (128, XL)); din("ropeS", (128, XL))
    din("maskSL", (128, 128), BF16); din("maskUI", (128, 128), BF16)
    din("ident", (128, 128), BF16)
    din("onesrow", (1, 128), BF16); din("onescol", (128, 1), BF16)
    din("selh", (16, 8 * 128), BF16); din("sele", (16, 16 * 128), BF16)
    din("mslbdn", (128, 128), BF16); din("msubdn", (128, 128), BF16)
    din("muibd", (128, 128), BF16); din("mbdn", (128, 128), BF16)
    din("mbd", (128, 128), BF16)
    din("validc", (128, 19)); din("mask3", (128, 3))
    din("nwcol", (128, 1)); din("outb", (D, 1))
    din("selh2", (128, 8 * 16), BF16); din("selhf", (16, 8 * 128))
    din("ctrlb2", (128, 9))
    din("msl4", (128, 512), BF16); din("msu4", (128, 512), BF16)
    din("mui4", (128, 512), BF16); din("mbd4", (128, 512), BF16)
    din("mbdn4", (128, 512), BF16); din("mbd8", (128, 1024), BF16)
    g["xn1sv"] = nc.dram_tensor("xn1sv", [D, L1N], BF16, kind="Internal").ap()
    g["kbsv"] = nc.dram_tensor("kbsv", [D, SPAN], BF16, kind="Internal").ap()

    dout("x1sp", (D, L0N))
    dout("g1sp", (D, SPAN), BF16)
    dout("qtsp", (NPAIR, 128, NCH * 128), BF16)
    dout("ogsp", (D, SPAN), BF16)
    dout("ufsp", (NPAIR, 128, NCH * 256), BF16)
    dout("ftsp", (NPAIR, 128, NCH * 128), BF16)
    dout("ktsp", (NPAIR, 128, NCH * 128), BF16)
    dout("ptsp", (NPAIR, 128, NCH * 128), BF16)
    dout("mpsp", (NPAIR, 128, NCH * 128), BF16)
    dout("stsp", (NPAIR, 128, NCH * 128), BF16)
    dout("mcsp", (NPAIR, 128, 128))
    dout("ccsp", (NPAIR, 128, 128))
    for name, shape, dt in dbg:
        dout(name, shape, dt)

    k = Ctx(nc)
    with k.es, nc.Block():
        emit_launch1(nc, k, g, dbg, stages, glim)
    return nc


def _ftv(t, n):
    v = t.ap().rearrange("p (a n) -> p a n", n=n)
    return [v[:, a2, :] for a2 in range(v.shape[1])]


def emit_launch1(nc, k, g, dbg=(), stages="ABCDE", glim=(NPAIR, NCH // 4)):
    sync, gp, pe, dv, ac = nc.sync, nc.gpsimd, nc.tensor, nc.vector, nc.scalar
    es = k.es
    sb = lambda name, shape, dtype=F32: es.enter_context(nc.sbuf_tensor(name, list(shape), dtype))
    psa = lambda name, shape, dtype=F32: es.enter_context(nc.psum_tensor(name, list(shape), dtype))
    DBG = {name for name, _, _ in dbg}

    ident = sb("identb", (128, 128), BF16)
    onesr = sb("onesrb", (1, 128), BF16)
    onesc = sb("onescb", (128, 1), BF16)
    onesrf = sb("onesrf", (1, 128), F32)
    onescf = sb("onescf", (128, 1), F32)
    mSL = sb("mSLb", (128, 128), BF16)
    mUI = sb("mUIb", (128, 128), BF16)
    validc = sb("validcb", (128, 19), F32)
    const_ev = []
    for t, name in [(ident, "ident"), (onesr, "onesrow"), (onesc, "onescol"),
                    (mSL, "maskSL"), (mUI, "maskUI"), (validc, "validc")]:
        const_ev.append(k.dma(sync.dma_start(out=t[:, :], in_=g[name]), "DL"))
    dv.memset(onesrf[:, :], 1.0)
    ev = k.fin(dv.memset(onescf[:, :], 1.0), "DV")
    const_ev.append(ev)

    es_xn = k.es.enter_context(ExitStack())
    xn = es_xn.enter_context(nc.sbuf_tensor("xnb", [128, 8 * XL], BF16, side="right"))
    xnv = _ftv(xn, XL)

    xTv = g["xT"].rearrange("(a p) n -> a p n", p=128)

    with ExitStack() as esA:
        sbA = lambda name, shape, dtype=F32: esA.enter_context(nc.sbuf_tensor(name, list(shape), dtype))
        psA = lambda name, shape, dtype=F32: esA.enter_context(nc.psum_tensor(name, list(shape), dtype))
        X = sbA("X", (128, 8 * XL), F32)
        Xv = _ftv(X, XL)
        x_ev = []
        for ft in range(8):
            x_ev.append(k.dma(sync.dma_start(out=Xv[ft], in_=xTv[ft]), "DL"))

        wrkQ = [sbA(f"wrkQ{i}", (128, 512), F32) for i in range(2)]
        pS = psA("pS", (1, 512))
        pQ = psA("pQ", (1, 512))
        srow = sbA("srow", (1, XL), F32)
        qrow = sbA("qrow", (1, XL), F32)
        q_ev = [None, None]
        ch = _chunks(XL)
        for ci, (c0, cn) in enumerate(ch):
            stat_done = []
            for ft in range(8):
                slot = ft % 2
                k.wait(ac, x_ev[ft])
                k.wait(ac, q_ev[slot])
                sq = k.fin(ac.activation(wrkQ[slot][:, 0:cn], Xv[ft][:, c0:c0 + cn],
                                         AF.Square), "AC")
                k.wait_all(pe, [sq] + ([const_ev[-1]] if ci == 0 and ft == 0 else []))
                pe.matmul(pS[0:1, 0:cn], onescf[:, :], Xv[ft][:, c0:c0 + cn],
                          start=(ft == 0), stop=(ft == 7))
                mm = pe.matmul(pQ[0:1, 0:cn], onescf[:, :], wrkQ[slot][:, 0:cn],
                               start=(ft == 0), stop=(ft == 7))
                q_ev[slot] = k.fin(mm, "PE")
            k.wait(dv, q_ev[(7) % 2])
            dv.tensor_copy(srow[0:1, c0:c0 + cn], pS[0:1, 0:cn])
            e = k.fin(dv.tensor_copy(qrow[0:1, c0:c0 + cn], pQ[0:1, 0:cn]), "DV")
            k.wait(pe, e)

        arow = sbA("arow", (1, XL), F32)
        brow = sbA("brow", (1, XL), F32)
        trow = sbA("trow", (1, XL), F32)
        dv.tensor_scalar_mul(srow[:, :], srow[:, :], 1.0 / D)
        dv.tensor_tensor(trow[:, :], srow[:, :], srow[:, :], ALU.mult)
        dv.scalar_tensor_tensor(trow[:, :], qrow[:, :], 1.0 / D, trow[:, :],
                                ALU.mult, ALU.subtract)
        ev = k.fin(dv.tensor_scalar_add(trow[:, :], trow[:, :], 1e-5), "DV")
        k.wait(ac, ev)
        ev = k.fin(ac.activation(trow[:, :], trow[:, :], AF.Sqrt), "AC")
        k.wait(dv, ev)
        dv.reciprocal(arow[:, :], trow[:, :])
        dv.tensor_tensor(brow[:, :], srow[:, :], arow[:, :], ALU.mult)
        ev = k.fin(dv.tensor_scalar_mul(brow[:, :], brow[:, :], -1.0), "DV")

        Abb = sbA("Abb", (128, XL), F32)
        Bbb = sbA("Bbb", (128, XL), F32)
        pbcA = psA("pbcA", (128, 512))
        pbcB = psA("pbcB", (128, 512))
        k.wait(pe, ev)
        bc_ev = []
        for (c0, cn) in _chunks(XL):
            e1 = k.fin(pe.matmul(pbcA[:, 0:cn], onesrf[0:1, :], arow[0:1, c0:c0 + cn],
                                 start=True, stop=True), "PE")
            e2 = k.fin(pe.matmul(pbcB[:, 0:cn], onesrf[0:1, :], brow[0:1, c0:c0 + cn],
                                 start=True, stop=True), "PE")
            k.wait(ac, e2)
            ac.copy(Abb[:, c0:c0 + cn], pbcA[:, 0:cn])
            e3 = k.fin(ac.copy(Bbb[:, c0:c0 + cn], pbcB[:, 0:cn]), "AC")
            bc_ev.append(e3)
            k.wait(pe, e3)
        wrk = sbA("wrkLN", (128, XL), F32)
        k.wait_all(dv, bc_ev)
        for ft in range(8):
            dv.tensor_tensor(wrk[:, :], Xv[ft], Abb[:, :], ALU.mult)
            k.fin(dv.tensor_tensor(xnv[ft], wrk[:, :], Bbb[:, :], ALU.add), "DV")
        xn_done = k.now("DV")
        if "xn" in DBG:
            k.wait(sync, xn_done)
            for ft in range(8):
                k.dma(sync.dma_start(out=g["xn"].rearrange("(a p) n -> a p n", p=128)[ft],
                                     in_=xnv[ft]), "DL")
        st_ev = k.now("DL")
    k.barrier([sync, gp, pe, dv, ac])

    if "B" not in stages:
        k.wait(sync, k.now("DL"))
        k.wait(gp, k.now("DG"))
        return
    es_de = k.es.enter_context(ExitStack())
    s_sb = es_de.enter_context(nc.sbuf_tensor("s_sb", [128, 8 * L0N], BF16))
    sv = _ftv(s_sb, L0N)
    es_bc = k.es.enter_context(ExitStack())
    qk = es_bc.enter_context(nc.sbuf_tensor("qk", [128, 8 * XL], BF16))
    qkv_ = _ftv(qk, XL)
    v_tm = es_bc.enter_context(nc.sbuf_tensor("v_tm", [128, 19 * 520], BF16))
    vtv = _ftv(v_tm, 520)
    with ExitStack() as esB:
        sbB = lambda name, shape, dtype=F32: esB.enter_context(nc.sbuf_tensor(name, list(shape), dtype))
        psB = lambda name, shape, dtype=F32: esB.enter_context(nc.psum_tensor(name, list(shape), dtype))
        wq = sbB("wq", (128, 8 * 1536), BF16)
        wqv = _ftv(wq, 1536)
        wstg = [sbB("wstgB0", (128, 1536), F32)]
        ropeC = sbB("ropeCb", (128, XL), F32)
        ropeS = sbB("ropeSb", (128, XL), F32)
        w_ev = []
        wqkv_d = g["Wqkv"].rearrange("(a p) n -> a p n", p=128)
        stg_ev = [None]
        for kt in range(8):
            k.wait(sync, stg_ev[0])
            e = k.dma(sync.dma_start(out=wstg[0][:, :], in_=wqkv_d[kt]), "DL")
            k.wait(ac, e)
            stg_ev[0] = ee = k.fin(ac.copy(wqv[kt], wstg[0][:, :]), "AC")
            w_ev.append(ee)
        if "wqd" in DBG:
            k.wait_all(sync, w_ev)
            wqd = g["wqd"].rearrange("(a p) n -> a p n", p=128)
            for kt in range(8):
                k.dma(sync.dma_start(out=wqd[kt], in_=wqv[kt]), "DL")
        rc_ev = k.dma(sync.dma_start(out=ropeC[:, :], in_=g["ropeC"]), "DL")
        rs_ev = k.dma(sync.dma_start(out=ropeS[:, :], in_=g["ropeS"]), "DL")
        ones_ev = k.fin(dv.memset(v_tm[:, :], 1.0), "DV")

        pP = [psB(f"pP{i}", (128, 512)) for i in range(2)]
        ev_evac = [None, None]
        for of in range(8):
            for ci, (c0, cn) in enumerate(_chunks(XL)):
                slot = (of * 5 + ci) % 2
                k.wait_all(pe, w_ev + [xn_done, ev_evac[slot]])
                for kt in range(8):
                    mm = pe.matmul(pP[slot][:, 0:cn], wqv[kt][:, 128 * of:128 * of + 128],
                                   xnv[kt][:, c0:c0 + cn], start=(kt == 0), stop=(kt == 7))
                pe_ev = k.fin(mm, "PE")
                k.wait(ac, pe_ev)
                ev_evac[slot] = k.fin(ac.copy(qkv_[of][:, c0:c0 + cn], pP[slot][:, 0:cn]), "AC")
        qk_done = [k.now("AC")]
        for tt in range(19):
            slot = tt % 2
            k.wait_all(pe, [ev_evac[slot], ones_ev])
            for kt in range(8):
                mm = pe.matmul(pP[slot][:, 0:512], xnv[kt][:, 128 * tt:128 * tt + 128],
                               wqv[kt][:, 1024:1536], start=(kt == 0), stop=(kt == 7))
            pe_ev = k.fin(mm, "PE")
            k.wait(ac, pe_ev)
            ev_evac[slot] = k.fin(
                ac.copy(vtv[tt].rearrange("p (h c) -> p h c", c=65)[:, :, 0:64],
                        pP[slot][:, 0:512].rearrange("p (h c) -> p h c", c=64)), "AC")
        v_done = k.now("AC")
        qsw = sbB("qsw", (128, XL), BF16)
        w1 = sbB("ropew1", (128, XL), F32)
        w2 = sbB("ropew2", (128, XL), F32)
        rope_ev = []
        for of in range(8):
            k.wait_all(dv, qk_done + [rc_ev, rs_ev])
            for (a, b2) in [(0, 32), (64, 96)]:
                dv.tensor_copy(qsw[a:a + 32, :], qkv_[of][b2:b2 + 32, :])
                dv.tensor_copy(qsw[b2:b2 + 32, :], qkv_[of][a:a + 32, :])
            dv.tensor_tensor(w1[:, :], qkv_[of], ropeC[:, :], ALU.mult)
            dv.tensor_tensor(w2[:, :], qsw[:, :], ropeS[:, :], ALU.mult)
            rope_ev.append(k.fin(dv.tensor_tensor(qkv_[of], w1[:, :], w2[:, :], ALU.add), "DV"))
        rope_done = k.now("DV")
        if "qkd" in DBG:
            k.wait(sync, rope_done)
            qkd = g["qkd"].rearrange("(a p) n -> a p n", p=128)
            for of in range(8):
                k.dma(sync.dma_start(out=qkd[of], in_=qkv_[of]), "DL")
        if "vtd" in DBG:
            k.wait(sync, v_done)
            vtd = g["vtd"].rearrange("(a p) n -> a p n", p=128)
            for tt in range(19):
                k.dma(sync.dma_start(out=vtd[tt], in_=vtv[tt]), "DL")

    if "C" not in stages:
        k.wait(sync, k.now("DL"))
        k.wait(gp, k.now("DG"))
        return
    k.barrier([sync, gp, pe, dv, ac])
    es_cd = k.es.enter_context(ExitStack())
    s_in = es_cd.enter_context(nc.sbuf_tensor("s_in", [128, 4 * L0N], BF16))
    siv = _ftv(s_in, L0N)
    with ExitStack() as esC:
        sbC = lambda name, shape, dtype=F32: esC.enter_context(nc.sbuf_tensor(name, list(shape), dtype))
        psC = lambda name, shape, dtype=F32: esC.enter_context(nc.psum_tensor(name, list(shape), dtype))
        psS = [psC(f"psS{i}", (128, 384)) for i in range(2)]
        psO = [psC(f"psO{i}", (128, 128)) for i in range(2)]
        psTr = psC("psTr", (64, 1024), BF16)
        Pt = [sbC(f"Pt{i}", (128, 384), BF16) for i in range(2)]
        if "dbgo" in DBG:
            nc.lookup_dbgo = [sbC("dbgob", (128, 200), F32)]
        if "dbgp" in DBG:
            nc.lookup_dbgp = [sbC("dbgpb", (128, 384), F32), sbC("dbgpb2", (128, 384), F32)]
        atm = [sbC(f"atm{i}", (128, 64), BF16) for i in range(2)]
        rc = [sbC(f"rc{i}", (128, 1), F32) for i in range(2)]
        rc2 = [sbC(f"rc2{i}", (128, 1), F32) for i in range(2)]
        evS = [None, None]
        evP = [None, None]
        evO = [None, None]
        evA = [None, None]
        evT = [None] * 8
        first = [rope_done, v_done] + qk_done
        inst = 0
        for qt in range(NQT):
            for h in range(H0):
                sS, sO = inst % 2, inst % 2
                sT = inst % 8
                ftk = 4 + h // 2
                ftq = h // 2
                rb = (h % 2) * 64
                k.wait_all(pe, first + [evS[sS]])
                first = []
                for s in range(3):
                    mm = pe.matmul(psS[sS][:, 128 * s:128 * s + 128],
                                   qkv_[ftk][rb:rb + 64, 128 * (qt + s):128 * (qt + s) + 128],
                                   qkv_[ftq][rb:rb + 64, L0S + 128 * qt:L0S + 128 * qt + 128],
                                   start=True, stop=True)
                e_mm = k.fin(mm, "PE")
                k.wait_all(ac, [e_mm, evP[sS]])
                e_exp = k.fin(ac.activation(Pt[sS][:, :], psS[sS][:, :], AF.Exp,
                                            scale=SCALE0), "AC")
                evS[sS] = e_exp
                k.wait(dv, e_exp)
                dv.scalar_tensor_tensor(Pt[sS][:, 0:128], Pt[sS][:, 0:128],
                                        validc[:, qt:qt + 1], mSL[:, :], ALU.mult, ALU.mult)
                dv.tensor_scalar(Pt[sS][:, 128:256], Pt[sS][:, 128:256],
                                 validc[:, qt + 1:qt + 2], None, ALU.mult)
                e_msk = k.fin(dv.scalar_tensor_tensor(Pt[sS][:, 256:384], Pt[sS][:, 256:384],
                                                      validc[:, qt + 2:qt + 3], mUI[:, :],
                                                      ALU.mult, ALU.mult), "DV")
                k.wait_all(pe, [e_msk, evO[sO]])
                for s in range(3):
                    mm = pe.matmul(psO[sO][:, 0:65], Pt[sS][:, 128 * s:128 * s + 128],
                                   vtv[qt + s][:, 65 * h:65 * h + 65],
                                   start=(s == 0), stop=(s == 2))
                evP[sS] = e_pv = k.fin(mm, "PE")
                k.wait_all(dv, [e_pv, evA[sO]])
                e1 = k.fin(dv.tensor_scalar_add(rc[sO][:, :], psO[sO][:, 64:65], 1e-30), "DV")
                k.wait(dv, e1)
                e2 = k.fin(dv.reciprocal(rc2[sO][:, :], rc[sO][:, :]), "DV")
                k.wait(dv, e2)
                e_nrm = k.fin(dv.tensor_scalar(atm[sO][:, :], psO[sO][:, 0:64],
                                               rc2[sO][:, 0:1], None, ALU.mult), "DV")
                evO[sO] = e_nrm
                if "dbgp" in DBG and inst == 43:
                    dv.tensor_copy(nc.lookup_dbgp[0][:, 0:384], Pt[sS][:, 0:384])
                    dbge2 = k.fin(dv.tensor_copy(nc.lookup_dbgp[1][:, 0:384], psS[sS][:, 0:384]), "DV")
                    k.wait(sync, dbge2)
                    k.dma(sync.dma_start(out=g["dbgp"][0:128, :], in_=nc.lookup_dbgp[0][:, :]), "DL")
                    k.dma(sync.dma_start(out=g["dbgp"][128:256, :], in_=nc.lookup_dbgp[1][:, :]), "DL")
                if "dbgo" in DBG and inst == 43:
                    dv.tensor_copy(nc.lookup_dbgo[0][:, 0:128], psO[sO][:, 0:128])
                    dv.tensor_copy(nc.lookup_dbgo[0][:, 128:129], rc[sO][:, 0:1])
                    dv.tensor_copy(nc.lookup_dbgo[0][:, 129:130], rc2[sO][:, 0:1])
                    dbge = k.fin(dv.tensor_copy(nc.lookup_dbgo[0][:, 130:194], atm[sO][:, 0:64]), "DV")
                    k.wait(sync, dbge)
                    k.dma(sync.dma_start(out=g["dbgo"], in_=nc.lookup_dbgo[0][:, :]), "DL")
                    e_nrm = dbge
                k.wait_all(pe, [e_nrm, evT[sT]])
                e_tr = k.fin(pe.transpose(psTr[0:64, 128 * sT:128 * sT + 128],
                                          atm[sO][:, :], ident[:, :]), "PE")
                evA[sO] = e_tr
                k.wait(ac, e_tr)
                evT[sT] = k.fin(ac.copy(siv[ftq][rb:rb + 64, 128 * qt:128 * qt + 128],
                                        psTr[0:64, 128 * sT:128 * sT + 128]), "AC")
                inst += 1
        attn_done = k.now("AC")
        if "sind" in DBG:
            k.wait(sync, attn_done)
            sind = g["sind"].rearrange("(a p) n -> a p n", p=128)
            for ft1 in range(4):
                k.dma(sync.dma_start(out=sind[ft1], in_=siv[ft1]), "DL")
    k.barrier([sync, gp, pe, dv, ac])

    if "D" not in stages:
        k.wait(sync, k.now("DL"))
        k.wait(gp, k.now("DG"))
        return
    with ExitStack() as esD:
        sbD = lambda name, shape, dtype=F32: esD.enter_context(nc.sbuf_tensor(name, list(shape), dtype))
        psD = lambda name, shape, dtype=F32: esD.enter_context(nc.psum_tensor(name, list(shape), dtype))
        wo = sbD("wo", (128, 4 * 1024), BF16)
        wov = _ftv(wo, 1024)
        wstgD = [sbD(f"wstgD{i}", (128, 1024), F32) for i in range(2)]
        wo_d = g["Wout"].rearrange("(a p) n -> a p n", p=128)
        wo_ev = []
        stg_ev = [None, None]
        for kt in range(4):
            sl = kt % 2
            k.wait(sync, stg_ev[sl])
            e = k.dma(sync.dma_start(out=wstgD[sl][:, :], in_=wo_d[kt]), "DL")
            k.wait(ac, e)
            stg_ev[sl] = ee = k.fin(ac.copy(wov[kt], wstgD[sl][:, :]), "AC")
            wo_ev.append(ee)
        pD = [psD(f"pD{i}", (128, 512)) for i in range(2)]
        ev_evac = [None, None]
        for of in range(8):
            for ci, (c0, cn) in enumerate(_chunks(L0N)):
                slot = (of * 5 + ci) % 2
                k.wait_all(pe, wo_ev + [attn_done, ev_evac[slot]])
                for kt in range(4):
                    mm = pe.matmul(pD[slot][:, 0:cn], wov[kt][:, 128 * of:128 * of + 128],
                                   siv[kt][:, c0:c0 + cn], start=(kt == 0), stop=(kt == 3))
                e = k.fin(mm, "PE")
                k.wait(ac, e)
                ev_evac[slot] = k.fin(ac.copy(sv[of][:, c0:c0 + cn], pD[slot][:, 0:cn]), "AC")
        s_done = k.now("AC")
        if "s0" in DBG:
            k.wait(sync, s_done)
            for of in range(8):
                k.dma(sync.dma_start(out=g["s0"].rearrange("(a p) n -> a p n", p=128)[of],
                                     in_=sv[of]), "DL")
    es_cd.close()
    es_bc.close()
    k.barrier([sync, gp, pe, dv, ac])

    if "E" not in stages:
        k.wait(sync, k.now("DL"))
        k.wait(gp, k.now("DG"))
        return
    with ExitStack() as esE:
        sbE = lambda name, shape, dtype=F32: esE.enter_context(nc.sbuf_tensor(name, list(shape), dtype))
        psE = lambda name, shape, dtype=F32: esE.enter_context(nc.psum_tensor(name, list(shape), dtype))
        acc = sbE("accF", (128, 8 * L0N), BF16)
        accv = _ftv(acc, L0N)
        wg = sbE("wgf", (128, 8 * 1536), BF16)
        wu = sbE("wuf", (128, 8 * 1536), BF16)
        wd = sbE("wdf", (128, 12 * 1024), BF16)
        wgv, wuv, wdv = _ftv(wg, 1536), _ftv(wu, 1536), _ftv(wd, 1024)
        hbuf = sbE("hbuf", (128, 12 * 512), BF16)
        hv = _ftv(hbuf, 512)
        sg = sbE("sgf", (128, 512), BF16)
        pG = psE("pG", (128, 512))
        pU = psE("pU", (128, 512))
        pDn = [psE(f"pDn{i}", (128, 512)) for i in range(2)]
        wgd = g["wgT"].rearrange("(a p) n -> a p n", p=128)
        wud = g["wuT"].rearrange("(a p) n -> a p n", p=128)
        wdd = g["wdT"].rearrange("(a p) n -> a p n", p=128)
        es_stg = ExitStack()
        wstgE = [es_stg.enter_context(nc.sbuf_tensor(f"wstgE{i}", [128, 1536], I32))
                 for i in range(2)]
        hh_ev = []
        stg_ev = [None, None]
        for hh in range(2):
            k.wait_all(ac, hh_ev)
            wl_g, wl_u, wl_d = [], [], []
            tiles = ([(wgv[kt], wgd[kt][:, 1536 * hh:1536 * (hh + 1)], wl_g) for kt in range(8)]
                     + [(wuv[kt], wud[kt][:, 1536 * hh:1536 * (hh + 1)], wl_u) for kt in range(8)]
                     + [(wdv[kt][:, 0:1024], wdd[12 * hh + kt], wl_d) for kt in range(12)])
            for ti, (dst, dsrc, bucket) in enumerate(tiles):
                sl = ti % 2
                k.wait(sync, stg_ev[sl])
                e = k.dma(sync.dma_start(out=wstgE[sl][:, 0:dsrc.shape[1]], in_=dsrc), "DL")
                k.wait(ac, e)
                stg_ev[sl] = ee = k.fin(ac.copy(dst, wstgE[sl][:, 0:dsrc.shape[1]]), "AC")
                bucket.append(ee)
            ev_g = None
            last_tt = None
            ev_u = [None] * 12
            for ci, (c0, cn) in enumerate(_chunks(L0N)):
                for ht in range(12):
                    k.wait_all(pe, wl_g + [xn_done, ev_g])
                    for kt in range(8):
                        mm = pe.matmul(pG[:, 0:cn], wgv[kt][:, 128 * ht:128 * ht + 128],
                                       xnv[kt][:, L0S + c0:L0S + c0 + cn],
                                       start=(kt == 0), stop=(kt == 7))
                    e_g = k.fin(mm, "PE")
                    k.wait_all(ac, [e_g, last_tt])
                    e_sg = k.fin(ac.activation(sg[:, 0:cn], pG[:, 0:cn], AF.Silu,
                                               scale=1.0 / 127.0), "AC")
                    ev_g = e_sg
                    k.wait_all(pe, wl_u + [e_sg, last_tt])
                    for kt in range(8):
                        mm = pe.matmul(pU[:, 0:cn], wuv[kt][:, 128 * ht:128 * ht + 128],
                                       xnv[kt][:, L0S + c0:L0S + c0 + cn],
                                       start=(kt == 0), stop=(kt == 7))
                    e_um = k.fin(mm, "PE")
                    k.wait(dv, e_um)
                    ev_u[ht] = last_tt = k.fin(
                        dv.tensor_tensor(hv[ht][:, 0:cn], sg[:, 0:cn],
                                         pU[:, 0:cn], ALU.mult), "DV")
                for of in range(8):
                    slot = of % 2
                    k.wait_all(pe, ev_u + wl_d)
                    for kt in range(12):
                        mm = pe.matmul(pDn[slot][:, 0:cn], wdv[kt][:, 128 * of:128 * of + 128],
                                       hv[kt][:, 0:cn], start=(kt == 0), stop=(kt == 11))
                    e_d = k.fin(mm, "PE")
                    if hh == 0:
                        k.wait(ac, e_d)
                        e = k.fin(ac.copy(accv[of][:, c0:c0 + cn], pDn[slot][:, 0:cn]), "AC")
                    else:
                        k.wait(dv, e_d)
                        e = k.fin(dv.tensor_tensor(accv[of][:, c0:c0 + cn],
                                                   accv[of][:, c0:c0 + cn],
                                                   pDn[slot][:, 0:cn], ALU.add), "DV")
                    k.wait(pe, e)
            hh_ev = [k.now("PE"), k.now("AC"), k.now("DV")]
        ffn_done = (k.now("DV"), k.now("AC"))
        es_stg.close()
        k.barrier([sync, dv, ac])
        xs = [sbE(f"xsE{i}", (128, 512), F32) for i in range(2)]
        gt = [sbE(f"gtE{i}", (128, 512), BF16) for i in range(2)]
        x1t = [sbE(f"x1tE{i}", (128, 512), F32) for i in range(2)]
        x1d = g["x1sp"].rearrange("(a p) n -> a p n", p=128)
        st_ev = [None, None]
        gt_ev = [None, None]
        for of in range(8):
            for ci, (c0, cn) in enumerate(_chunks(L0N)):
                slot = (of * 5 + ci) % 2
                k.wait(sync, st_ev[slot])
                e_x = k.dma(sync.dma_start(out=xs[slot][:, 0:cn],
                                           in_=xTv[of][:, L0S + c0:L0S + c0 + cn]), "DL")
                k.wait_all(ac, [ffn_done[0], ffn_done[1], gt_ev[slot]])
                e_gt = k.fin(ac.activation(gt[slot][:, 0:cn], accv[of][:, c0:c0 + cn],
                                           AF.Silu, scale=1.0 / 16129.0), "AC")
                k.wait_all(dv, [e_gt, e_x])
                dv.tensor_tensor(gt[slot][:, 0:cn], gt[slot][:, 0:cn],
                                 sv[of][:, c0:c0 + cn], ALU.mult)
                gt_ev[slot] = e_c = k.fin(
                    dv.tensor_tensor(x1t[slot][:, 0:cn], xs[slot][:, 0:cn],
                                     gt[slot][:, 0:cn], ALU.add), "DV")
                k.wait(sync, e_c)
                st_ev[slot] = k.dma(sync.dma_start(out=x1d[of][:, c0:c0 + cn],
                                                   in_=x1t[slot][:, 0:cn]), "DL")
    es_de.close()
    x1_done = k.now("DL")
    k.barrier([sync, gp, pe, dv, ac])

    if "F" not in stages:
        k.wait(sync, k.now("DL"))
        k.wait(gp, k.now("DG"))
        return

    xn1v = [t[:, 0:L1N] for t in xnv]
    x1dv = g["x1sp"].rearrange("(a p) n -> a p n", p=128)

    with ExitStack() as esF1:
        sbF = lambda name, shape, dtype=F32: esF1.enter_context(nc.sbuf_tensor(name, list(shape), dtype))
        psF = lambda name, shape, dtype=F32: esF1.enter_context(nc.psum_tensor(name, list(shape), dtype))
        X1 = sbF("X1", (128, 8 * L1N), F32)
        X1v = _ftv(X1, L1N)
        x1_ev = []
        for ft in range(8):
            x1_ev.append(k.dma(sync.dma_start(out=X1v[ft], in_=x1dv[ft][:, 64:64 + L1N]), "DL"))
        wrkQ = [sbF(f"wrkQF{i}", (128, 512), F32) for i in range(2)]
        pS = psF("pSF", (1, 512))
        pQ = psF("pQF", (1, 512))
        pbcF1 = psF("pbcF1", (128, 512))
        pbcF2 = psF("pbcF2", (128, 512))
        srow = sbF("srowF", (1, L1N), F32)
        qrow = sbF("qrowF", (1, L1N), F32)
        q_ev = [None, None]
        for ci, (c0, cn) in enumerate(_chunks(L1N)):
            for ft in range(8):
                slot = ft % 2
                k.wait(ac, x1_ev[ft])
                k.wait(ac, q_ev[slot])
                sq = k.fin(ac.activation(wrkQ[slot][:, 0:cn], X1v[ft][:, c0:c0 + cn],
                                         AF.Square), "AC")
                k.wait(pe, sq)
                pe.matmul(pS[0:1, 0:cn], onescf[:, :], X1v[ft][:, c0:c0 + cn],
                          start=(ft == 0), stop=(ft == 7))
                mm = pe.matmul(pQ[0:1, 0:cn], onescf[:, :], wrkQ[slot][:, 0:cn],
                               start=(ft == 0), stop=(ft == 7))
                q_ev[slot] = k.fin(mm, "PE")
            k.wait(dv, q_ev[1])
            dv.tensor_copy(srow[0:1, c0:c0 + cn], pS[0:1, 0:cn])
            e = k.fin(dv.tensor_copy(qrow[0:1, c0:c0 + cn], pQ[0:1, 0:cn]), "DV")
            k.wait(pe, e)
        arow = sbF("arowF", (1, L1N), F32)
        brow = sbF("browF", (1, L1N), F32)
        dv.tensor_scalar_mul(srow[:, :], srow[:, :], 1.0 / D)
        dv.tensor_tensor(brow[:, :], srow[:, :], srow[:, :], ALU.mult)
        dv.scalar_tensor_tensor(brow[:, :], qrow[:, :], 1.0 / D, brow[:, :],
                                ALU.mult, ALU.subtract)
        ev = k.fin(dv.tensor_scalar_add(brow[:, :], brow[:, :], 1e-5), "DV")
        k.wait(ac, ev)
        ev = k.fin(ac.activation(brow[:, :], brow[:, :], AF.Sqrt), "AC")
        k.wait(dv, ev)
        e2 = k.fin(dv.reciprocal(arow[:, :], brow[:, :]), "DV")
        k.wait(dv, e2)
        dv.tensor_tensor(brow[:, :], srow[:, :], arow[:, :], ALU.mult)
        ev = k.fin(dv.tensor_scalar_mul(brow[:, :], brow[:, :], -1.0), "DV")
        Abb = sbF("AbbF", (128, L1N), F32)
        Bbb = sbF("BbbF", (128, L1N), F32)
        k.wait(pe, ev)
        bc_ev = []
        for (c0, cn) in _chunks(L1N):
            k.fin(pe.matmul(pbcF1[:, 0:cn], onesrf[0:1, :], arow[0:1, c0:c0 + cn],
                            start=True, stop=True), "PE")
            e2 = k.fin(pe.matmul(pbcF2[:, 0:cn], onesrf[0:1, :], brow[0:1, c0:c0 + cn],
                                 start=True, stop=True), "PE")
            k.wait(ac, e2)
            ac.copy(Abb[:, c0:c0 + cn], pbcF1[:, 0:cn])
            e3 = k.fin(ac.copy(Bbb[:, c0:c0 + cn], pbcF2[:, 0:cn]), "AC")
            bc_ev.append(e3)
            k.wait(pe, e3)
        wrk = sbF("wrkLNF", (128, L1N), F32)
        k.wait_all(dv, bc_ev)
        for ft in range(8):
            dv.tensor_tensor(wrk[:, :], X1v[ft], Abb[:, :], ALU.mult)
            k.fin(dv.tensor_tensor(xn1v[ft], wrk[:, :], Bbb[:, :], ALU.add), "DV")
        xn1_done = k.now("DV")
        if "xn1" in DBG:
            k.wait(sync, xn1_done)
            for ft in range(8):
                k.dma(sync.dma_start(out=g["xn1"].rearrange("(a p) n -> a p n", p=128)[ft],
                                     in_=xn1v[ft]), "DL")
    k.barrier([sync, gp, pe, dv, ac])

    es_l1 = k.es.enter_context(ExitStack())
    q_in = es_l1.enter_context(nc.sbuf_tensor("q_in", [128, 8 * L1N], BF16))
    k_in = es_l1.enter_context(nc.sbuf_tensor("k_in", [128, 8 * L1N], BF16))
    v_in = es_l1.enter_context(nc.sbuf_tensor("v_in", [128, 8 * L1N], BF16))
    qiv, kiv, viv = _ftv(q_in, L1N), _ftv(k_in, L1N), _ftv(v_in, L1N)
    convw = es_l1.enter_context(nc.sbuf_tensor("convwF", [128, 3 * 8 * 4], F32))
    convv_ = _ftv(convw, 4)
    m3 = es_l1.enter_context(nc.sbuf_tensor("m3F", [128, 3], F32))
    selh2 = es_l1.enter_context(nc.sbuf_tensor("selh2F", [128, 8 * 16], BF16))
    selh2v = _ftv(selh2, 16)
    selhb = es_l1.enter_context(nc.sbuf_tensor("selhbF", [16, 8 * 128], BF16))
    selhbv = _ftv(selhb, 128)
    selhf = es_l1.enter_context(nc.sbuf_tensor("selhfF", [16, 8 * 128], F32))
    selhfv = _ftv(selhf, 128)
    cbc = es_l1.enter_context(nc.sbuf_tensor("cbcF", [128, 9], F32))
    es_mid = k.es.enter_context(ExitStack())
    abq = es_mid.enter_context(nc.sbuf_tensor("abqF", [48, L1N], F32))

    ld_ev = []
    for tsb, name in [(m3, "mask3"), (selh2, "selh2"), (selhb, "selh"),
                      (selhf, "selhf"), (cbc, "ctrlb2")]:
        ld_ev.append(k.dma(sync.dma_start(out=tsb[:, :], in_=g[name]), "DL"))
    for ci_, name in [(0, "convq"), (1, "convk"), (2, "convv")]:
        cd = g[name].rearrange("(a p) n -> a p n", p=128)
        for ft in range(8):
            ld_ev.append(k.dma(sync.dma_start(out=convv_[8 * ci_ + ft], in_=cd[ft]), "DL"))

    with ExitStack() as esF2:
        sbF2 = lambda name, shape, dtype=F32: esF2.enter_context(nc.sbuf_tensor(name, list(shape), dtype))
        psF2 = lambda name, shape, dtype=F32: esF2.enter_context(nc.psum_tensor(name, list(shape), dtype))
        og = sbF2("ogF", (128, 8 * L1N), BF16)
        ogv = _ftv(og, L1N)
        wstg = sbF2("wstgF", (128, 128), F32)
        wti = [sbF2(f"wtiF{i}", (128, 8 * 128), BF16) for i in range(2)]
        pF = [psF2(f"pFF{i}", (128, 512)) for i in range(2)]
        qkvT_d = g["qkvT"].rearrange("(a p) n -> a p n", p=128)
        ctrlT_d = g["ctrlT"].rearrange("(a p) n -> a p n", p=128)
        stg_ev = None
        wfree = [None, None]
        ev_evac = [None, None]
        for of in range(33):
            wsl = of % 2
            wtiv = _ftv(wti[wsl], 128)
            ncw = 128
            cnv = []
            for kt in range(8):
                k.wait_all(sync, [stg_ev])
                if of < 24:
                    dsrc = qkvT_d[kt][:, 128 * of:128 * of + 128]
                else:
                    c0w = 128 * (of - 24)
                    dsrc = ctrlT_d[kt][:, c0w:c0w + 128]
                e = k.dma(sync.dma_start(out=wstg[:, 0:ncw], in_=dsrc), "DL")
                k.wait_all(ac, [e, wfree[wsl]])
                stg_ev = ee = k.fin(ac.copy(wtiv[kt][:, 0:ncw], wstg[:, 0:ncw]), "AC")
                cnv.append(ee)
            for ci, (c0, cn) in enumerate(_chunks(L1N)):
                slot = (of * 5 + ci) % 2
                k.wait_all(pe, cnv + [xn1_done, ev_evac[slot]])
                for kt in range(8):
                    mm = pe.matmul(pF[slot][0:ncw, 0:cn], wtiv[kt][:, 0:ncw],
                                   xn1v[kt][:, c0:c0 + cn], start=(kt == 0), stop=(kt == 7))
                e_mm = k.fin(mm, "PE")
                if ci == 4:
                    wfree[wsl] = e_mm
                k.wait(ac, e_mm)
                if of < 24:
                    dst = (qiv, kiv, viv)[of // 8][of % 8]
                    ev_evac[slot] = k.fin(ac.copy(dst[:, c0:c0 + cn], pF[slot][:, 0:cn]), "AC")
                else:
                    t = of - 24
                    evs = None
                    if t < 8:
                        evs = ac.activation(ogv[t][0:64, c0:c0 + cn],
                                            pF[slot][64:128, 0:cn], AF.Sigmoid,
                                            bias=cbc[64:128, t:t + 1])
                    if t >= 1:
                        evs = ac.activation(ogv[t - 1][64:128, c0:c0 + cn],
                                            pF[slot][0:64, 0:cn], AF.Sigmoid,
                                            bias=cbc[0:64, t:t + 1])
                    if t == 0:
                        evs = ac.activation(abq[0:16, c0:c0 + cn], pF[slot][0:16, 0:cn],
                                            AF.Sigmoid, bias=cbc[0:16, t:t + 1])
                        evs = ac.activation(abq[32:48, c0:c0 + cn], pF[slot][32:48, 0:cn],
                                            AF.Sigmoid, bias=cbc[32:48, t:t + 1])
                    ev_evac[slot] = k.fin(evs, "AC")
        proj_done = [k.now("AC")]
        k.wait_all(sync, proj_done)
        og_d = g["ogsp"].rearrange("(a p) n -> a p n", p=128)
        xn1_d = g["xn1sv"].rearrange("(a p) n -> a p n", p=128)
        sp_ev = []
        for ft in range(8):
            sp_ev.append(k.dma(sync.dma_start(out=og_d[ft], in_=ogv[ft][:, SOFF:SOFF + SPAN]), "DL"))
            sp_ev.append(k.dma(sync.dma_start(out=xn1_d[ft], in_=xn1v[ft]), "DL"))
        k.wait_all(sync, sp_ev)
        k.barrier([sync, gp, pe, dv, ac])
    es_xn.close()
    k.barrier([sync, gp, pe, dv, ac])

    kbt = es_l1.enter_context(nc.sbuf_tensor("kbt", [128, SPAN], BF16, side="right"))
    rows = es_mid.enter_context(nc.sbuf_tensor("rowsF", [16, 5 * SPAN], F32))
    rowv = _ftv(rows, SPAN)
    rbf = es_mid.enter_context(nc.sbuf_tensor("rbfF", [16, SPAN], BF16))
    sct = es_l1.enter_context(nc.sbuf_tensor("sctF", [128, 8 * 32], F32, side="right"))
    sctv = _ftv(sct, 32)

    with ExitStack() as esF3:
        sbF3 = lambda name, shape, dtype=F32: esF3.enter_context(nc.sbuf_tensor(name, list(shape), dtype))
        psF3 = lambda name, shape, dtype=F32: esF3.enter_context(nc.psum_tensor(name, list(shape), dtype))
        k.wait_all(dv, proj_done)
        e = k.fin(dv.tensor_scalar(abq[0:16, SOFF:SOFF + SPAN], abq[0:16, SOFF:SOFF + SPAN],
                                   0.98, 0.01, ALU.mult, ALU.add), "DV")
        ab_done = [e]

        es_acc = ExitStack()
        acc = es_acc.enter_context(nc.sbuf_tensor("accF3", [128, SPAN], F32))
        conv_done = []
        k.wait_all(dv, ld_ev)
        for ci_, tiles in [(0, qiv), (1, kiv), (2, viv)]:
            for ft in range(8):
                tl = tiles[ft]
                k.wait_all(dv, proj_done + conv_done[-1:])
                dv.tensor_tensor(tl[:, 61:64], tl[:, 61:64], m3[:, :], ALU.mult)
                dv.tensor_scalar(acc[:, :], tl[:, 61:61 + SPAN],
                                 convv_[8 * ci_ + ft][:, 0:1], None, ALU.mult)
                for j in range(1, 4):
                    dv.scalar_tensor_tensor(acc[:, :], tl[:, 61 + j:61 + j + SPAN],
                                            convv_[8 * ci_ + ft][:, j:j + 1], acc[:, :],
                                            ALU.mult, ALU.add)
                e = k.fin(dv.engine_nop(), "DV")
                k.wait(ac, e)
                conv_done.append(k.fin(ac.activation(tl[:, SOFF:SOFF + SPAN], acc[:, :],
                                                     AF.Silu), "AC"))
        conv_all = k.now("AC")
        k.wait(dv, conv_all)
        e = k.fin(dv.tensor_copy(rowv[4][0:16, :], abq[32:48, SOFF:SOFF + SPAN]), "DV")
        beta_done = e
        es_acc.close()

        sqb = sbF3("sqbF", (128, SPAN), BF16)
        pR = psF3("pRF", (16, SPAN))
        for which, tiles, dstrow in [(0, qiv, 2), (1, kiv, 3)]:
            sq_ev = None
            for ft in range(8):
                k.wait_all(ac, [conv_all, sq_ev])
                e_sq = k.fin(ac.activation(sqb[:, :], tiles[ft][:, SOFF:SOFF + SPAN],
                                           AF.Square), "AC")
                k.wait(pe, e_sq)
                for (c0, cn) in _chunks(SPAN):
                    mm = pe.matmul(pR[0:16, c0:c0 + cn], selh2v[ft][:, 0:16],
                                   sqb[:, c0:c0 + cn], start=(ft == 0), stop=(ft == 7))
                sq_ev = k.fin(mm, "PE")
            e = sq_ev
            k.wait(dv, e)
            e = k.fin(dv.tensor_copy(rowv[1][0:16, :], pR[0:16, 0:SPAN]), "DV")
            k.wait(pe, e)
            k.wait(ac, e)
            e = k.fin(ac.activation(rowv[1][0:16, :], rowv[1][0:16, :], AF.Sqrt), "AC")
            k.wait(dv, e)
            e = k.fin(dv.tensor_scalar_max(rowv[1][0:16, :], rowv[1][0:16, :], 1e-6), "DV")
            k.wait(dv, e)
            e = k.fin(dv.reciprocal(rowv[dstrow][0:16, :], rowv[1][0:16, :]), "DV")
            k.wait(dv, e)
            l2n_done = e
        k.wait_all(ac, ab_done + [l2n_done])
        e = k.fin(ac.activation(rowv[1][0:16, :], abq[0:16, SOFF:SOFF + SPAN], AF.Ln), "AC")
        k.wait(dv, e)
        cvv = rowv[1][0:16, :].rearrange("p (c u) -> p c u", u=CH)
        for sshift in [1, 2, 4, 8, 16, 32]:
            dv.tensor_tensor(cvv[:, :, sshift:CH], cvv[:, :, sshift:CH],
                             cvv[:, :, 0:CH - sshift], ALU.add)
        e = k.fin(dv.engine_nop(), "DV")
        k.wait(ac, e)
        e_gam = k.fin(ac.activation(rowv[0][0:16, :], rowv[1][0:16, :], AF.Exp), "AC")
        k.wait(dv, e_gam)
        gcv2 = rowv[0][0:16, :].rearrange("p (c u) -> p c u", u=CH)
        e_gc = k.fin(dv.tensor_copy(rowv[1][0:16, 0:NCH], gcv2[:, :, CH - 1]), "DV")

        cbt = sbF3("cbtF", (128, SPAN), BF16)
        pB = [psF3(f"pBF{i}", (128, 512)) for i in range(2)]
        k.wait(pe, e_gc)
        for p in range(8):
            e_mm = k.fin(pe.matmul(pB[0][:, 0:NCH], selhfv[p][:, :], rowv[1][0:16, 0:NCH],
                                   start=True, stop=True), "PE")
            k.wait(ac, e_mm)
            e_sc = k.fin(ac.copy(sctv[p], pB[0][:, 0:NCH]), "AC")
            k.wait(pe, e_sc)
        sct_done = k.now("AC")
        pe_sct_done = k.now("PE")
        RC = rowv[1]
        last_mm = None
        mul_done = []
        k.wait_all(dv, [beta_done])
        k.wait(dv, pe_sct_done)
        for r_ in [3, 2, 1, 4]:
            if r_ == 3:
                dv.tensor_tensor(RC[0:16, :], rowv[0][0:16, :], rowv[3][0:16, :], ALU.mult)
                e = k.fin(dv.tensor_tensor(RC[0:16, :], RC[0:16, :],
                                           rowv[4][0:16, :], ALU.mult), "DV")
            elif r_ == 2:
                e = k.fin(dv.reciprocal(RC[0:16, :], rowv[0][0:16, :]), "DV")
                k.wait(dv, e)
                e = k.fin(dv.tensor_tensor(RC[0:16, :], RC[0:16, :],
                                           rowv[3][0:16, :], ALU.mult), "DV")
            elif r_ == 1:
                e = k.fin(dv.tensor_tensor(RC[0:16, :], rowv[0][0:16, :],
                                           rowv[2][0:16, :], ALU.mult), "DV")
            else:
                e = k.fin(dv.tensor_copy(RC[0:16, :], rowv[4][0:16, :]), "DV")
            k.wait_all(dv, [e, last_mm])
            e_rb = k.fin(dv.tensor_copy(rbf[0:16, 0:SPAN], RC[0:16, :]), "DV")
            iskb = (r_ == 3)
            tiles = (None, qiv, kiv, kiv, viv)[r_]
            for ft in range(8):
                ev_evac2 = None
                for ci, (c0, cn) in enumerate(_chunks(SPAN)):
                    slot = ci % 2
                    k.wait_all(pe, [e_rb, ev_evac2, sct_done, last_mm])
                    last_mm = e_mm = k.fin(pe.matmul(pB[slot][:, 0:cn], selhbv[ft][:, :],
                                                     rbf[0:16, c0:c0 + cn],
                                                     start=True, stop=True), "PE")
                    k.wait_all(ac, [e_mm] + (mul_done[-1:] if ci == 0 else []))
                    ev_evac2 = k.fin(ac.copy(cbt[:, c0:c0 + cn], pB[slot][:, 0:cn]), "AC")
                k.wait_all(dv, [ev_evac2] + mul_done[-1:])
                if iskb:
                    e_m = k.fin(dv.tensor_tensor(kbt[:, :], tiles[ft][:, SOFF:SOFF + SPAN],
                                                 cbt[:, :], ALU.mult), "DV")
                    k.wait(sync, e_m)
                    e_kb = k.dma(sync.dma_start(out=g["kbsv"].rearrange("(a p) n -> a p n", p=128)[ft],
                                                in_=kbt[:, :]), "DL")
                    mul_done.append(e_kb)
                else:
                    e_m = k.fin(dv.tensor_tensor(tiles[ft][:, SOFF:SOFF + SPAN],
                                                 tiles[ft][:, SOFF:SOFF + SPAN],
                                                 cbt[:, :], ALU.mult), "DV")
                mul_done.append(e_m)
        scaled_done = k.now("DV")
        if "l1t" in DBG:
            k.wait(sync, scaled_done)
            l1t_d = g["l1t"].rearrange("(a w) n -> a w n", w=128)
            for ft in range(8):
                k.dma(sync.dma_start(out=l1t_d[ft], in_=kiv[ft][:, SOFF:SOFF + SPAN]), "DL")
                k.dma(sync.dma_start(out=l1t_d[8 + ft], in_=g["kbsv"].rearrange("(a p) n -> a p n", p=128)[ft]), "DL")
                k.dma(sync.dma_start(out=l1t_d[16 + ft], in_=viv[ft][:, SOFF:SOFF + SPAN]), "DL")
            k.dma(sync.dma_start(out=g["l1r"][0:16, 0:SPAN], in_=rowv[0][0:16, :]), "DL")
            k.dma(sync.dma_start(out=g["l1r"][16:32, 0:SPAN], in_=abq[0:16, SOFF:SOFF + SPAN]), "DL")
            k.dma(sync.dma_start(out=g["l1r"][32:48, 0:SPAN], in_=abq[32:48, SOFF:SOFF + SPAN]), "DL")
        k.wait(sync, k.now("DL"))
    es_mid.close()
    k.barrier([sync, gp, pe, dv, ac])

    if "G" not in stages:
        k.wait(sync, k.now("DL"))
        k.wait(gp, k.now("DG"))
        return
    emit_stageG(nc, k, g, ident, qiv, kiv, viv, sctv, DBG, glim)
    es_l1.close()
    k.barrier([sync, gp, pe, dv, ac])

    if "H" not in stages:
        k.wait(sync, k.now("DL"))
        k.wait(gp, k.now("DG"))
        return
    emit_stageH(nc, k, g, ident, DBG)
    k.wait(sync, k.now("DL"))
    k.wait(gp, k.now("DG"))



def emit_stageG(nc, k, g, ident, qiv, kiv, viv, sctv, DBG, glim=(NPAIR, NCH // 4)):
    sync, gp, pe, dv, ac = nc.sync, nc.gpsimd, nc.tensor, nc.vector, nc.scalar
    qs = [t[:, SOFF:SOFF + SPAN] for t in qiv]
    ks = [t[:, SOFF:SOFF + SPAN] for t in kiv]
    vs = [t[:, SOFF:SOFF + SPAN] for t in viv]
    kb_d = g["kbsv"].rearrange("(a p) n -> a p n", p=128)
    with ExitStack() as esG:
        sb = lambda name, shape, dtype=F32: esG.enter_context(nc.sbuf_tensor(name, list(shape), dtype))
        ps = lambda name, shape, dtype=F32: esG.enter_context(nc.psum_tensor(name, list(shape), dtype))
        msl4 = sb("msl4G", (128, 512), BF16)
        msu4 = sb("msu4G", (128, 512), BF16)
        mui4 = sb("mui4G", (128, 512), BF16)
        mbd8 = sb("mbd8G", (128, 1024), BF16)
        ld_ev = []
        for t, name in [(msl4, "msl4"), (msu4, "msu4"), (mui4, "mui4"), (mbd8, "mbd8")]:
            ld_ev.append(k.dma(sync.dma_start(out=t[:, :], in_=g[name]), "DL"))
        qP = sb("qPG", (128, NCH * 128), BF16)
        khP = sb("khPG", (128, NCH * 128), BF16)
        kbP = sb("kbPG", (128, NCH * 128), BF16)
        UF = sb("UFG", (128, NCH * 256), BF16)
        FT = sb("FTG", (128, NCH * 128), BF16)
        KTM = sb("KTMG", (128, NCH * 128), BF16)
        ptg = [sb(f"ptgG{i}", (128, 512), BF16) for i in range(2)]
        kbl = sb("kblG", (128, SPAN), BF16)
        MPRE = sb("MPREG", (128, (NCH + 1) * 128), BF16)
        ST0b = sb("ST0G", (128, (NCH + 1) * 128), BF16)
        MT2 = sb("MT2G", (128, 2 * 128), BF16)
        WMC = sb("WMCG", (128, 2 * 128), BF16)
        NAg = sb("NAgG", (128, 512), BF16)
        NATg = sb("NATgG", (128, 512), BF16)
        PgA = sb("PgAG", (128, 512), BF16)
        PgAT = sb("PgATG", (128, 512), BF16)
        negNc = sb("negNcG", (128, 128), BF16)
        MCf = sb("MCfG", (128, 128), F32)
        CCf = sb("CCfG", (128, 128), F32)
        psA = ps("psAG", (128, 512))
        psB = ps("psBG", (128, 512))
        psT = ps("psTG", (128, 1024), BF16)
        psC = [ps(f"psCG{i}", (128, 128)) for i in range(2)]
        psD = [ps(f"psDG{i}", (128, 128)) for i in range(2)]

        e_z = None
        for t in (qP, khP, kbP):
            e_z = k.fin(dv.memset(t[:, :], 0.0), "DV")
        k.wait(pe, e_z)
        for j in range(4):
            pe.transpose(psT[64:128, 256 * j:256 * j + 64],
                         qP[64:128, 0:64], ident[64:128, 64:128])
            pe.transpose(psT[0:64, 256 * j + 64:256 * j + 128],
                         qP[0:64, 64:128], ident[0:64, 0:64])

        ufd = g["ufsp"]; ftd = g["ftsp"]; ktd = g["ktsp"]; ptd = g["ptsp"]
        mpd = g["mpsp"]; std = g["stsp"]; mcd = g["mcsp"]; ccd = g["ccsp"]
        qtd = g["qtsp"]
        spill_ev = []
        k.wait_all(dv, ld_ev + [e_z])

        def pview(t):
            v3 = t.rearrange("p (c u) -> p c u", u=128)
            return v3

        dv.tensor_copy(MPRE[:, 0:128], ident[:, :])
        e_init0 = k.fin(dv.memset(ST0b[:, 0:128], 0.0), "DV")
        k.wait(pe, e_init0)
        e_kb = k.dma(sync.dma_start(out=kbl[:, :], in_=kb_d[0]), "DL")
        pad_free = None
        qt_prev = None
        for p in range(glim[0]):
            PH = glim[3] if len(glim) > 3 else 31
            k.wait_all(dv, [e_kb, pad_free, qt_prev])
            k.wait_all(ac, spill_ev)
            pb_ev = []
            for (dst, srcv) in [(qP, qs), (khP, ks), (kbP, None)]:
                d3 = pview(dst[:, :])
                if srcv is None:
                    se = kbl[0:64, :].rearrange("p (c u) -> p c u", u=CH)
                    so = kbl[64:128, :].rearrange("p (c u) -> p c u", u=CH)
                else:
                    se = srcv[p][0:64, :].rearrange("p (c u) -> p c u", u=CH)
                    so = srcv[p][64:128, :].rearrange("p (c u) -> p c u", u=CH)
                dv.tensor_copy(d3[0:64, :, 0:64], se)
                pb_ev.append(k.fin(dv.tensor_copy(d3[64:128, :, 64:128], so), "DV"))
            k.wait_all(sync, pb_ev)
            qt_sp = k.dma(sync.dma_start(out=qtd[p], in_=qP[:, :]), "DL")
            qt_prev = qt_sp
            if p + 1 < glim[0]:
                e_kb = k.dma(sync.dma_start(out=kbl[:, :], in_=kb_d[p + 1]), "DL")
            ev_psA = None
            ev_psB = None
            ev_psT = None
            ev_chain = None
            pt_sp = [None, None]
            for gi in range(glim[1]):
                cs = [4 * gi + j for j in range(4)]
                if not (PH & 1):
                    break
                k.wait_all(pe, [ev_psA] + pb_ev)
                for j, c in enumerate(cs):
                    mm = pe.matmul(psA[:, 128 * j:128 * j + 128], kbP[:, 128 * c:128 * c + 128],
                                   khP[:, 128 * c:128 * c + 128], start=True, stop=True)
                e = k.fin(mm, "PE")
                k.wait(dv, e)
                ev_psA = k.fin(dv.tensor_tensor(NAg[:, :], psA[:, :], msl4[:, :], ALU.mult), "DV")
                k.wait_all(pe, [ev_psB])
                for j, c in enumerate(cs):
                    mm = pe.matmul(psB[:, 128 * j:128 * j + 128], khP[:, 128 * c:128 * c + 128],
                                   kbP[:, 128 * c:128 * c + 128], start=True, stop=True)
                e = k.fin(mm, "PE")
                k.wait(dv, e)
                ev_psB = k.fin(dv.tensor_tensor(NATg[:, :], psB[:, :], msu4[:, :], ALU.mult), "DV")
                k.wait_all(pe, [ev_psA])
                for j, c in enumerate(cs):
                    mm = pe.matmul(psA[:, 128 * j:128 * j + 128], khP[:, 128 * c:128 * c + 128],
                                   qP[:, 128 * c:128 * c + 128], start=True, stop=True)
                e = k.fin(mm, "PE")
                k.wait_all(dv, [e, pt_sp[gi % 2]])
                e_ptw = k.fin(dv.tensor_tensor(ptg[gi % 2][:, :], psA[:, :],
                                               mui4[:, :], ALU.mult), "DV")
                ev_psA = e_ptw
                k.wait(sync, e_ptw)
                pt_sp[gi % 2] = k.dma(sync.dma_start(out=ptd[p][:, 512 * gi:512 * gi + 512],
                                                     in_=ptg[gi % 2][:, :]), "DL")
                if not (PH & 2):
                    continue
                k.wait_all(pe, [ev_psT])
                for j, c in enumerate(cs):
                    pe.transpose(psT[0:64, 256 * j:256 * j + 64],
                                 vs[p][0:64, CH * c:CH * c + CH], ident[0:64, 0:64])
                    pe.transpose(psT[64:128, 256 * j + 64:256 * j + 128],
                                 vs[p][64:128, CH * c:CH * c + CH], ident[64:128, 64:128])
                    mm = pe.transpose(psT[:, 256 * j + 128:256 * j + 256],
                                      kbP[:, 128 * c:128 * c + 128], ident[:, :])
                e = k.fin(mm, "PE")
                k.wait(dv, e)
                e_uf0 = k.fin(dv.tensor_tensor(UF[:, 1024 * gi:1024 * gi + 1024], psT[:, :],
                                               mbd8[:, :], ALU.mult), "DV")
                k.wait_all(pe, [e_uf0])
                for j, c in enumerate(cs):
                    mm = pe.transpose(psT[:, 128 * j:128 * j + 128],
                                      khP[:, 128 * c:128 * c + 128], ident[:, :])
                e = k.fin(mm, "PE")
                if gi == glim[1] - 1:
                    pad_free = e
                k.wait(ac, e)
                ev_psT = k.fin(ac.copy(KTM[:, 512 * gi:512 * gi + 512], psT[:, 0:512]), "AC")
                if not (PH & 4):
                    continue
                powers = [(PgA, PgAT), (NAg, NATg), (PgA, PgAT), (NAg, NATg)]
                curP, curT = NAg, NATg
                e_apl = e_uf0
                for lev in range(6):
                    for rnd in range(2):
                        k.wait_all(pe, [ev_psB, e_apl, ev_psT])
                        for j in (0, 1):
                            c = cs[2 * rnd + j]
                            mm = pe.matmul(psB[:, 256 * j:256 * j + 256],
                                           curT[:, 128 * (2 * rnd + j):128 * (2 * rnd + j) + 128],
                                           UF[:, 256 * c:256 * c + 256], start=True, stop=True)
                        e = k.fin(mm, "PE")
                        k.wait(dv, e)
                        ev_psB = k.fin(dv.tensor_tensor(
                            UF[:, 1024 * gi + 512 * rnd:1024 * gi + 512 * rnd + 512],
                            UF[:, 1024 * gi + 512 * rnd:1024 * gi + 512 * rnd + 512],
                            psB[:, :], ALU.add), "DV")
                        e_apl = ev_psB
                    if lev == 5:
                        break
                    if lev < 4:
                        nxtP, nxtT = powers[lev]
                        k.wait_all(pe, [ev_psA, ev_psB])
                        for j in range(4):
                            mm = pe.matmul(psA[:, 128 * j:128 * j + 128], curT[:, 128 * j:128 * j + 128],
                                           curP[:, 128 * j:128 * j + 128], start=True, stop=True)
                        e = k.fin(mm, "PE")
                        k.wait(ac, e)
                        ev_psA = e_p = k.fin(ac.copy(nxtP[:, :], psA[:, :]), "AC")
                        k.wait_all(pe, [ev_psB, e_p])
                        for j in range(4):
                            mm = pe.matmul(psB[:, 128 * j:128 * j + 128], curP[:, 128 * j:128 * j + 128],
                                           curT[:, 128 * j:128 * j + 128], start=True, stop=True)
                        e = k.fin(mm, "PE")
                        k.wait(ac, e)
                        ev_psB = k.fin(ac.copy(nxtT[:, :], psB[:, :]), "AC")
                        curP, curT = nxtP, nxtT
                    else:
                        dstT = PgA if curP is not PgA else PgB
                        k.wait_all(pe, [ev_psA])
                        for j in range(4):
                            mm = pe.matmul(psA[:, 128 * j:128 * j + 128], curP[:, 128 * j:128 * j + 128],
                                           curT[:, 128 * j:128 * j + 128], start=True, stop=True)
                        e = k.fin(mm, "PE")
                        k.wait(ac, e)
                        ev_psA = k.fin(ac.copy(dstT[:, :], psA[:, :]), "AC")
                        curT = dstT
                if not (PH & 8):
                    continue
                k.wait_all(pe, [ev_psT, ev_psB])
                for j, c in enumerate(cs):
                    mm = pe.transpose(psT[:, 128 * j:128 * j + 128],
                                      UF[:, 256 * c + 128:256 * c + 256], ident[:, :])
                e = k.fin(mm, "PE")
                k.wait(ac, e)
                ev_psT = k.fin(ac.activation(FT[:, 512 * gi:512 * gi + 512], psT[:, 0:512],
                                             AF.Copy, scale=-1.0), "AC")
                for j, c in (enumerate(cs) if (len(glim) < 3 or glim[2]) else []):
                    so, sn = c % 2, (c + 1) % 2
                    k.wait_all(pe, [ev_psB, ev_psT, ev_chain])
                    e = k.fin(pe.matmul(psC[0][:, :], UF[:, 256 * c + 128:256 * c + 256],
                                        KTM[:, 128 * c:128 * c + 128],
                                        start=True, stop=True), "PE")
                    k.wait(dv, e)
                    e_nc = k.fin(dv.tensor_scalar_mul(negNc[:, :], psC[0][:, :], -1.0), "DV")
                    mt_old = ident[:, :] if c == 0 else MT2[:, 128 * so:128 * so + 128]
                    k.wait_all(pe, [e_nc])
                    pe.matmul(psC[1][:, :], ident[:, :], mt_old,
                              start=True, stop=False)
                    e = k.fin(pe.matmul(psC[1][:, :], negNc[:, :], mt_old,
                                        start=False, stop=True), "PE")
                    k.wait(ac, e)
                    e_mt = k.fin(ac.activation(MT2[:, 128 * sn:128 * sn + 128], psC[1][:, :],
                                               AF.Copy, scale=sctv[p][:, c:c + 1]), "AC")
                    pe.matmul(psC[0][:, :], ident[:, :], MPRE[:, 128 * c:128 * c + 128],
                              start=True, stop=False)
                    e = k.fin(pe.matmul(psC[0][:, :], mt_old,
                                        negNc[:, :], start=False, stop=True), "PE")
                    k.wait(ac, e)
                    e_mp = k.fin(ac.activation(MPRE[:, 128 * (c + 1):128 * (c + 2)], psC[0][:, :],
                                               AF.Copy, scale=sctv[p][:, c:c + 1]), "AC")
                    if c == NCH - 1:
                        e_mp = k.fin(ac.activation(MCf[:, :], psC[0][:, :], AF.Copy,
                                                   scale=sctv[p][:, c:c + 1]), "AC")
                    k.wait_all(pe, [e_mt, e_mp])
                    pe.matmul(psD[0][:, :], ident[:, :], UF[:, 256 * c:256 * c + 128],
                              start=True, stop=False)
                    e = k.fin(pe.matmul(psD[0][:, :], FT[:, 128 * c:128 * c + 128],
                                        ST0b[:, 128 * c:128 * c + 128],
                                        start=False, stop=True), "PE")
                    k.wait(ac, e)
                    e_wm = k.fin(ac.copy(WMC[:, 128 * so:128 * so + 128], psD[0][:, :]), "AC")
                    k.wait_all(pe, [e_wm])
                    pe.matmul(psD[1][:, :], ident[:, :], ST0b[:, 128 * c:128 * c + 128],
                              start=True, stop=False)
                    e = k.fin(pe.matmul(psD[1][:, :], KTM[:, 128 * c:128 * c + 128],
                                        WMC[:, 128 * so:128 * so + 128],
                                        start=False, stop=True), "PE")
                    k.wait(ac, e)
                    e_st = k.fin(ac.activation(ST0b[:, 128 * (c + 1):128 * (c + 2)], psD[1][:, :],
                                               AF.Copy, scale=sctv[p][:, c:c + 1]), "AC")
                    if c == NCH - 1:
                        e_st = k.fin(ac.activation(CCf[:, :], psD[1][:, :], AF.Copy,
                                                   scale=sctv[p][:, c:c + 1]), "AC")
                    ev_chain = e_st
                    k.wait(pe, e_st)
            if not (PH & 16):
                continue
            k.wait_all(sync, [ev_chain, ev_psA, ev_psB, ev_psT, k.now("DV"), k.now("AC")])
            spill_ev = [
                k.dma(sync.dma_start(out=ufd[p], in_=UF[:, :]), "DL"),
                k.dma(sync.dma_start(out=ftd[p], in_=FT[:, :]), "DL"),
                k.dma(sync.dma_start(out=ktd[p], in_=KTM[:, :]), "DL"),
                k.dma(sync.dma_start(out=mpd[p], in_=MPRE[:, 0:NCH * 128]), "DL"),
                k.dma(sync.dma_start(out=std[p], in_=ST0b[:, 0:NCH * 128]), "DL"),
                k.dma(sync.dma_start(out=mcd[p], in_=MCf[:, :]), "DL"),
                k.dma(sync.dma_start(out=ccd[p], in_=CCf[:, :]), "DL"),
            ]
        k.wait_all(sync, spill_ev + [qt_prev])


def emit_stageH(nc, k, g, ident, DBG):
    sync, gp, pe, dv, ac = nc.sync, nc.gpsimd, nc.tensor, nc.vector, nc.scalar
    with ExitStack() as esH:
        sb = lambda name, shape, dtype=F32: esH.enter_context(nc.sbuf_tensor(name, list(shape), dtype))
        ps = lambda name, shape, dtype=F32: esH.enter_context(nc.psum_tensor(name, list(shape), dtype))
        xh = sb("xhH", (128, 8 * SPAN), BF16)
        xhv = _ftv(xh, SPAN)
        acc = sb("accH", (128, 8 * SPAN), F32)
        accv = _ftv(acc, SPAN)
        hT = sb("hTH", (128, 4 * SPAN), BF16)
        hv = _ftv(hT, SPAN)
        WUP, WGA, WDO = 0, 4096, 8192
        wbf = [sb(f"wbfH{i}", (128, 12288), BF16) for i in range(2)]
        stg = sb("stgH", (128, 4 * 1024), F32)
        cbE = sb("cbEH", (128, SPAN), BF16)
        cmbT = sb("cmbTH", (16, SPAN), BF16)
        routbf = sb("routbfH", (128, 8 * 16), BF16)
        rstg = sb("rstgH", (128, 8 * 16), F32)
        rbias = sb("rbiasH", (128, 16), F32)
        zrow = sb("zrowH", (128, 5 * 16), F32)
        mrow = sb("mrowH", (128, 8), F32)
        cmbb = sb("cmbbH", (128, 16), BF16)
        sgb = [sb(f"sgbH{i}", (128, 512), BF16) for i in range(2)]
        sgc = [sb(f"sgcH{i}", (128, 512), BF16) for i in range(2)]
        sel_e = sb("seleH", (16, 16 * 128), BF16)
        gbf = sb("gbfH", (128, SPAN), BF16)
        psG = [ps(f"psGH{i}", (128, 512)) for i in range(2)]
        psU = [ps(f"psUH{i}", (128, 512)) for i in range(2)]
        psD = [ps(f"psDH{i}", (128, 512)) for i in range(2)]

        routv = _ftv(routbf, 16)
        rstgv = _ftv(rstg, 16)

        xn1_d = g["xn1sv"].rearrange("(a p) n -> a p n", p=128)
        ld = []
        for ft in range(8):
            ld.append(k.dma(sync.dma_start(out=xhv[ft], in_=xn1_d[ft][:, SOFF:SOFF + SPAN]), "DL"))
        rout_d = g["routT"].rearrange("(a p) n -> a p n", p=128)
        for ft in range(8):
            ld.append(k.dma(sync.dma_start(out=rstgv[ft], in_=rout_d[ft]), "DL"))
        ld.append(k.dma(sync.dma_start(out=rbias[:, :], in_=g["rbias"]), "DL"))
        ld.append(k.dma(sync.dma_start(out=sel_e[:, :], in_=g["sele"]), "DL"))
        k.wait_all(ac, ld)
        cv = k.fin(ac.copy(routbf[:, :], rstg[:, :]), "AC")

        esR2 = ExitStack()
        psZ = esR2.enter_context(nc.psum_tensor("psZH", [128, 16], F32))
        psT2 = esR2.enter_context(nc.psum_tensor("psT2H", [16, 128], BF16))
        Z0, I1, Z2, I2, CM = 0, 16, 32, 48, 64
        e_ev = None
        e_tr = None
        k.wait_all(pe, [cv])
        for tc in range(16):
            k.wait_all(pe, [e_ev])
            for ft in range(8):
                mm = pe.matmul(psZ[:, :], xhv[ft][:, 128 * tc:128 * tc + 128],
                               routv[ft], start=(ft == 0), stop=(ft == 7))
            e_mm = k.fin(mm, "PE")
            k.wait(dv, e_mm)
            e_ev = k.fin(dv.tensor_tensor(zrow[:, Z0:Z0 + 16], psZ[:, :], rbias[:, :],
                                          ALU.add), "DV")
            k.wait(dv, e_ev)
            e = k.fin(dv.reduce_max(mrow[:, 0:1], zrow[:, Z0:Z0 + 16], axis=AXL.X), "DV")
            k.wait(dv, e)
            e = k.fin(dv.tensor_scalar(zrow[:, I1:I1 + 16], zrow[:, Z0:Z0 + 16],
                                       mrow[:, 0:1], None, ALU.is_ge), "DV")
            k.wait(dv, e)
            e = k.fin(dv.scalar_tensor_tensor(zrow[:, Z2:Z2 + 16], zrow[:, I1:I1 + 16],
                                              -1e30, zrow[:, Z0:Z0 + 16],
                                              ALU.mult, ALU.add), "DV")
            k.wait(dv, e)
            e = k.fin(dv.reduce_max(mrow[:, 1:2], zrow[:, Z2:Z2 + 16], axis=AXL.X), "DV")
            k.wait(dv, e)
            e = k.fin(dv.tensor_scalar(zrow[:, I2:I2 + 16], zrow[:, Z2:Z2 + 16],
                                       mrow[:, 1:2], None, ALU.is_ge), "DV")
            k.wait(dv, e)
            e = k.fin(dv.tensor_tensor(mrow[:, 2:3], mrow[:, 0:1], mrow[:, 1:2],
                                       ALU.subtract), "DV")
            k.wait(dv, e)
            e = k.fin(dv.tensor_scalar_mul(mrow[:, 3:4], mrow[:, 2:3], -1.0), "DV")
            k.wait(ac, e)
            ac.activation(mrow[:, 4:5], mrow[:, 2:3], AF.Sigmoid)
            e_p = k.fin(ac.activation(mrow[:, 5:6], mrow[:, 3:4], AF.Sigmoid), "AC")
            k.wait(dv, e_p)
            e = k.fin(dv.tensor_scalar(zrow[:, CM:CM + 16], zrow[:, I1:I1 + 16],
                                       mrow[:, 4:5], None, ALU.mult), "DV")
            k.wait(dv, e)
            e = k.fin(dv.scalar_tensor_tensor(zrow[:, CM:CM + 16], zrow[:, I2:I2 + 16],
                                              mrow[:, 5:6], zrow[:, CM:CM + 16],
                                              ALU.mult, ALU.add), "DV")
            k.wait(ac, e)
            e = k.fin(ac.copy(cmbb[:, :], zrow[:, CM:CM + 16]), "AC")
            k.wait_all(pe, [e, e_tr])
            e_mt = k.fin(pe.transpose(psT2[:, :], cmbb[:, :], ident[:, :]), "PE")
            k.wait(ac, e_mt)
            e_tr = k.fin(ac.copy(cmbT[0:16, 128 * tc:128 * tc + 128], psT2[0:16, :]), "AC")
        cmb_done = k.now("AC")
        k.barrier([pe, dv, ac])
        esR2.close()
        psC = ps("psCH", (128, 512))

        eup_r = g["eupT"].rearrange("e (a p) n -> (e a) p n", p=128)
        ega_r = g["egaT"].rearrange("e (a p) n -> (e a) p n", p=128)
        edo_r = g["edoT"].rearrange("e (a p) n -> (e a) p n", p=128)
        shu_r = g["shuT"].rearrange("(a p) n -> a p n", p=128)
        shg_r = g["shgT"].rearrange("(a p) n -> a p n", p=128)
        shd_r = g["shdT"].rearrange("(a p) n -> a p n", p=128)

        def wtiles(e):
            out = []
            if e < 16:
                for ft in range(8):
                    out.append((WUP + 512 * ft, 512, eup_r[8 * e + ft]))
                    out.append((WGA + 512 * ft, 512, ega_r[8 * e + ft]))
                for i in range(4):
                    out.append((WDO + 1024 * i, 1024, edo_r[4 * e + i]))
            else:
                hh = e - 16
                for ft in range(8):
                    out.append((WUP + 512 * ft, 512, shu_r[ft][:, 512 * hh:512 * hh + 512]))
                    out.append((WGA + 512 * ft, 512, shg_r[ft][:, 512 * hh:512 * hh + 512]))
                for i in range(4):
                    out.append((WDO + 1024 * i, 1024, shd_r[4 * hh + i]))
            return out

        stg_free = [None] * 4

        def make_loader(e, pe_free):
            tiles = wtiles(e)
            dst = wbf[e % 2]
            state = {"i": 0, "evs": []}

            def step():
                i = state["i"]
                if i >= len(tiles):
                    return
                c0, w, src = tiles[i]
                s = i % 4
                k.wait_all(sync, [stg_free[s]])
                e_d = k.dma(sync.dma_start(out=stg[:, 1024 * s:1024 * s + w], in_=src), "DL")
                k.wait_all(ac, [e_d] + ([pe_free] if i == 0 else []))
                ev = k.fin(ac.copy(dst[:, c0:c0 + w], stg[:, 1024 * s:1024 * s + w]), "AC")
                stg_free[s] = ev
                state["evs"].append(ev)
                state["i"] = i + 1

            return step, state

        step0, st0 = make_loader(0, None)
        for _ in range(20):
            step0()
        pe_done = [None, None]
        ld_state = st0
        sg_ev = [None, None]
        sgfree = [None, None]
        h_ev = [None, None]
        e_cb = None
        psD_ev = [None, None]

        for e in range(18):
            wv = wbf[e % 2]
            wup = [wv[:, WUP + 512 * ft:WUP + 512 * ft + 512] for ft in range(8)]
            wga = [wv[:, WGA + 512 * ft:WGA + 512 * ft + 512] for ft in range(8)]
            wdo = [wv[:, WDO + 1024 * i:WDO + 1024 * i + 1024] for i in range(4)]
            ld_evs = list(ld_state["evs"])
            if e < 17:
                nstep, ld_state = make_loader(e + 1, pe_done[(e + 1) % 2])
            else:
                nstep = lambda: None
            if e < 16:
                k.wait_all(pe, [cmb_done, e_cb])
                for hf in range(4):
                    k.wait_all(pe, [e_cb])
                    e_mm = k.fin(pe.matmul(psC[:, :], sel_e[:, 128 * e:128 * e + 128],
                                           cmbT[0:16, 512 * hf:512 * hf + 512],
                                           start=True, stop=True), "PE")
                    k.wait_all(ac, [e_mm])
                    e_cb = k.fin(ac.copy(cbE[:, 512 * hf:512 * hf + 512], psC[:, :]), "AC")
            for hid in range(4):
                for tq in range(4):
                    off = 512 * tq
                    sl = (4 * hid + tq) % 2
                    k.wait_all(pe, ld_evs + [sg_ev[sl]])
                    for ft in range(8):
                        mm = pe.matmul(psG[sl][:, :], wga[ft][:, 128 * hid:128 * hid + 128],
                                       xhv[ft][:, off:off + 512],
                                       start=(ft == 0), stop=(ft == 7))
                    e_g = k.fin(mm, "PE")
                    k.wait_all(ac, [e_g, sgfree[sl]])
                    sg_ev[sl] = k.fin(ac.activation(sgb[sl][:, :], psG[sl][:, :],
                                                    AF.Silu), "AC")
                    k.wait_all(pe, [h_ev[sl]])
                    for ft in range(8):
                        mm = pe.matmul(psU[sl][:, :], wup[ft][:, 128 * hid:128 * hid + 128],
                                       xhv[ft][:, off:off + 512],
                                       start=(ft == 0), stop=(ft == 7))
                    e_u = k.fin(mm, "PE")
                    if e < 16:
                        k.wait_all(dv, [sg_ev[sl], e_cb])
                        e_sc = k.fin(dv.tensor_tensor(sgc[sl][:, :], sgb[sl][:, :],
                                                      cbE[:, off:off + 512], ALU.mult), "DV")
                        sct = sgc[sl]
                    else:
                        k.wait_all(dv, [sg_ev[sl]])
                        e_sc = None
                        sct = sgb[sl]
                    k.wait_all(dv, [e_u, e_sc])
                    h_ev[sl] = k.fin(dv.tensor_tensor(hv[hid][:, off:off + 512],
                                                      psU[sl][:, :], sct[:, :],
                                                      ALU.mult), "DV")
                    sgfree[sl] = h_ev[sl]
                    nstep()
            k.wait_all(pe, [k.now("DV")])
            for tq in range(4):
                off = 512 * tq
                for do2 in range(8):
                    sl = do2 % 2
                    k.wait_all(pe, [psD_ev[sl]])
                    for hid in range(4):
                        mm = pe.matmul(psD[sl][:, :], wdo[hid][:, 128 * do2:128 * do2 + 128],
                                       hv[hid][:, off:off + 512],
                                       start=(hid == 0), stop=(hid == 3))
                    e_d = k.fin(mm, "PE")
                    if e == 0:
                        k.wait(ac, e_d)
                        psD_ev[sl] = k.fin(ac.copy(accv[do2][:, off:off + 512],
                                                   psD[sl][:, :]), "AC")
                    else:
                        k.wait(dv, e_d)
                        psD_ev[sl] = k.fin(dv.tensor_tensor(accv[do2][:, off:off + 512],
                                                            accv[do2][:, off:off + 512],
                                                            psD[sl][:, :], ALU.add), "DV")
                    if tq % 2 == 0:
                        nstep()
            pe_done[e % 2] = k.now("PE")
        g1_d = g["g1sp"].rearrange("(a p) n -> a p n", p=128)
        acc_done = [k.now("DV"), k.now("AC")]
        e_gd = None
        for do2 in range(8):
            k.wait_all(ac, acc_done + [e_gd])
            e_s = k.fin(ac.activation(gbf[:, :], accv[do2][:, :], AF.Silu), "AC")
            k.wait(sync, e_s)
            e_gd = k.dma(sync.dma_start(out=g1_d[do2], in_=gbf[:, :]), "DL")
        k.wait(sync, k.now("DL"))

def build_launch2():
    nc = bass.Bass("TRN2", target_bir_lowering=False)
    g = {}

    def din(name, shape, dtype=F32):
        g[name] = nc.dram_tensor(name, list(shape), dtype, kind="ExternalInput").ap()

    din("rlM", (3 * 128, 1024)); din("rlC", (3 * 128, 1024))
    din("uf2", (NPAIR * 128, NCH * 256), BF16)
    din("ft2", (NPAIR * 128, NCH * 128), BF16)
    din("pt2", (NPAIR * 128, NCH * 128), BF16)
    din("mp2", (NPAIR * 128, NCH * 128), BF16)
    din("st2", (NPAIR * 128, NCH * 128), BF16)
    din("qt2", (NPAIR * 128, NCH * 128), BF16)
    din("x1f", (D, SPAN))
    din("og2", (D, SPAN), BF16)
    din("g12", (D, SPAN), BF16)
    din("owT", (D, D))
    din("outb2", (128, 8)); din("nwcol2", (128, 1))
    din("sel2", (128, 2), BF16); din("sel2f", (2, 128))
    din("ident2", (128, 128), BF16)
    g["ysp"] = nc.dram_tensor("ysp", [D, SPAN], F32, kind="ExternalOutput").ap()

    k = Ctx(nc)
    with k.es, nc.Block():
        emit_launch2(nc, k, g)
    return nc


def emit_launch2(nc, k, g):
    sync, gp, pe, dv, ac = nc.sync, nc.gpsimd, nc.tensor, nc.vector, nc.scalar
    es = k.es
    sb = lambda name, shape, dtype=F32: es.enter_context(nc.sbuf_tensor(name, list(shape), dtype))
    ps = lambda name, shape, dtype=F32: es.enter_context(nc.psum_tensor(name, list(shape), dtype))

    ident = sb("identL2", (128, 128), BF16)
    sel2 = sb("sel2L2", (128, 2), BF16)
    sel2f = sb("sel2fL2", (2, 128), F32)
    nwc = sb("nwcL2", (128, 1), F32)
    outb = sb("outbL2", (128, 8), F32)
    rdF = sb("rdFL2", (128, 8 * SPAN), BF16)
    rdFv = _ftv(rdF, SPAN)
    Sst = sb("SstL2", (128, 8 * 128), BF16)
    Sstv = _ftv(Sst, 128)
    owb = sb("owbL2", (128, 8 * 1024), BF16)
    owbv = _ftv(owb, 1024)

    ld0 = []
    ld0.append(k.dma(sync.dma_start(out=ident[:, :], in_=g["ident2"]), "DL"))
    ld0.append(k.dma(sync.dma_start(out=sel2[:, :], in_=g["sel2"]), "DL"))
    ld0.append(k.dma(sync.dma_start(out=sel2f[:, :], in_=g["sel2f"]), "DL"))
    ld0.append(k.dma(sync.dma_start(out=nwc[:, :], in_=g["nwcol2"]), "DL"))
    ld0.append(k.dma(sync.dma_start(out=outb[:, :], in_=g["outb2"]), "DL"))

    with ExitStack() as esR:
        rlm = esR.enter_context(nc.sbuf_tensor("rlmL2", [128, 3 * 1024], F32))
        rlc = esR.enter_context(nc.sbuf_tensor("rlcL2", [128, 3 * 1024], F32))
        rlmb = esR.enter_context(nc.sbuf_tensor("rlmbL2", [128, 3 * 1024], BF16))
        rlcb = esR.enter_context(nc.sbuf_tensor("rlcbL2", [128, 3 * 1024], BF16))
        psRel = esR.enter_context(nc.psum_tensor("psRelL2", [128, 128], F32))
        rlM_d = g["rlM"].rearrange("(j p) n -> j p n", p=128)
        rlC_d = g["rlC"].rearrange("(j p) n -> j p n", p=128)
        rlmv = _ftv(rlm, 1024)
        rlcv = _ftv(rlc, 1024)
        rlmbv = _ftv(rlmb, 1024)
        rlcbv = _ftv(rlcb, 1024)
        ldr = []
        for j in range(3):
            ldr.append(k.dma(sync.dma_start(out=rlmv[j], in_=rlM_d[j]), "DL"))
            ldr.append(k.dma(sync.dma_start(out=rlcv[j], in_=rlC_d[j]), "DL"))
        k.wait_all(ac, ld0 + ldr)
        ac.copy(rlmb[:, :], rlm[:, :])
        cvr = k.fin(ac.copy(rlcb[:, :], rlc[:, :]), "AC")
        k.wait(dv, cvr)
        e_prev = None
        for p in range(8):
            e_s = k.fin(dv.tensor_copy(Sstv[p], rlcbv[0][:, 128 * p:128 * p + 128]), "DV")
            for j in (1, 2):
                k.wait_all(pe, [e_s, e_prev])
                pe.matmul(psRel[:, :], rlmbv[j][:, 128 * p:128 * p + 128], Sstv[p],
                          start=True, stop=False)
                e_mm = k.fin(pe.matmul(psRel[:, :], ident[:, :],
                                       rlcbv[j][:, 128 * p:128 * p + 128],
                                       start=False, stop=True), "PE")
                k.wait(ac, e_mm)
                e_s = e_prev = k.fin(ac.copy(Sstv[p], psRel[:, :]), "AC")
        relay_done = [k.now("AC"), k.now("DV")]
    k.barrier([sync, gp, pe, dv, ac])

    psB2 = es.enter_context(nc.psum_tensor("psB2L2", [128, 512], F32))
    ow_d = g["owT"].rearrange("(a p) n -> a p n", p=128)
    with ExitStack() as esM:
        sbM = lambda name, shape, dtype=F32: esM.enter_context(nc.sbuf_tensor(name, list(shape), dtype))
        psM = lambda name, shape, dtype=F32: esM.enter_context(nc.psum_tensor(name, list(shape), dtype))
        owstg = sbM("owstgL2", (128, 2 * 1024), F32)
        stg_ev = [None, None]
        for a in range(8):
            s = a % 2
            k.wait_all(sync, [stg_ev[s]])
            e_d = k.dma(sync.dma_start(out=owstg[:, 1024 * s:1024 * s + 1024], in_=ow_d[a]), "DL")
            k.wait(ac, e_d)
            stg_ev[s] = k.fin(ac.copy(owbv[a], owstg[:, 1024 * s:1024 * s + 1024]), "AC")
        ow_done = k.now("AC")

        uf_d = g["uf2"].rearrange("(a p) (c u) -> a p c u", p=128, u=256)
        ft_d = g["ft2"].rearrange("(a p) n -> a p n", p=128)
        pt_d = g["pt2"].rearrange("(a p) n -> a p n", p=128)
        mp_d = g["mp2"].rearrange("(a p) n -> a p n", p=128)
        st_d = g["st2"].rearrange("(a p) n -> a p n", p=128)
        qt_d = g["qt2"].rearrange("(a p) n -> a p n", p=128)
        NC8 = NCH * 128
        spl = [sbM(f"splL2{i}", (128, 6 * NC8), BF16) for i in range(2)]
        UFo, FTo, PTo, MPo, STo, QTo = (i * NC8 for i in range(6))
        TST = sbM("TSTL2", (128, NC8), BF16)
        WmB = sbM("WmBL2", (128, NC8), BF16)
        rdT = sbM("rdTL2", (128, NC8), BF16)
        sqt = sbM("sqtL2", (128, SPAN), BF16)
        rs = sbM("rsL2", (2, SPAN), F32)
        rs2 = sbM("rs2L2", (2, SPAN), F32)
        rsbc = sbM("rsbcL2", (128, SPAN), BF16)
        psA = psM("psAL2", (128, 512))
        psB = psM("psBL2", (128, 512))
        psR = psM("psRL2", (128, 512))
        psT = psM("psTL2", (128, 512), BF16)
        psS = psM("psSL2", (2, 512))

        def load_pair(p, s, free_ev):
            evs = []
            dst = spl[s]
            k.wait_all(sync, [free_ev])
            evs.append(k.dma(sync.dma_start(
                out=dst[:, UFo:UFo + NC8].rearrange("p (c u) -> p c u", u=128),
                in_=uf_d[p][:, :, 0:128]), "DL"))
            for off, src in [(FTo, ft_d), (PTo, pt_d), (MPo, mp_d),
                             (STo, st_d), (QTo, qt_d)]:
                evs.append(k.dma(sync.dma_start(out=dst[:, off:off + NC8], in_=src[p]), "DL"))
            return evs

        pair_free = [None, None]
        rms_pe = None
        prev_nm = None
        ld_ev = load_pair(0, 0, None)
        k.wait_all(pe, relay_done + [ow_done])
        for p in range(8):
            s = p % 2
            w = spl[s]
            if p < 7:
                nxt_ev = load_pair(p + 1, 1 - s, pair_free[1 - s])
            eA = eB = eR = eT = None
            k.wait_all(pe, ld_ev)
            for cg in range(8):
                k.wait_all(pe, [eA])
                for j in range(4):
                    c = 4 * cg + j
                    pe.matmul(psA[:, 128 * j:128 * j + 128],
                              w[:, MPo + 128 * c:MPo + 128 * c + 128], Sstv[p],
                              start=True, stop=False)
                    mm = pe.matmul(psA[:, 128 * j:128 * j + 128], ident[:, :],
                                   w[:, STo + 128 * c:STo + 128 * c + 128],
                                   start=False, stop=True)
                e = k.fin(mm, "PE")
                k.wait(ac, e)
                eA = k.fin(ac.copy(TST[:, 512 * cg:512 * cg + 512], psA[:, :]), "AC")
                k.wait_all(pe, [eB, eA])
                for j in range(4):
                    c = 4 * cg + j
                    pe.matmul(psB[:, 128 * j:128 * j + 128], ident[:, :],
                              w[:, UFo + 128 * c:UFo + 128 * c + 128],
                              start=True, stop=False)
                    mm = pe.matmul(psB[:, 128 * j:128 * j + 128],
                                   w[:, FTo + 128 * c:FTo + 128 * c + 128],
                                   TST[:, 128 * c:128 * c + 128],
                                   start=False, stop=True)
                e = k.fin(mm, "PE")
                k.wait(ac, e)
                eB = k.fin(ac.copy(WmB[:, 512 * cg:512 * cg + 512], psB[:, :]), "AC")
                k.wait_all(pe, [eR, eB])
                for j in range(4):
                    c = 4 * cg + j
                    pe.matmul(psR[:, 128 * j:128 * j + 128],
                              w[:, PTo + 128 * c:PTo + 128 * c + 128],
                              WmB[:, 128 * c:128 * c + 128],
                              start=True, stop=False)
                    mm = pe.matmul(psR[:, 128 * j:128 * j + 128],
                                   w[:, QTo + 128 * c:QTo + 128 * c + 128],
                                   TST[:, 128 * c:128 * c + 128],
                                   start=False, stop=True)
                e = k.fin(mm, "PE")
                k.wait(ac, e)
                eR = k.fin(ac.copy(rdT[:, 512 * cg:512 * cg + 512], psR[:, :]), "AC")
                k.wait_all(pe, [eT, eR])
                for j in range(4):
                    c = 4 * cg + j
                    mm = pe.transpose(psT[:, 128 * j:128 * j + 128],
                                      rdT[:, 128 * c:128 * c + 128], ident[:, :])
                e = k.fin(mm, "PE")
                k.wait(dv, e)
                pv = psT[:, :].rearrange("p (c u) -> p c u", u=128)
                rv = rdFv[p].rearrange("p (g c u) -> p g c u", g=8, u=64)
                dv.tensor_copy(rv[0:64, cg, :, :], pv[0:64, :, 0:64])
                eT = k.fin(dv.tensor_copy(rv[64:128, cg, :, :], pv[64:128, :, 64:128]), "DV")
            pair_free[s] = k.now("PE")
            k.wait_all(ac, [eT, rms_pe])
            e_sq = k.fin(ac.activation(sqt[:, :], rdFv[p], AF.Square), "AC")
            eS = None
            for tq in range(4):
                off = 512 * tq
                k.wait_all(pe, [e_sq, eS])
                e_mm = k.fin(pe.matmul(psS[0:2, :], sel2[:, :], sqt[:, off:off + 512],
                                       start=True, stop=True), "PE")
                k.wait(dv, e_mm)
                eS = k.fin(dv.tensor_scalar(rs[0:2, off:off + 512], psS[0:2, :],
                                            1.0 / 64.0, 1e-6, ALU.mult, ALU.add), "DV")
            k.wait(ac, eS)
            eS = k.fin(ac.activation(rs2[0:2, :], rs[0:2, :], AF.Sqrt), "AC")
            k.wait(dv, eS)
            eS = k.fin(dv.reciprocal(rs[0:2, :], rs2[0:2, :]), "DV")
            eB2 = None
            for tq in range(4):
                off = 512 * tq
                k.wait_all(pe, [eS, eB2])
                e_mm = k.fin(pe.matmul(psB2[:, :], sel2f[:, :], rs[0:2, off:off + 512],
                                       start=True, stop=True), "PE")
                k.wait_all(ac, [e_mm, prev_nm])
                eB2 = k.fin(ac.activation(rsbc[:, off:off + 512], psB2[:, :],
                                          AF.Copy, scale=nwc[:, 0:1]), "AC")
            rms_pe = k.now("PE")
            k.wait_all(dv, [eB2])
            prev_nm = e_nm = k.fin(dv.tensor_tensor(rdFv[p], rdFv[p], rsbc[:, :],
                                                    ALU.mult), "DV")
            k.wait(pe, e_nm)
            ld_ev = nxt_ev if p < 7 else []
        scan_done = [k.now("DV"), k.now("AC")]
    k.barrier([sync, gp, pe, dv, ac])

    with ExitStack() as esO:
        sbO = lambda name, shape, dtype=F32: esO.enter_context(nc.sbuf_tensor(name, list(shape), dtype))
        x1_d = g["x1f"].rearrange("(a p) n -> a p n", p=128)
        og_d = g["og2"].rearrange("(a p) n -> a p n", p=128)
        g1_d = g["g12"].rearrange("(a p) n -> a p n", p=128)
        y_d = g["ysp"].rearrange("(a p) n -> a p n", p=128)
        x1t = [sbO(f"x1tL2{i}", (128, SPAN), F32) for i in range(2)]
        ogt = [sbO(f"ogtL2{i}", (128, SPAN), BF16) for i in range(2)]
        g1t = [sbO(f"g1tL2{i}", (128, SPAN), BF16) for i in range(2)]
        prj = sbO("prjL2", (128, 512), F32)
        yb = sbO("ybL2", (128, SPAN), F32)
        dc_free = [None, None]
        ldx = {}

        def load_dc(dc, free_evs):
            s = dc % 2
            evs = []
            k.wait_all(sync, free_evs if free_evs else [])
            evs.append(k.dma(sync.dma_start(out=x1t[s][:, :], in_=x1_d[dc]), "DL"))
            evs.append(k.dma(sync.dma_start(out=ogt[s][:, :], in_=og_d[dc]), "DL"))
            evs.append(k.dma(sync.dma_start(out=g1t[s][:, :], in_=g1_d[dc]), "DL"))
            return evs

        ldx[0] = load_dc(0, None)
        e_pr = None
        e_y = None
        y_dma = None
        for dc in range(8):
            s = dc % 2
            if dc < 7:
                ldx[dc + 1] = load_dc(dc + 1, [dc_free[1 - s]])
            k.wait_all(pe, scan_done)
            for tq in range(4):
                off = 512 * tq
                k.wait_all(pe, [e_pr] + ldx[dc])
                for p in range(8):
                    mm = pe.matmul(psB2[:, :], owbv[p][:, 128 * dc:128 * dc + 128],
                                   rdFv[p][:, off:off + 512],
                                   start=(p == 0), stop=(p == 7))
                e_mm = k.fin(mm, "PE")
                k.wait_all(dv, [e_mm, y_dma])
                e_pr = k.fin(dv.scalar_tensor_tensor(prj[:, :], psB2[:, :],
                                                     outb[:, dc:dc + 1],
                                                     ogt[s][:, off:off + 512],
                                                     ALU.add, ALU.mult), "DV")
                dv.tensor_tensor(prj[:, :], prj[:, :], g1t[s][:, off:off + 512], ALU.mult)
                e_y = k.fin(dv.tensor_tensor(yb[:, off:off + 512], prj[:, :],
                                             x1t[s][:, off:off + 512], ALU.add), "DV")
            dc_free[s] = e_y
            k.wait(sync, e_y)
            y_dma = k.dma(sync.dma_start(out=y_d[dc], in_=yb[:, :]), "DL")
        k.wait(sync, k.now("DL"))


def host_inputs2(inputs, res1):
    ident = np.eye(128, dtype=BF)
    identf = np.eye(128, dtype=np.float32)
    sel2 = np.zeros((128, 2), dtype=BF)
    sel2[0:64, 0] = 1
    sel2[64:128, 1] = 1
    sel2f = np.zeros((2, 128), np.float32)
    sel2f[0, 0:64] = 1
    sel2f[1, 64:128] = 1
    nw = np.asarray(inputs["L1_attn_norm_w"]).astype(np.float32)
    nwcol = np.tile(nw, 2).reshape(128, 1)
    outb = np.ascontiguousarray(
        np.asarray(inputs["L1_attn_out_b"]).astype(np.float32).reshape(8, 128).T)
    owT = np.ascontiguousarray(np.asarray(inputs["L1_attn_out_w"], np.float32).T)
    in_maps = []
    for c in range(NCORES):
        b, q = divmod(c, 4)
        rlM = np.zeros((3, 128, 1024), np.float32)
        rlC = np.zeros((3, 128, 1024), np.float32)
        for j in range(3):
            if j < q:
                pred = 4 * b + j
                mc = np.asarray(res1.results[pred]["mcsp"], np.float32)
                cc = np.asarray(res1.results[pred]["ccsp"], np.float32)
                rlM[j] = mc.transpose(1, 0, 2).reshape(128, 1024)
                rlC[j] = cc.transpose(1, 0, 2).reshape(128, 1024)
            else:
                rlM[j] = np.tile(identf, (1, 8))
        r = res1.results[c]
        m = {
            "rlM": rlM.reshape(384, 1024), "rlC": rlC.reshape(384, 1024),
            "uf2": np.asarray(r["ufsp"]).reshape(NPAIR * 128, NCH * 256),
            "ft2": np.asarray(r["ftsp"]).reshape(NPAIR * 128, NCH * 128),
            "pt2": np.asarray(r["ptsp"]).reshape(NPAIR * 128, NCH * 128),
            "mp2": np.asarray(r["mpsp"]).reshape(NPAIR * 128, NCH * 128),
            "st2": np.asarray(r["stsp"]).reshape(NPAIR * 128, NCH * 128),
            "qt2": np.asarray(r["qtsp"]).reshape(NPAIR * 128, NCH * 128),
            "x1f": np.ascontiguousarray(np.asarray(r["x1sp"], np.float32)[:, 128:128 + SPAN]),
            "og2": np.asarray(r["ogsp"]),
            "g12": np.asarray(r["g1sp"]),
            "owT": owT, "outb2": outb, "nwcol2": nwcol,
            "sel2": sel2, "sel2f": sel2f, "ident2": ident,
        }
        in_maps.append(m)
    return in_maps


def kernel(**inputs):
    in_maps = host_inputs(inputs)
    nc1 = build_launch1(stages="ABCDEFGH")
    res1 = run_bass_kernel_spmd(nc1, in_maps, core_ids=list(range(NCORES)))
    in_maps2 = host_inputs2(inputs, res1)
    nc2 = build_launch2()
    res2 = run_bass_kernel_spmd(nc2, in_maps2, core_ids=list(range(NCORES)))
    out = np.empty((B, T, D), np.float32)
    for c in range(NCORES):
        b, q = divmod(c, 4)
        out[b, SPAN * q:SPAN * (q + 1), :] = np.asarray(res2.results[c]["ysp"], np.float32).T
    return out
